# version 10
# baseline (speedup 1.0000x reference)
import contextlib

import numpy as np
import ml_dtypes

import concourse.bass as bass
import concourse.mybir as mybir
from concourse.bass_utils import run_bass_kernel_spmd

F32 = mybir.dt.float32
BF16 = mybir.dt.bfloat16
FP8 = mybir.dt.float8e4
AF = mybir.ActivationFunctionType
ALU = mybir.AluOpType
AX = mybir.AxisListType

NCORES = 8
DEBUG_L1 = False
FLAGS = {"conv_dve": True, "tail_psum": True, "sbf_act": True,
         "tbank": True, "lnexp": True}
T = 2048
D = 2048
NCHUNK = 16
H = 32
PROJ = 8224
NEB = 65
SWEEP = 4
CPS = 4



class _Op:
    __slots__ = ("eng", "emit", "reads", "writes", "dma", "slot", "idx",
                 "deps", "need_inc", "rank", "dbg")

    def __init__(self, eng, emit, reads, writes, dma, slot, idx, dbg):
        self.eng = eng
        self.emit = emit
        self.reads = reads
        self.writes = writes
        self.dma = dma
        self.slot = slot
        self.idx = idx
        self.deps = []
        self.need_inc = False
        self.rank = None
        self.dbg = dbg


QUEUE_ENG = {"sp": "sync", "act": "scalar", "pool": "gpsimd"}
ENGS = ("tensor", "vector", "scalar", "gpsimd", "sync")


class Prog:

    def __init__(self):
        self.ops = []
        self.last_write = {}
        self.readers = {}
        self.epoch_idx = None

    def op(self, eng, emit, reads=(), writes=(), dma=None, slot=None, dbg=""):
        if dma is not None:
            assert slot is not None, dbg
            qt = "sw" if dma == "pool" else "hw"
            if slot.startswith("batch:"):
                slot = f"batch:{qt}:{slot[6:]}"
            else:
                slot = f"{qt}:{slot}"
        o = _Op(eng, emit, tuple(reads), tuple(writes), dma, slot,
                len(self.ops), dbg)
        deps = set()
        if self.epoch_idx is not None and self.epoch_idx != len(self.ops):
            deps.add(self.epoch_idx)
        for k in o.reads:
            w = self.last_write.get(k)
            if w is not None:
                deps.add(w)
        for k in o.writes:
            w = self.last_write.get(k)
            if w is not None:
                deps.add(w)
            for r in self.readers.get(k, ()):
                deps.add(r)
        o.deps = sorted(deps)
        for k in o.reads:
            self.readers.setdefault(k, []).append(o.idx)
        for k in o.writes:
            self.last_write[k] = o.idx
            self.readers[k] = []
        self.ops.append(o)
        return o

    def finalize(self, nc, es):
        ops = self.ops
        for o in ops:
            for d in o.deps:
                dep = ops[d]
                if (dep.dma is None and o.dma is None
                        and dep.eng == "tensor" and o.eng == "tensor"):
                    continue
                dep.need_inc = True
        eng_count = {e: 0 for e in ENGS}
        slot_count = {}
        slot_total = {}
        for o in ops:
            if o.dma is not None:
                slot_count[o.slot] = slot_count.get(o.slot, 0) + 16
                o.rank = slot_count[o.slot]
                o.need_inc = True
            elif o.need_inc:
                eng_count[o.eng] += 1
                o.rank = eng_count[o.eng]
        slot_total = dict(slot_count)
        sems = {e: es.enter_context(nc.semaphore(f"sem_{e}"))
                for e in ("tensor", "vector", "scalar", "gpsimd")}
        dsems = {s: es.enter_context(nc.semaphore(f"ds_{si}"))
                 for si, s in enumerate(slot_total)}
        streams = {e: [] for e in ENGS}
        watermark = {e: {} for e in ENGS}
        for o in ops:
            my_eng = QUEUE_ENG[o.dma] if o.dma is not None else o.eng
            waits = {}
            for d in o.deps:
                dep = ops[d]
                if dep.dma is not None:
                    key = ("d", dep.slot)
                    if dep.slot.startswith("batch:"):
                        val = slot_total[dep.slot]
                    else:
                        val = dep.rank
                else:
                    if (dep.eng == "tensor" and o.dma is None
                            and o.eng == "tensor"):
                        continue
                    key, val = ("e", dep.eng), dep.rank
                    assert val is not None, (o.dbg, dep.dbg)
                if waits.get(key, 0) < val:
                    waits[key] = val
            wm = watermark[my_eng]
            items = []
            for key, val in sorted(waits.items()):
                if wm.get(key, 0) >= val:
                    continue
                wm[key] = val
                items.append((key, val))
            streams[my_eng].append((o, items))
        return streams, sems, dsems


def run_streams(nc, es, prog):
    streams, sems, dsems = prog.finalize(nc, es)

    def emit(engobj, name):
        for o, waits in streams[name]:
            for (kind, qn), val in waits:
                engobj.wait_ge(dsems[qn] if kind == "d" else sems[qn], val)
            inst = o.emit(engobj)
            if o.need_inc:
                if o.dma is not None:
                    inst.then_inc(dsems[o.slot], 16)
                else:
                    inst.then_inc(sems[o.eng], 1)

    block = es.enter_context(nc.Block())

    @block.tensor
    def _(e):
        emit(e, "tensor")

    @block.vector
    def _(e):
        emit(e, "vector")

    @block.scalar
    def _(e):
        emit(e, "scalar")

    @block.gpsimd
    def _(e):
        emit(e, "gpsimd")

    @block.sync
    def _(e):
        emit(e, "sync")



def _consts_np():
    k = np.arange(128)
    cum = (k[:, None] <= k[None, :]).astype(np.float32)
    gt = (k[:, None] > k[None, :]).astype(np.float32)
    neglt = np.where(k[:, None] < k[None, :], np.float32(-1e30), np.float32(0))
    return {
        "I_BF": np.eye(128, dtype=np.float32).astype(ml_dtypes.bfloat16),
        "I_F8": np.eye(128, dtype=np.float32).astype(ml_dtypes.float8_e4m3fn),
        "I_F32": np.eye(128, dtype=np.float32),
        "CUM_BF": cum.astype(ml_dtypes.bfloat16),
        "CUM_F32": cum,
        "GT_BF": gt.astype(ml_dtypes.bfloat16),
        "GT_F32": gt,
        "NEGLT_BF": neglt.astype(ml_dtypes.bfloat16),
        "I4_BF": np.tile(np.eye(128, dtype=np.float32), (1, 4)).astype(
            ml_dtypes.bfloat16),
        "ONES_F32": np.ones((128, 128), np.float32),
    }


_NPDT = {"I_BF": BF16, "I_F8": FP8, "I_F32": F32, "CUM_BF": BF16,
         "CUM_F32": F32, "GT_BF": BF16, "GT_F32": F32, "NEGLT_BF": BF16,
         "I4_BF": BF16, "ONES_F32": F32}


class B:
    def __init__(self):
        self.nc = bass.Bass("TRN2", target_bir_lowering=False)
        self.es = contextlib.ExitStack()
        self.prog = Prog()
        self.t = {}

    def dram_in(self, name, shape, dtype):
        ap = self.nc.dram_tensor(name, list(shape), dtype,
                                 kind="ExternalInput").ap()
        self.t[name] = ap
        return ap

    def dram_out(self, name, shape, dtype):
        ap = self.nc.dram_tensor(name, list(shape), dtype,
                                 kind="ExternalOutput").ap()
        self.t[name] = ap
        return ap

    def dram_tmp(self, name, shape, dtype):
        ap = self.nc.dram_tensor(name, list(shape), dtype, kind="Internal").ap()
        self.t[name] = ap
        return ap

    def sb(self, name, shape, dtype, es=None):
        h = (es or self.es).enter_context(
            self.nc.sbuf_tensor(name, list(shape), dtype))
        self.t[name] = h
        return h

    def ps(self, name, shape):
        h = self.es.enter_context(self.nc.psum_tensor(name, list(shape), F32))
        self.t[name] = h
        return h

    def load_consts(self, names):
        cn = _consts_np()
        out = {}
        for name in names:
            arr = np.ascontiguousarray(cn[name])
            if _NPDT[name] == FP8:
                dr = self.nc.inline_tensor(
                    arr.view(np.uint8), name + "_d").ap().bitcast(FP8)
            else:
                dr = self.nc.inline_tensor(arr, name + "_d").ap()
            t = self.sb(name, list(arr.shape), _NPDT[name])
            self.dma("sp", t[:, :], dr, [], [name], slot="batch:consts",
                     dbg=f"cst_{name}")
            out[name] = t
        return out

    def dma(self, q, out, in_, reads, writes, slot=None, slow=False, dbg=""):
        def em(e, out=out, in_=in_, slow=slow):
            return e.dma_start(out=out, in_=in_,
                               allow_slow_non_contiguous=slow)
        self.prog.op(QUEUE_ENG[q], em, reads, writes, dma=q,
                     slot=slot or "batch:misc", dbg=dbg)

    def mm(self, out, lhsT, rhs, start, stop, reads, writes, dbg="",
           perf_mode=None):
        def em(e, out=out, lhsT=lhsT, rhs=rhs, start=start, stop=stop,
               perf_mode=perf_mode):
            return e.matmul(out, lhsT, rhs, start=start, stop=stop,
                            perf_mode=perf_mode)
        self.prog.op("tensor", em, reads, writes, dbg=dbg)

    def act(self, out, in_, func, reads, writes, bias=0.0, scale=1.0,
            accum=None, dbg=""):
        def em(e, out=out, in_=in_, func=func, bias=bias, scale=scale,
               accum=accum):
            return e.activation(out, in_, func, bias=bias, scale=scale,
                                accum_out=accum)
        self.prog.op("scalar", em, reads, writes, dbg=dbg)

    def vop(self, eng, fn, reads, writes, dbg=""):
        self.prog.op(eng, fn, reads, writes, dbg=dbg)

    def fence(self, keys):
        self.prog.op("gpsimd", lambda e: e.nop(), keys, [], dbg="fence")

    def epoch(self):
        keys = sorted(set(self.prog.last_write) | set(self.prog.readers))
        o = self.prog.op("gpsimd", lambda e: e.nop(), keys, keys, dbg="epoch")
        self.prog.epoch_idx = o.idx



def col_max_to_scalar(b, col_ap, col_key, out_ap, out_key, tag, row_sb,
                      row_key):
    bounce = b.dram_tmp("bnc_" + tag, [128], F32)
    b.dma("pool", bounce, col_ap, [col_key], ["bnc" + tag], slot="bn" + tag,
          slow=True, dbg="bw" + tag)
    b.dma("pool", row_sb, bounce.rearrange("(a k) -> a k", a=1),
          ["bnc" + tag], [row_key], slot="bn" + tag, dbg="br" + tag)

    def em(e):
        return e.reduce_max(out_ap, row_sb, axis=AX.X)
    b.vop("vector", em, [row_key], [out_key], dbg="bm" + tag)


def bcast_row(b, row_ap, row_keys, n, out_ap, out_key, tag):
    bounce = b.dram_tmp("bcr_" + tag, [1, n], F32)
    b.dma("pool", bounce, row_ap, row_keys, ["bcr" + tag], slot="bc" + tag,
          dbg="cw" + tag)
    b.dma("pool", out_ap, bounce.broadcast_to([128, n]), ["bcr" + tag],
          [out_key], slot="bc" + tag, slow=True, dbg="cr" + tag)


def ffn_scalars(b, amax_row_ap, amax_row_key, wam_ap, wam_key, bc, tag,
                es=None):
    sc = b.sb("sc_" + tag, [1, 4], F32, es=es)
    row = b.sb("scrow_" + tag, [1, 4], F32, es=es)

    def e0(e):
        return e.reduce_max(sc[0:1, 0:1], amax_row_ap, axis=AX.X)
    b.vop("vector", e0, [amax_row_key], [tag + "s0"], dbg=tag + "mx")
    b.vop("vector", lambda e: e.tensor_scalar(
        sc[0:1, 0:1], sc[0:1, 0:1], 1.0 / 448.0, 1e-12, ALU.mult, ALU.max),
        [tag + "s0"], [tag + "s0"], dbg=tag + "s")
    b.vop("vector", lambda e: e.tensor_scalar(
        sc[0:1, 1:2], wam_ap, 1.0 / 448.0, 1e-12, ALU.mult, ALU.max),
        [wam_key], [tag + "s1"], dbg=tag + "sw")
    b.vop("vector", lambda e: e.reciprocal(sc[0:1, 2:3], sc[0:1, 0:1]),
          [tag + "s0"], [tag + "s2"], dbg=tag + "inv")
    b.vop("vector", lambda e: e.tensor_scalar(
        sc[0:1, 2:3], sc[0:1, 2:3], 0.5, None, ALU.mult),
        [tag + "s2"], [tag + "s2"], dbg=tag + "half")
    b.vop("vector", lambda e: e.tensor_tensor(
        sc[0:1, 3:4], sc[0:1, 0:1], sc[0:1, 1:2], ALU.mult),
        [tag + "s0", tag + "s1"], [tag + "s3"], dbg=tag + "ssw")
    b.vop("vector", lambda e: e.tensor_scalar(
        sc[0:1, 3:4], sc[0:1, 3:4], 2.0, None, ALU.mult),
        [tag + "s3"], [tag + "s3"], dbg=tag + "g2")
    b.vop("vector", lambda e: e.reciprocal(row[0:1, 2:3], sc[0:1, 1:2]),
          [tag + "s1"], [tag + "r2"], dbg=tag + "isw")
    b.vop("vector", lambda e: e.tensor_scalar(
        row[0:1, 2:3], row[0:1, 2:3], 0.5, None, ALU.mult),
        [tag + "r2"], [tag + "r2"], dbg=tag + "hsw")
    b.vop("vector", lambda e: e.tensor_copy(row[0:1, 0:1], sc[0:1, 2:3]),
          [tag + "s2"], [tag + "r0"], dbg=tag + "c0")
    b.vop("vector", lambda e: e.tensor_copy(row[0:1, 1:2], sc[0:1, 3:4]),
          [tag + "s3"], [tag + "r1"], dbg=tag + "c1")
    bcast_row(b, row[0:1, 0:3], [tag + "r0", tag + "r1", tag + "r2"], 3,
              bc[:, 0:3], "bc", tag)


def qscale_cols(b, qsc, qbs, m_sb, m_key, r_sb, r_key, bc):
    b.vop("vector", lambda e: e.tensor_single_scalar(
        qsc[:, :], r_sb[:, :], bc[:, 0:1], ALU.mult),
        [r_key, "bc"], ["qsc"], dbg="qsc")
    b.vop("vector", lambda e: e.tensor_tensor(
        qbs[:, :], m_sb[:, :], qsc[:, :], ALU.mult),
        [m_key, "qsc"], ["qbs"], dbg="qbs0")
    b.vop("vector", lambda e: e.tensor_scalar(
        qbs[:, :], qbs[:, :], -1.0, None, ALU.mult),
        ["qbs"], ["qbs"], dbg="qbs")


def load_wq(b, es, bc, wq_sb):
    WQT = b.t["WQT"]
    tiles = []
    for dc in range(16):
        t = b.sb(f"wqt_{dc}", [128, 2048], BF16, es=es)
        tiles.append(t)
        b.dma(("sp", "act")[dc % 2], t[:, :], WQT[dc, :, :], [], [f"wqt{dc}"],
              slot=f"batch:wqld{dc % 2}", dbg=f"wql{dc}")
    wmx = b.sb("wmx", [128, 2], F32, es=es)
    for dc in range(16):
        def em(e, dc=dc):
            return e.reduce_max(wmx[:, 1:2], tiles[dc][:, :], axis=AX.X,
                                apply_absolute_value=True)
        b.vop("vector", em, [f"wqt{dc}"], ["wmx1"], dbg=f"wam{dc}")

        def em2(e, dc=dc):
            if dc == 0:
                return e.tensor_copy(wmx[:, 0:1], wmx[:, 1:2])
            return e.tensor_tensor(wmx[:, 0:1], wmx[:, 0:1], wmx[:, 1:2],
                                   ALU.max)
        b.vop("vector", em2, ["wmx1", "wmx0"], ["wmx0"], dbg=f"wamx{dc}")
    return tiles, wmx


def quant_wq(b, tiles, wq_sb):
    for dc in range(16):
        b.act(wq_sb[:, dc * 2048:(dc + 1) * 2048], tiles[dc][:, :],
              AF.Identity, reads=[f"wqt{dc}", "bc"], writes=[f"wq{dc}"],
              scale=b.t["bcb"][:, 2:3], dbg=f"wqq{dc}")


def ffn_chunk(b, c, src_ap, src_key, qsc, qbs, wq_sb, banks, tbanks, q1, q1T,
              kq, kqT, f1, f1key, IF8, bc):
    b.act(q1[:, :], src_ap, AF.Identity, reads=[src_key, "qsc", "qbs"],
          writes=[kq], scale=qsc[:, c:c + 1], bias=qbs[:, c:c + 1],
          dbg=f"qz{c}")
    for dc in range(16):
        tb, tk = tbanks[dc % 2]
        b.mm(tb[:, 0:128], q1[:, dc * 128:(dc + 1) * 128], IF8[:, :],
             True, True, [kq, "I_F8"], [tk], dbg=f"qT{c}_{dc}")
        if dc % 2 == 0:
            def ev(e, dc=dc, tb=tb):
                return e.tensor_copy(q1T[:, dc * 128:(dc + 1) * 128],
                                     tb[:, 0:128])
            b.vop("vector", ev, [tk], [f"{kqT}{dc}"], dbg=f"qTe{c}_{dc}")
        else:
            b.act(q1T[:, dc * 128:(dc + 1) * 128], tb[:, 0:128],
                  AF.Identity, reads=[tk], writes=[f"{kqT}{dc}"],
                  dbg=f"qTe{c}_{dc}")
    wqr = wq_sb.rearrange("p (d x) -> p d x", x=2048)
    for osl in range(4):
        pso, pk = banks[osl]
        for j in range(8):
            b.mm(pso[:, :],
                 q1T[:, j * 256:(j + 1) * 256].rearrange(
                     "p (o t) -> p o t", o=2),
                 wqr[:, 2 * j:2 * j + 2, osl * 512:(osl + 1) * 512],
                 j == 0, j == 7,
                 [f"{kqT}{2 * j}", f"{kqT}{2 * j + 1}",
                  f"wq{2 * j}", f"wq{2 * j + 1}"], [pk],
                 perf_mode=mybir.MatmulPerfMode.DoubleRow,
                 dbg=f"fm{c}_{osl}_{j}")
        if osl % 2 == 0:
            b.act(f1[:, osl * 512:(osl + 1) * 512], pso[:, :], AF.Identity,
                  reads=[pk, "bc"], writes=[f1key], scale=bc[:, 1:2],
                  dbg=f"fe{c}_{osl}")
        else:
            def evf(e, osl=osl, pso=pso):
                return e.tensor_single_scalar(
                    f1[:, osl * 512:(osl + 1) * 512], pso[:, :], bc[:, 1:2],
                    ALU.mult)
            b.vop("vector", evf, [pk, "bc"], [f1key], dbg=f"fe{c}_{osl}")



def build_L0():
    b = B()
    nc = b.nc
    u = b.dram_in("u", [NCHUNK, 128, D], F32)
    M1 = b.dram_out("m1", [128, NCHUNK], F32)
    R1 = b.dram_out("r1", [128, NCHUNK], F32)
    AMAX1 = b.dram_out("amax1", [1, 1], F32)
    with b.es:
        ub = [b.sb(f"ub{i}", [128, D], F32) for i in range(3)]
        scrap = b.sb("scrap", [128, D], F32)
        sumc = b.sb("sumc", [128, NCHUNK], F32)
        sqc = b.sb("sqc", [128, NCHUNK], F32)
        mxc = b.sb("mxc", [128, NCHUNK], F32)
        mnc = b.sb("mnc", [128, NCHUNK], F32)
        m1 = b.sb("m1s", [128, NCHUNK], F32)
        r1 = b.sb("r1s", [128, NCHUNK], F32)
        g = b.sb("g", [128, NCHUNK], F32)
        tmp = b.sb("tmp", [128, NCHUNK], F32)
        acol = b.sb("acol", [128, 1], F32)
        epsc = b.sb("epsc", [128, 1], F32)
        one = b.sb("one", [1, 2], F32)
        row = b.sb("row", [1, 128], F32)
        b.vop("vector", lambda e: e.memset(epsc[:, :], 1e-5), [], ["epsc"])

        for c in range(NCHUNK):
            buf, bk = ub[c % 3], f"ub{c % 3}"
            b.dma(("sp", "pool", "act")[c % 3], buf[:, :], u[c, :, :], [],
                  [bk], slot=bk, dbg=f"u{c}")

            b.act(scrap[:, :], buf[:, :], AF.Identity, reads=[bk],
                  writes=["scrap", f"sum{c}"], accum=sumc[:, c:c + 1],
                  dbg=f"sm{c}")
            b.act(scrap[:, :], buf[:, :], AF.Square, reads=[bk],
                  writes=["scrap", f"sq{c}"], accum=sqc[:, c:c + 1],
                  dbg=f"sq{c}")

            def emx(e, c=c, buf=buf):
                return e.reduce_max(mxc[:, c:c + 1], buf[:, :], axis=AX.X)
            b.vop("vector", emx, [bk], [f"mx{c}"], dbg=f"mx{c}")

            def emn(e, c=c, buf=buf):
                return e.tensor_reduce(mnc[:, c:c + 1], buf[:, :], axis=AX.X,
                                       op=ALU.min)
            b.vop("vector", emn, [bk], [f"mn{c}"], dbg=f"mn{c}")

        sall = [f"sum{c}" for c in range(NCHUNK)]
        qall = [f"sq{c}" for c in range(NCHUNK)]
        mxall = [f"mx{c}" for c in range(NCHUNK)]
        mnall = [f"mn{c}" for c in range(NCHUNK)]
        b.vop("vector", lambda e: e.tensor_scalar(
            m1[:, :], sumc[:, :], 1.0 / D, None, ALU.mult), sall, ["m1"],
            dbg="m1")
        b.vop("vector", lambda e: e.tensor_scalar(
            tmp[:, :], sqc[:, :], 1.0 / D, None, ALU.mult), qall, ["tmp"],
            dbg="ex2")
        b.vop("vector", lambda e: e.tensor_tensor(
            r1[:, :], m1[:, :], m1[:, :], ALU.mult), ["m1"], ["r1"], dbg="m2")
        b.vop("vector", lambda e: e.tensor_tensor(
            r1[:, :], tmp[:, :], r1[:, :], ALU.subtract), ["tmp", "r1"],
            ["r1"], dbg="var")
        b.act(r1[:, :], r1[:, :], AF.Sqrt, reads=["r1", "epsc"], writes=["r1"],
              bias=epsc[:, 0:1], dbg="std")
        b.vop("vector", lambda e: e.reciprocal(r1[:, :], r1[:, :]), ["r1"],
              ["r1"], dbg="rstd")
        b.vop("vector", lambda e: e.tensor_tensor(
            g[:, :], mxc[:, :], m1[:, :], ALU.subtract), mxall + ["m1"], ["g"],
            dbg="g1")
        b.vop("vector", lambda e: e.tensor_tensor(
            tmp[:, :], m1[:, :], mnc[:, :], ALU.subtract), mnall + ["m1"],
            ["tmp"], dbg="g2")
        b.vop("vector", lambda e: e.tensor_tensor(
            g[:, :], g[:, :], tmp[:, :], ALU.max), ["g", "tmp"], ["g"],
            dbg="g3")
        b.vop("vector", lambda e: e.tensor_tensor(
            g[:, :], g[:, :], r1[:, :], ALU.mult), ["g", "r1"], ["g"],
            dbg="g4")

        def ea(e):
            return e.reduce_max(acol[:, :], g[:, :], axis=AX.X)
        b.vop("vector", ea, ["g"], ["acol"], dbg="acol")
        col_max_to_scalar(b, acol[:, 0:1], "acol", one[0:1, 0:1], "a1", "a",
                          row[0:1, :], "rowbuf")

        b.dma("sp", M1, m1[:, :], ["m1"], ["o1"], slot="batch:outs",
              dbg="st_m1")
        b.dma("sp", R1, r1[:, :], ["r1"], ["o2"], slot="batch:outs",
              dbg="st_r1")
        b.dma("sp", AMAX1, one[0:1, 0:1], ["a1"], ["o3"], slot="batch:outs",
              dbg="st_a1")
        b.fence(["o1", "o2", "o3"])
        run_streams(nc, b.es, b.prog)
    return nc



def build_L1():
    b = B()
    nc = b.nc
    u = b.dram_in("u", [NCHUNK, 128, D], F32)
    WA = b.dram_in("WA", [NEB, 128, 16, 128], F32)
    WB = b.dram_in("WB", [4, 16, 128, 512], F32)
    b.dram_in("WQT", [16, 128, 2048], BF16)
    M1 = b.dram_in("m1", [128, NCHUNK], F32)
    R1 = b.dram_in("r1", [128, NCHUNK], F32)
    A1V = b.dram_in("amax1v", [1, 8], F32)
    CWH = b.dram_in("cwh", [128, 192], F32)
    CBH = b.dram_in("cbh", [128, 48], F32)
    ZBH = b.dram_in("zbh", [128, 16], F32)
    RWH = b.dram_in("rwh", [128, 16], F32)
    DREP = b.dram_in("drep", [128, H * 64], F32)
    X3 = b.dram_out("x3", [NCHUNK, 128, D], F32)
    M3 = b.dram_out("m3", [128, NCHUNK], F32)
    R3 = b.dram_out("r3", [128, NCHUNK], F32)
    AMAX3 = b.dram_out("amax3", [1, 1], F32)
    if DEBUG_L1:
        x2d = b.dram_out("x2d", [NCHUNK, 128, D], F32)
        XBCD = b.dram_out("xbcd", [128, 48 * 515], BF16)
        ZSTD = b.dram_out("zstd", [128, CPS * D], BF16)
        ATSD = b.dram_out("atsd", [128, 32], F32)
        EXPAD = b.dram_out("expad", [128, 32], F32)
        YBD = b.dram_out("ybd", [128, D], BF16)
        H2TD = b.dram_out("h2td", [128, 16 * 512], BF16)
        BCD = b.dram_out("bcd", [128, 4], F32)
        QSCD = b.dram_out("qscd", [128, NCHUNK], F32)
        QBSD = b.dram_out("qbsd", [128, NCHUNK], F32)
        Q1D = b.dram_out("q1d", [128, D], mybir.dt.uint8)
        F1D = b.dram_out("f1d", [128, D], BF16)
    else:
        x2d = b.dram_tmp("x2d", [NCHUNK, 128, D], F32)
    WINBF = b.dram_tmp("winbf", [NEB, 128, 2048], BF16)
    WOUTBF = b.dram_tmp("woutbf", [4, 16, 128, 512], BF16)

    with b.es:
        cI = b.load_consts(["I_BF", "I_F8", "I_F32", "CUM_BF", "CUM_F32",
                            "GT_BF", "GT_F32", "NEGLT_BF", "ONES_F32"])
        psA = b.ps("psA", [128, 512])
        psB = b.ps("psB", [128, 512])
        psT = b.ps("psT", [128, 512])
        psE = b.ps("psE", [128, 512])
        psS = b.ps("psS", [128, 512])
        psYa = b.ps("psYa", [128, 512])
        psYb = b.ps("psYb", [128, 512])
        psSt = b.ps("psSt", [128, 512])

        st = {}
        for nm in ("m1s", "r1s", "m2s", "r2s", "m3s", "r3s", "qsc", "qbs",
                   "h2sc", "h2bs", "mx3s", "mn3s"):
            st[nm] = b.sb(nm, [128, NCHUNK], F32)
        epsc = b.sb("epsc", [128, 1], F32)
        b.vop("vector", lambda e: e.memset(epsc[:, :], 1e-5), [], ["epsc"])
        rmsw = b.sb("rmsw", [128, 16], F32)
        zbias = b.sb("zbias", [128, 16], F32)
        cw = b.sb("cw", [128, 192], F32)
        cb = b.sb("cb", [128, 48], F32)
        DREPb = b.sb("DREPb", [128, H * 64], BF16)
        scrap = b.sb("scrap", [128, D], BF16)
        bc = b.sb("bcb", [128, 4], F32)
        a1row = b.sb("a1row", [1, 8], F32)
        warow = b.sb("warow", [1, 1], F32)
        acol3 = b.sb("acol3", [128, 1], F32)
        one3 = b.sb("one3", [1, 2], F32)
        rowb = b.sb("rowb", [1, 128], F32)
        stc = b.sb("stc", [128, 4], F32)

        b.dma("sp", st["m1s"][:, :], M1, [], ["m1"], slot="batch:init", dbg="ld_m1")
        b.dma("sp", st["r1s"][:, :], R1, [], ["r1"], slot="batch:init", dbg="ld_r1")
        b.dma("sp", rmsw[:, :], RWH, [], ["rmsw"], slot="batch:init", dbg="ld_rw")
        b.dma("sp", zbias[:, :], ZBH, [], ["zbias"], slot="batch:init", dbg="ld_zb")
        b.dma("sp", cw[:, :], CWH, [], ["cw"], slot="batch:init", dbg="ld_cw")
        b.dma("sp", cb[:, :], CBH, [], ["cb"], slot="batch:init", dbg="ld_cb")
        b.dma("sp", a1row[0:1, :], A1V, [], ["a1row"], slot="batch:init", dbg="ld_a1")

        with contextlib.ExitStack() as fes:
            wq = b.sb("wq", [128, 16 * 2048], FP8, es=fes)
            drepf = b.sb("drepf", [128, H * 64], F32, es=fes)
            b.dma("pool", drepf[:, :], DREP, [], ["drepr"], slot="drep",
                  dbg="ld_drepr")
            b.vop("vector", lambda e: e.tensor_copy(DREPb[:, :],
                                                    drepf[:, :]),
                  ["drepr"], ["DREPb"], dbg="drepc")
            tiles, wmx = load_wq(b, fes, bc, wq)
            col_max_to_scalar(b, wmx[:, 0:1], "wmx0", warow[0:1, 0:1], "wam",
                              "w1", rowb[0:1, :], "rowbufW")
            ffn_scalars(b, a1row[0:1, :], "a1row", warow[0:1, 0:1], "wam",
                        bc, "f1", es=fes)
            qscale_cols(b, st["qsc"], st["qbs"], st["m1s"], "m1", st["r1s"],
                        "r1", bc)
            quant_wq(b, tiles, wq)
            ub = [b.sb(f"ub{i}", [128, D], F32, es=fes) for i in range(2)]
            q1 = [b.sb(f"q1_{i}", [128, D], FP8, es=fes) for i in range(2)]
            q1T = [b.sb(f"q1T_{i}", [128, D], FP8, es=fes) for i in range(2)]
            f1 = [b.sb(f"f1_{i}", [128, D], BF16, es=fes) for i in range(2)]
            x2b = [b.sb(f"x2b{i}", [128, D], F32, es=fes) for i in range(2)]
            for c in range(NCHUNK):
                i = c % 2
                b.dma("pool", ub[i][:, :], u[c, :, :], [], [f"ub{i}"],
                      slot=f"ub{i}", dbg=f"u{c}")
                ffn_chunk(b, c, ub[i][:, :], f"ub{i}", st["qsc"], st["qbs"],
                          wq,
                          [(psA, "psA"), (psB, "psB"), (psE, "psE"),
                           (psS, "psS")],
                          [(psT, "psT"), (psSt, "psSt")],
                          q1[i], q1T[i], f"q1{i}", f"q1T{i}", f1[i],
                          f"f1{i}", cI["I_F8"], bc)
                if DEBUG_L1 and c == 0:
                    b.dma("sp", BCD, bc[:, :], ["bc"], ["dg7"], dbg="dBC", slot="batch:dbg")
                    b.dma("sp", QSCD, st["qsc"][:, :], ["qsc"], ["dg8"],
                          dbg="dQS", slot="batch:dbg")
                    b.dma("sp", QBSD, st["qbs"][:, :], ["qbs"], ["dg9"],
                          dbg="dQB", slot="batch:dbg")
                    b.dma("sp", Q1D, q1[0][:, :].bitcast(mybir.dt.uint8),
                          ["q10"], ["dgA"], slot="batch:dbg", dbg="dQ1")
                    b.dma("sp", F1D, f1[0][:, :], ["f10"], ["dgB"], dbg="dF1", slot="batch:dbg")

                def eadd(e, i=i):
                    return e.tensor_tensor(x2b[i][:, :], ub[i][:, :],
                                           f1[i][:, :], ALU.add)
                b.vop("vector", eadd, [f"ub{i}", f"f1{i}"], [f"x2b{i}"],
                      dbg=f"x2a{c}")
                b.act(scrap[:, :], x2b[i][:, :], AF.Identity,
                      reads=[f"x2b{i}"], writes=["scrap", "h2", f"s2_{c}"],
                      accum=st["m2s"][:, c:c + 1], dbg=f"s2{c}")
                b.act(scrap[:, :], x2b[i][:, :], AF.Square,
                      reads=[f"x2b{i}"], writes=["scrap", "h2", f"v2_{c}"],
                      accum=st["r2s"][:, c:c + 1], dbg=f"v2{c}")
                b.dma("pool", x2d[c, :, :], x2b[i][:, :], [f"x2b{i}"],
                      [f"x2d{c}"], slot=f"x2b{i}", dbg=f"x2s{c}")
        b.epoch()
        s2a = [f"s2_{c}" for c in range(NCHUNK)]
        v2a = [f"v2_{c}" for c in range(NCHUNK)]
        tmps = b.sb("tmps", [128, NCHUNK], F32)
        b.vop("vector", lambda e: e.tensor_scalar(
            st["m2s"][:, :], st["m2s"][:, :], 1.0 / D, None, ALU.mult),
            s2a, ["m2"], dbg="m2")
        b.vop("vector", lambda e: e.tensor_scalar(
            st["r2s"][:, :], st["r2s"][:, :], 1.0 / D, None, ALU.mult),
            v2a, ["r2x"], dbg="r2x")
        b.vop("vector", lambda e: e.tensor_tensor(
            tmps[:, :], st["m2s"][:, :], st["m2s"][:, :], ALU.mult),
            ["m2"], ["tmps"], dbg="m2q")
        b.vop("vector", lambda e: e.tensor_tensor(
            st["r2s"][:, :], st["r2s"][:, :], tmps[:, :], ALU.subtract),
            ["r2x", "tmps"], ["r2y"], dbg="v2f")
        b.act(st["r2s"][:, :], st["r2s"][:, :], AF.Sqrt,
              reads=["r2y", "epsc"], writes=["r2z"], bias=epsc[:, 0:1],
              dbg="sd2")
        b.vop("vector", lambda e: e.reciprocal(st["r2s"][:, :],
                                               st["r2s"][:, :]),
              ["r2z"], ["r2"], dbg="rs2")
        b.vop("vector", lambda e: e.tensor_tensor(
            st["h2bs"][:, :], st["m2s"][:, :], st["r2s"][:, :], ALU.mult),
            ["m2", "r2"], ["h2b0"], dbg="h2b0")
        b.vop("vector", lambda e: e.tensor_scalar(
            st["h2bs"][:, :], st["h2bs"][:, :], -1.0, None, ALU.mult),
            ["h2b0"], ["h2bs"], dbg="h2b")

        x2s = b.sb("x2s", [128, CPS * D], F32)
        h2 = scrap
        h2T = b.sb("h2T", [128, 16 * 512], BF16)
        wtmp = [b.sb(f"wtmp{i}", [128, 2048], F32) for i in range(2)]
        winb = [b.sb(f"winb{i}", [128, 2048], BF16) for i in range(2)]
        xbc = b.sb("xbc", [128, 48 * 515], BF16)
        tailn = b.sb("tailn", [128, 48 * 4], BF16)
        zfm = [b.sb("zfm0", [128, 512], BF16)] * 2
        zst = b.sb("zst", [128, CPS * D], BF16)
        afm = b.sb("afm", [32, 512], F32)
        asp = b.sb("asp", [32, 512], F32)
        ats = b.sb("ats", [128, 32], F32)
        cssb = b.sb("cssb", [128, 32], F32)
        expA = b.sb("expA", [128, 32], F32)
        decay = b.sb("decay", [128, 32], F32)
        expTb = b.sb("expTb", [128, 32], F32)
        Xt = b.sb("Xt", [128, 2048], BF16)
        Bt = b.sb("Bt", [128, 2048], BF16)
        amb4 = b.sb("amb4", [128, 512], BF16)
        lmb4 = b.sb("lmb4", [128, 512], BF16)
        mpbh = b.sb("mpbh", [128, 1024], BF16)
        xdb = [b.sb(f"xdb{i}", [128, 64], BF16) for i in range(2)]
        XDt = b.sb("XDt", [128, D], BF16)
        yob = [b.sb(f"yob{i}", [128, 512], BF16) for i in range(2)]
        Sf = b.sb("Sf", [128, 16 * 64], F32)
        Sb_ = b.sb("Sb", [128, 16 * 64], BF16)
        Ybuf = b.sb("Ybuf", [128, D], BF16)
        ynT = b.sb("ynT", [128, 16 * 512], BF16)
        wob = [wtmp[0][:, 0:512], wtmp[1][:, 0:512]]
        wobb = [b.sb(f"wobb{i}", [128, 512], BF16) for i in range(2)]
        cdg = [b.sb(f"cdg{i}", [128, 128], BF16) for i in range(2)]

        skeys = [f"Sf{h}" for h in range(H)]
        sbkeys = [f"Sb{h}" for h in range(H)]
        b.vop("vector", lambda e: e.memset(Sf[:, :], 0.0), [], skeys, dbg="S0")
        b.vop("gpsimd", lambda e: e.memset(Sb_[:, :], 0.0), [], sbkeys,
              dbg="Sb0")

        def eguard(e):
            return e.memset(
                xbc.rearrange("p (t c) -> p t c", c=515)[:, :, 0:3], 0.0)
        b.vop("gpsimd", eguard, [], ["xbcg"], dbg="guard")

        for s in range(SWEEP):
            sw = f"s{s}"
            h2Tkeys = [f"h2T{dc}" for dc in range(16)]
            for k in range(CPS):
                c = s * CPS + k
                b.dma("pool", x2s[:, k * D:(k + 1) * D], x2d[c, :, :],
                      [f"x2d{c}"], [f"x2s{k}"], slot=f"x2sl{k}",
                      dbg=f"{sw}x2l{k}")
                b.act(h2[:, :], x2s[:, k * D:(k + 1) * D], AF.Identity,
                      reads=[f"x2s{k}", "r2", "h2bs"], writes=["h2", "scrap"],
                      scale=st["r2s"][:, c:c + 1], bias=st["h2bs"][:, c:c + 1],
                      dbg=f"{sw}h2{k}")
                for dc in range(16):
                    tb, tk = (psT, "psT") if dc % 2 == 0 else (psSt, "psSt")
                    b.mm(tb[:, 0:128], h2[:, dc * 128:(dc + 1) * 128],
                         cI["I_BF"][:, :], True, True, ["h2", "I_BF"], [tk],
                         dbg=f"{sw}hT{k}_{dc}")
                    dst = h2T[:, dc * 512 + k * 128: dc * 512 + k * 128 + 128]
                    if dc % 2 == 0:
                        def ev(e, dst=dst, tb=tb):
                            return e.tensor_copy(dst, tb[:, 0:128])
                        b.vop("vector", ev, [tk], [f"h2T{dc}"],
                              dbg=f"{sw}hTe{k}_{dc}")
                    else:
                        b.act(dst, tb[:, 0:128], AF.Identity, reads=[tk],
                              writes=[f"h2T{dc}"], dbg=f"{sw}hTe{k}_{dc}")
            if s > 0:
                def etail(e):
                    return e.tensor_copy(
                        xbc.rearrange("p (t c) -> p t c", c=515)[:, :, 0:3],
                        tailn.rearrange("p (t c) -> p t c", c=4)[:, :, 1:4])
                b.vop("gpsimd", etail, ["tailn"], ["xbcg"], dbg=f"{sw}tlr")

            for eb in range(NEB):
                i = eb % 2
                wkey = f"win{i}"
                if s == 0:
                    b.dma("sp", wtmp[i][:, :],
                          WA[eb].rearrange("p a j -> p (a j)"), [],
                          [f"wt{i}"], slot=f"wt{i}", dbg=f"{sw}wl{eb}")

                    def ecast(e, i=i):
                        return e.tensor_copy(winb[i][:, :], wtmp[i][:, :])
                    b.vop(("vector", "gpsimd")[i], ecast, [f"wt{i}"], [wkey],
                          dbg=f"{sw}wc{eb}")
                    b.dma("act", WINBF[eb, :, :], winb[i][:, :], [wkey],
                          [f"wbf{eb}"], slot=f"wb{i}", dbg=f"{sw}ws{eb}")
                else:
                    b.dma("sp", winb[i][:, :], WINBF[eb, :, :], [f"wbf{eb}"],
                          [wkey], slot=f"wb{i}", dbg=f"{sw}wl{eb}")
                pk = ("psA", "psB")[i]
                pso = (psA, psB)[i]
                for dc in range(16):
                    b.mm(pso[:, :], winb[i][:, dc * 128:(dc + 1) * 128],
                         h2T[:, dc * 512:(dc + 1) * 512], dc == 0, dc == 15,
                         [wkey] + h2Tkeys, [pk], dbg=f"{sw}pj{eb}_{dc}")
                if eb < 48:
                    dst = xbc[:, eb * 515 + 3: eb * 515 + 515]

                    def evx(e, dst=dst, pso=pso):
                        return e.tensor_copy(dst, pso[:, :])
                    b.vop("vector", evx, [pk], [f"xbcp{eb}"],
                          dbg=f"{sw}xe{eb}")
                    if s < SWEEP - 1 and FLAGS["tail_psum"]:
                        def etl(e, eb=eb, pso=pso):
                            return e.tensor_copy(
                                tailn[:, eb * 4:(eb + 1) * 4],
                                pso[:, 508:512])
                        b.vop("vector", etl, [pk], ["tailn"],
                              dbg=f"{sw}tlp{eb}")
                elif eb < 64:
                    zi = eb - 48
                    zb = zfm[0]
                    b.act(zb[:, :], pso[:, :], AF.Silu, reads=[pk, "zbias"],
                          writes=["zfm"], bias=zbias[:, zi:zi + 1],
                          dbg=f"{sw}zs{zi}")
                    for k in range(CPS):
                        tb, tk = (psT, "psT") if k % 2 == 0 else (psSt, "psSt")
                        b.mm(tb[:, 0:128], zb[:, k * 128:(k + 1) * 128],
                             cI["I_BF"][:, :], True, True,
                             ["zfm", "I_BF"], [tk],
                             dbg=f"{sw}zT{zi}_{k}")
                        dst = zst[:, k * D + zi * 128: k * D + zi * 128 + 128]
                        if k % 2 == 0:
                            def evz(e, dst=dst, tb=tb):
                                return e.tensor_copy(dst, tb[:, 0:128])
                            b.vop("vector", evz, [tk], [f"zst{k}"],
                                  dbg=f"{sw}zTe{zi}_{k}")
                        else:
                            b.act(dst, tb[:, 0:128], AF.Identity, reads=[tk],
                                  writes=[f"zst{k}"], dbg=f"{sw}zTe{zi}_{k}")
                else:
                    def eva(e, pso=pso):
                        return e.tensor_copy(asp[:, :], pso[0:32, :])
                    b.vop("vector", eva, [pk], ["asp"], dbg=f"{sw}ae")
                    b.act(afm[:, :], asp[:, :], AF.Abs, reads=["asp"],
                          writes=["afm0"], dbg=f"{sw}ab")
                    b.act(afm[:, :], afm[:, :], AF.Exp, reads=["afm0"],
                          writes=["afm0"], scale=-1.0, dbg=f"{sw}ax")
                    b.vop("vector", lambda e: e.tensor_scalar(
                        afm[:, :], afm[:, :], 1.0, None, ALU.add),
                        ["afm0"], ["afm0"], dbg=f"{sw}a1")
                    b.act(afm[:, :], afm[:, :], AF.Ln, reads=["afm0"],
                          writes=["afm0"], dbg=f"{sw}al")
                    b.act(asp[:, :], asp[:, :], AF.Relu, reads=["asp"],
                          writes=["asp"], dbg=f"{sw}ar")
                    b.vop("vector", lambda e: e.tensor_tensor(
                        afm[:, :], afm[:, :], asp[:, :], ALU.add),
                        ["afm0", "asp"], ["afm0"], dbg=f"{sw}aa")
                    b.vop("vector", lambda e: e.tensor_scalar(
                        afm[:, :], afm[:, :], -1.0, None, ALU.mult),
                        ["afm0"], ["afm"], dbg=f"{sw}an")

            for t in range(48):
                base = t * 515
                for w in range(4):
                    j = (t * 4 + w) % 2

                    def edg(e, t=t, w=w, j=j):
                        return e.tensor_scalar(
                            cdg[j][:, :], cI["I_BF"][:, :],
                            cw[:, t * 4 + w: t * 4 + w + 1], None, ALU.mult)
                    b.vop("vector" if FLAGS["conv_dve"] else "gpsimd", edg,
                          ["cw", "I_BF"], [f"cd{j}"], dbg=f"{sw}cd{t}_{w}")
                    b.mm(psS[:, :], cdg[j][:, :],
                         xbc[:, base + w: base + w + 512], w == 0, w == 3,
                         [f"cd{j}", f"xbcp{t}", "xbcg"], ["psS"],
                         dbg=f"{sw}cv{t}_{w}")

                if not FLAGS["tail_psum"] and s < SWEEP - 1:
                    def etn(e, t=t):
                        return e.tensor_copy(
                            tailn[:, t * 4 + 1:(t + 1) * 4],
                            xbc[:, t * 515 + 512: t * 515 + 515])
                    b.vop("vector", etn, [f"xbcp{t}"], ["tailn"],
                          dbg=f"{sw}tl{t}")
                b.act(xbc[:, base + 3: base + 515], psS[:, :], AF.Silu,
                      reads=["psS", "cb"], writes=[f"xbcp{t}"],
                      bias=cb[:, t:t + 1], dbg=f"{sw}cs{t}")

            if DEBUG_L1 and s == 0:
                b.dma("sp", XBCD, xbc[:, :],
                      [f"xbcp{t}" for t in range(48, slot="batch:dbg")], ["dbg1"], dbg="dXBC")
                b.dma("sp", ZSTD, zst[:, :],
                      [f"zst{k}" for k in range(CPS, slot="batch:dbg")], ["dbg2"], dbg="dZST")
                b.dma("sp", H2TD, h2T[:, :], h2Tkeys, ["dbg3"], dbg="dH2T", slot="batch:dbg")
            for k in range(CPS):
                c = s * CPS + k
                ck = k * 128
                for j in range(16):
                    b.mm(psT[:, 0:128],
                         xbc[:, j * 515 + 3 + ck: j * 515 + 3 + ck + 128],
                         cI["I_BF"][:, :], True, True, [f"xbcp{j}", "I_BF"],
                         ["psT"], dbg=f"{sw}Xt{k}_{j}")

                    def evxt(e, j=j):
                        return e.tensor_copy(Xt[:, j * 128:(j + 1) * 128],
                                             psT[:, 0:128])
                    b.vop("vector", evxt, ["psT"], ["Xt"],
                          dbg=f"{sw}Xte{k}_{j}")
                    b.mm(psSt[:, 0:128],
                         xbc[:, (16 + j) * 515 + 3 + ck:
                             (16 + j) * 515 + 3 + ck + 128],
                         cI["I_BF"][:, :], True, True,
                         [f"xbcp{16 + j}", "I_BF"], ["psSt"],
                         dbg=f"{sw}Bt{k}_{j}")

                    def evbt(e, j=j):
                        return e.activation(Bt[:, j * 128:(j + 1) * 128],
                                            psSt[:, 0:128], AF.Identity)
                    b.prog.op("scalar", evbt, ["psSt"], ["Bt"],
                              dbg=f"{sw}Bte{k}_{j}")
                b.mm(psE[:, 0:32], afm[0:32, ck:ck + 128],
                     cI["I_F32"][0:32, 0:32], True, True, ["afm", "I_F32"],
                     ["psE"], dbg=f"{sw}at{k}")

                def evat(e):
                    return e.tensor_copy(ats[:, :], psE[:, 0:32])
                b.vop("vector", evat, ["psE"], ["ats"], dbg=f"{sw}ate{k}")
                b.mm(psE[:, 0:32], cI["CUM_F32"][:, :], ats[:, :], True, True,
                     ["ats", "CUM_F32"], ["psE"], dbg=f"{sw}cs{k}")

                def evcs(e):
                    return e.tensor_copy(cssb[:, :], psE[:, 0:32])
                b.vop("vector", evcs, ["psE"], ["cssb"], dbg=f"{sw}cse{k}")
                b.act(expA[:, :], cssb[:, :], AF.Exp, reads=["cssb"],
                      writes=["expA"], dbg=f"{sw}eA{k}")
                b.mm(psE[:, 0:32], cI["GT_F32"][:, :], ats[:, :], True, True,
                     ["ats", "GT_F32"], ["psE"], dbg=f"{sw}dc{k}")
                b.act(decay[:, :], psE[:, 0:32], AF.Exp, reads=["psE"],
                      writes=["decay"], dbg=f"{sw}dcE{k}")
                def etb(e):
                    return e.tensor_tensor(expTb[:, :], expA[:, :],
                                           decay[:, :], ALU.mult)
                b.vop("vector", etb, ["expA", "decay"], ["expTb"],
                      dbg=f"{sw}tc{k}")

                def exdall(e):
                    return e.tensor_tensor(XDt[:, :], Xt[:, :], DREPb[:, :],
                                           ALU.mult)
                b.vop("vector", exdall, ["Xt", "DREPb"], ["XDt"],
                      dbg=f"{sw}xda{k}")

                for half in range(4):
                    hkey = f"yo{half % 2}"
                    for hh in range(8):
                        h = half * 8 + hh
                        r = h % 2
                        tile = h // 2
                        prow = (h % 2) * 64
                        bE, bEk = ((psE, "psE"), (psA, "psA"))[r]
                        bS, bSk = ((psS, "psS"), (psB, "psB"))[r]
                        bT, bTk = ((psSt, "psSt"), (psT, "psT"))[r]
                        fmB = xbc[:, (16 + tile) * 515 + 3 + ck:
                                  (16 + tile) * 515 + 3 + ck + 128]
                        fmC = xbc[:, (32 + tile) * 515 + 3 + ck:
                                  (32 + tile) * 515 + 3 + ck + 128]
                        fmBh = fmB[prow:prow + 64, :]
                        fmCh = fmC[prow:prow + 64, :]

                        def eam(e, r=r, h=h):
                            return e.tensor_scalar(
                                amb4[:, r * 128:(r + 1) * 128],
                                cI["CUM_BF"][:, :],
                                ats[:, h:h + 1], None, ALU.mult)
                        b.vop("vector", eam, ["ats", "CUM_BF"], [f"am{r}"],
                              dbg=f"{sw}am{k}_{h}")
                        b.mm(bE[:, 0:128], cI["GT_BF"][:, :],
                             amb4[:, r * 128:(r + 1) * 128],
                             True, False, [f"am{r}", "GT_BF"], [bEk],
                             dbg=f"{sw}E1{k}_{h}")
                        b.mm(bE[:, 0:128], cI["NEGLT_BF"][:, :],
                             cI["I_BF"][:, :], False, True,
                             ["NEGLT_BF", "I_BF"], [bEk],
                             dbg=f"{sw}E2{k}_{h}")
                        b.act(lmb4[:, r * 128:(r + 1) * 128], bE[:, 0:128],
                              AF.Exp, reads=[bEk], writes=[f"lm{r}"],
                              dbg=f"{sw}L{k}_{h}")
                        b.mm(bS[:, 0:128], fmBh, fmCh, True, True,
                             [f"xbcp{16 + tile}", f"xbcp{32 + tile}"], [bSk],
                             dbg=f"{sw}sc{k}_{h}")

                        def emp(e, r=r, bS=bS):
                            return e.tensor_tensor(
                                mpbh[:, r * 128:(r + 1) * 128],
                                lmb4[:, r * 128:(r + 1) * 128],
                                bS[:, 0:128], ALU.mult)
                        b.vop("vector", emp, [f"lm{r}", bSk], [f"mp{r}"],
                              dbg=f"{sw}M{k}_{h}")

                        def exd(e, r=r, h=h):
                            return e.tensor_scalar(
                                xdb[r][:, :], Xt[:, h * 64:(h + 1) * 64],
                                decay[:, h:h + 1], None, ALU.mult)
                        b.vop("vector", exd, ["Xt", "decay"], [f"xd{r}"],
                              dbg=f"{sw}xd{k}_{h}")
                        b.mm(psYa[:, hh * 64:(hh + 1) * 64],
                             mpbh[:, r * 128:(r + 1) * 128],
                             Xt[:, h * 64:(h + 1) * 64], True, True,
                             [f"mp{r}", "Xt"], ["psYa"],
                             dbg=f"{sw}Yd{k}_{h}")
                        if prow == 0:
                            stl = Bt[:, h * 64:(h + 1) * 64]
                            sto = bT[0:64, 0:64]
                        else:
                            stl = Bt[:, (h - 1) * 64:(h + 1) * 64]
                            sto = bT[:, 0:64]
                        b.mm(sto, stl, xdb[r][:, :],
                             True, True, ["Bt", f"xd{r}"], [bTk],
                             dbg=f"{sw}st{k}_{h}")
                        b.mm(psYb[:, hh * 64:(hh + 1) * 64],
                             fmCh,
                             Sb_[prow:prow + 64, tile * 64:(tile + 1) * 64],
                             True, True,
                             [f"xbcp{32 + tile}", f"Sb{h}"], ["psYb"],
                             dbg=f"{sw}Yo{k}_{h}")

                        def est(e, h=h, tile=tile, prow=prow, bT=bT):
                            return e.scalar_tensor_tensor(
                                Sf[prow:prow + 64, tile * 64:(tile + 1) * 64],
                                Sf[prow:prow + 64, tile * 64:(tile + 1) * 64],
                                expTb[prow:prow + 64, h:h + 1],
                                bT[prow:prow + 64, 0:64],
                                ALU.mult, ALU.add)
                        b.vop("vector", est,
                              [bTk, "expTb", f"Sf{h}"], [f"Sf{h}"],
                              dbg=f"{sw}Su{k}_{h}")
                    for hh in range(8):
                        h = half * 8 + hh
                        b.act(yob[half % 2][:, hh * 64:(hh + 1) * 64],
                              psYb[:, hh * 64:(hh + 1) * 64], AF.Identity,
                              reads=["psYb", "expA"], writes=[hkey],
                              scale=expA[:, h:h + 1], dbg=f"{sw}yoe{k}_{h}")

                    def eass(e, half=half):
                        return e.tensor_tensor(
                            Ybuf[:, half * 512:(half + 1) * 512], psYa[:, :],
                            yob[half % 2][:, :], ALU.add)
                    b.vop("vector", eass, ["psYa", hkey], ["Ybuf"],
                          dbg=f"{sw}Ya{k}_{half}")

                    def eassB(e, half=half):
                        return e.tensor_tensor(
                            Ybuf[:, half * 512:(half + 1) * 512],
                            Ybuf[:, half * 512:(half + 1) * 512],
                            XDt[:, half * 512:(half + 1) * 512], ALU.add)
                    b.vop("vector", eassB, ["Ybuf", "XDt"], ["Ybuf"],
                          dbg=f"{sw}Yb{k}_{half}")
                skl = [f"Sf{h}" for h in range(H)]
                sbl = [f"Sb{h}" for h in range(H)]

                def esb(e):
                    return e.copy(Sb_[:, :], Sf[:, :])
                b.vop("scalar", esb, skl, sbl, dbg=f"{sw}Sc{k}")

                if DEBUG_L1 and s == 0 and k == 0:
                    b.dma("sp", ATSD, ats[:, :], ["ats"], ["dbg4"], dbg="dATS", slot="batch:dbg")
                    b.dma("sp", EXPAD, expA[:, :], ["expA"], ["dbg5"],
                          dbg="dEXP", slot="batch:dbg")
                    b.dma("sp", YBD, Ybuf[:, :], ["Ybuf"], ["dbg6"], dbg="dYB", slot="batch:dbg")

                def egate(e, k=k):
                    return e.tensor_tensor(Ybuf[:, :], Ybuf[:, :],
                                           zst[:, k * D:(k + 1) * D],
                                           ALU.mult)
                b.vop("vector", egate, ["Ybuf", f"zst{k}"], ["Ybuf"],
                      dbg=f"{sw}gt{k}")
                b.act(scrap[:, :], Ybuf[:, :], AF.Square, reads=["Ybuf"],
                      writes=["scrap", "h2", "ms0"], accum=stc[:, 0:1],
                      dbg=f"{sw}ms{k}")
                b.vop("vector", lambda e: e.tensor_scalar(
                    stc[:, 1:2], stc[:, 0:1], 1.0 / D, epsc[:, 0:1],
                    ALU.mult, ALU.add),
                    ["ms0", "epsc"], ["ms1"], dbg=f"{sw}msd{k}")
                if FLAGS["lnexp"]:
                    b.act(stc[:, 1:2], stc[:, 1:2], AF.Ln,
                          reads=["ms1"], writes=["ms2"], dbg=f"{sw}msq{k}")
                    b.act(stc[:, 2:3], stc[:, 1:2], AF.Exp, scale=-0.5,
                          reads=["ms2"], writes=["ms3"], dbg=f"{sw}msr{k}")
                else:
                    b.act(stc[:, 1:2], stc[:, 1:2], AF.Sqrt,
                          reads=["ms1"], writes=["ms2"], dbg=f"{sw}msq{k}")
                    b.vop("vector", lambda e: e.reciprocal(stc[:, 2:3],
                                                           stc[:, 1:2]),
                          ["ms2"], ["ms3"], dbg=f"{sw}msr{k}")

                def eyn(e):
                    return e.tensor_scalar(Ybuf[:, :], Ybuf[:, :], stc[:, 2:3],
                                           None, ALU.mult)
                b.vop("vector", eyn, ["Ybuf", "ms3"], ["Ybuf"],
                      dbg=f"{sw}yn{k}")
                for ec in range(16):
                    tb, tk = (psT, "psT") if ec % 2 == 0 else (psSt, "psSt")
                    b.mm(tb[:, 0:128], Ybuf[:, ec * 128:(ec + 1) * 128],
                         cI["I_BF"][:, :], True, True, ["Ybuf", "I_BF"],
                         [tk], dbg=f"{sw}yT{k}_{ec}")
                    dst = ynT[:, ec * 512 + k * 128: ec * 512 + k * 128 + 128]
                    if ec % 2 == 0:
                        def evy(e, dst=dst, tb=tb):
                            return e.tensor_copy(dst, tb[:, 0:128])
                        b.vop("vector", evy, [tk], [f"ynT{ec}"],
                              dbg=f"{sw}yTe{k}_{ec}")
                    else:
                        b.act(dst, tb[:, 0:128], AF.Identity, reads=[tk],
                              writes=[f"ynT{ec}"], dbg=f"{sw}yTe{k}_{ec}")

            ynTall = [f"ynT{ec}" for ec in range(16)]
            for sd in range(4):
                for ec in range(16):
                    i = (sd * 16 + ec) % 2
                    wkey = f"wo{i}"
                    if s == 0:
                        b.dma("sp", wob[i], WB[sd, ec, :, :], [],
                              [f"wt{i}"], slot=f"wt{i}", dbg=f"{sw}ol{sd}_{ec}")
                        b.act(wobb[i][:, :], wob[i], AF.Identity,
                              reads=[f"wt{i}", "rmsw"], writes=[wkey],
                              scale=rmsw[:, ec:ec + 1], dbg=f"{sw}oc{sd}_{ec}")
                        b.dma("act", WOUTBF[sd, ec, :, :], wobb[i][:, :],
                              [wkey], [f"obf{sd}_{ec}"], slot=f"wob{i}",
                              dbg=f"{sw}os{sd}_{ec}")
                    else:
                        b.dma("sp", wobb[i][:, :], WOUTBF[sd, ec, :, :],
                              [f"obf{sd}_{ec}"], [wkey], slot=f"wob{i}",
                              dbg=f"{sw}ol{sd}_{ec}")
                    for k in range(CPS):
                        pk = ("psA", "psB", "psE", "psS")[k]
                        pso = (psA, psB, psE, psS)[k]
                        b.mm(pso[:, :],
                             ynT[:, ec * 512 + k * 128: ec * 512 + k * 128 + 128],
                             wobb[i][:, :], ec == 0, ec == 15,
                             [wkey] + ynTall, [pk], dbg=f"{sw}om{sd}_{ec}_{k}")
                        if ec == 15:
                            def eres(e, k=k, sd=sd, pso=pso):
                                return e.tensor_tensor(
                                    x2s[:, k * D + sd * 512:
                                        k * D + sd * 512 + 512],
                                    x2s[:, k * D + sd * 512:
                                        k * D + sd * 512 + 512],
                                    pso[:, :], ALU.add)
                            b.vop("vector", eres, [pk, f"x2s{k}"],
                                  [f"x2s{k}"], dbg=f"{sw}rs{sd}_{k}")
            for k in range(CPS):
                c = s * CPS + k
                xk = f"x2s{k}"
                b.act(scrap[:, :], x2s[:, k * D:(k + 1) * D], AF.Identity,
                      reads=[xk], writes=["scrap", "h2", f"s3_{c}"],
                      accum=st["m3s"][:, c:c + 1], dbg=f"{sw}s3{k}")
                b.act(scrap[:, :], x2s[:, k * D:(k + 1) * D], AF.Square,
                      reads=[xk], writes=["scrap", "h2", f"v3_{c}"],
                      accum=st["r3s"][:, c:c + 1], dbg=f"{sw}v3{k}")

                def emx3(e, c=c, k=k):
                    return e.reduce_max(st["mx3s"][:, c:c + 1],
                                        x2s[:, k * D:(k + 1) * D], axis=AX.X)
                b.vop("vector", emx3, [xk], [f"mh{c}"], dbg=f"{sw}mh{k}")

                def emn3(e, c=c, k=k):
                    return e.tensor_reduce(st["mn3s"][:, c:c + 1],
                                           x2s[:, k * D:(k + 1) * D],
                                           axis=AX.X, op=ALU.min)
                b.vop("vector", emn3, [xk], [f"ml{c}"], dbg=f"{sw}ml{k}")
                b.dma("sp", X3[c, :, :], x2s[:, k * D:(k + 1) * D], [xk],
                      [f"x3o{c}"], slot=f"x2sl{k}", dbg=f"{sw}x3s{k}")

        s3a = [f"s3_{c}" for c in range(NCHUNK)]
        v3a = [f"v3_{c}" for c in range(NCHUNK)]
        mha = [f"mh{c}" for c in range(NCHUNK)]
        mla = [f"ml{c}" for c in range(NCHUNK)]
        tmp3 = b.sb("tmp3", [128, NCHUNK], F32)
        b.vop("vector", lambda e: e.tensor_scalar(
            st["m3s"][:, :], st["m3s"][:, :], 1.0 / D, None, ALU.mult),
            s3a, ["m3"], dbg="m3")
        b.vop("vector", lambda e: e.tensor_scalar(
            st["r3s"][:, :], st["r3s"][:, :], 1.0 / D, None, ALU.mult),
            v3a, ["r3x"], dbg="r3x")
        b.vop("vector", lambda e: e.tensor_tensor(
            tmp3[:, :], st["m3s"][:, :], st["m3s"][:, :], ALU.mult),
            ["m3"], ["tmp3"], dbg="m3q")
        b.vop("vector", lambda e: e.tensor_tensor(
            st["r3s"][:, :], st["r3s"][:, :], tmp3[:, :], ALU.subtract),
            ["r3x", "tmp3"], ["r3y"], dbg="v3f")
        b.act(st["r3s"][:, :], st["r3s"][:, :], AF.Sqrt,
              reads=["r3y", "epsc"], writes=["r3z"], bias=epsc[:, 0:1],
              dbg="sd3")
        b.vop("vector", lambda e: e.reciprocal(st["r3s"][:, :],
                                               st["r3s"][:, :]),
              ["r3z"], ["r3"], dbg="rs3")
        b.vop("vector", lambda e: e.tensor_tensor(
            st["mx3s"][:, :], st["mx3s"][:, :], st["m3s"][:, :], ALU.subtract),
            mha + ["m3"], ["g3a"], dbg="g3a")
        b.vop("vector", lambda e: e.tensor_tensor(
            st["mn3s"][:, :], st["m3s"][:, :], st["mn3s"][:, :], ALU.subtract),
            mla + ["m3"], ["g3b"], dbg="g3b")
        b.vop("vector", lambda e: e.tensor_tensor(
            st["mx3s"][:, :], st["mx3s"][:, :], st["mn3s"][:, :], ALU.max),
            ["g3a", "g3b"], ["g3c"], dbg="g3c")
        b.vop("vector", lambda e: e.tensor_tensor(
            st["mx3s"][:, :], st["mx3s"][:, :], st["r3s"][:, :], ALU.mult),
            ["g3c", "r3"], ["g3"], dbg="g3")

        def ea3(e):
            return e.reduce_max(acol3[:, 0:1], st["mx3s"][:, :], axis=AX.X)
        b.vop("vector", ea3, ["g3"], ["ac3"], dbg="ac3")
        col_max_to_scalar(b, acol3[:, 0:1], "ac3", one3[0:1, 0:1], "a3", "a3",
                          rowb[0:1, :], "rw3")
        b.dma("sp", M3, st["m3s"][:, :], ["m3"], ["om3"], slot="batch:outs",
              dbg="st_m3")
        b.dma("sp", R3, st["r3s"][:, :], ["r3"], ["or3"], slot="batch:outs",
              dbg="st_r3")
        b.dma("sp", AMAX3, one3[0:1, 0:1], ["a3"], ["oa3"],
              slot="batch:outs", dbg="st_a3")
        b.fence(["om3", "or3", "oa3"] + [f"x3o{c}" for c in range(NCHUNK)])
        run_streams(nc, b.es, b.prog)
    return nc



def build_L2():
    b = B()
    nc = b.nc
    X3 = b.dram_in("x3", [NCHUNK, 128, D], F32)
    b.dram_in("WQT", [16, 128, 2048], BF16)
    M3 = b.dram_in("m3", [128, NCHUNK], F32)
    R3 = b.dram_in("r3", [128, NCHUNK], F32)
    A3V = b.dram_in("amax3v", [1, 8], F32)
    OUT = b.dram_out("out", [NCHUNK, 128, D], F32)
    with b.es:
        cI = b.load_consts(["I_F8"])
        psA = b.ps("psA", [128, 512])
        psB = b.ps("psB", [128, 512])
        psE = b.ps("psE", [128, 512])
        psS = b.ps("psS", [128, 512])
        psT = b.ps("psT", [128, 512])
        psSt = b.ps("psSt", [128, 512])
        m3 = b.sb("m3s", [128, NCHUNK], F32)
        r3 = b.sb("r3s", [128, NCHUNK], F32)
        qsc = b.sb("qsc", [128, NCHUNK], F32)
        qbs = b.sb("qbs", [128, NCHUNK], F32)
        bc = b.sb("bcb", [128, 4], F32)
        a3row = b.sb("a3row", [1, 8], F32)
        warow = b.sb("warow", [1, 1], F32)
        rowb = b.sb("rowb", [1, 128], F32)
        b.dma("sp", m3[:, :], M3, [], ["m3"], dbg="ld_m3")
        b.dma("sp", r3[:, :], R3, [], ["r3"], dbg="ld_r3")
        b.dma("sp", a3row[0:1, :], A3V, [], ["a3row"], dbg="ld_a3")
        wq = b.sb("wq", [128, 16 * 2048], FP8)
        tiles, wmx = load_wq(b, b.es, bc, wq)
        col_max_to_scalar(b, wmx[:, 0:1], "wmx0", warow[0:1, 0:1], "wam",
                          "w2", rowb[0:1, :], "rowbufW")
        ffn_scalars(b, a3row[0:1, :], "a3row", warow[0:1, 0:1], "wam", bc,
                    "f2")
        qscale_cols(b, qsc, qbs, m3, "m3", r3, "r3", bc)
        quant_wq(b, tiles, wq)
        xb = [b.sb(f"xb{i}", [128, D], F32) for i in range(2)]
        q1 = [b.sb(f"q1_{i}", [128, D], FP8) for i in range(2)]
        q1T = [b.sb(f"q1T_{i}", [128, D], FP8) for i in range(2)]
        f1 = [b.sb(f"f1_{i}", [128, D], BF16) for i in range(2)]
        ob = [b.sb(f"ob{i}", [128, D], F32) for i in range(2)]
        outk = []
        for c in range(NCHUNK):
            i = c % 2
            b.dma("pool", xb[i][:, :], X3[c, :, :], [], [f"xb{i}"],
                  slot=f"xb{i}", dbg=f"x3l{c}")
            ffn_chunk(b, c, xb[i][:, :], f"xb{i}", qsc, qbs, wq,
                      [(psA, "psA"), (psB, "psB"), (psE, "psE"),
                       (psS, "psS")],
                      [(psT, "psT"), (psSt, "psSt")],
                      q1[i], q1T[i], f"q1{i}", f"q1T{i}", f1[i], f"f1{i}",
                      cI["I_F8"], bc)

            def eadd(e, i=i):
                return e.tensor_tensor(ob[i][:, :], xb[i][:, :], f1[i][:, :],
                                       ALU.add)
            b.vop("vector", eadd, [f"xb{i}", f"f1{i}"], [f"ob{i}"],
                  dbg=f"oa{c}")
            b.dma("sp", OUT[c, :, :], ob[i][:, :], [f"ob{i}"], [f"oo{c}"],
                  slot=f"ob{i}", dbg=f"os{c}")
            outk.append(f"oo{c}")
        b.fence(outk)
        run_streams(nc, b.es, b.prog)
    return nc



def _host_prep(W_in, W_out, ffn_w, conv_w, conv_b, z_bias, rms_w, Dv):
    Wp = np.zeros((NEB * 128, D), np.float32)
    Wp[:PROJ] = np.asarray(W_in, np.float32)
    WA = np.ascontiguousarray(Wp.reshape(NEB, 128, 16, 128)
                              .transpose(0, 3, 2, 1))
    WB = np.ascontiguousarray(np.asarray(W_out, np.float32)
                              .reshape(4, 512, 16, 128).transpose(0, 2, 3, 1))
    WQT = np.ascontiguousarray(np.asarray(ffn_w).T.reshape(16, 128, 2048))
    CWH = np.ascontiguousarray(np.asarray(conv_w, np.float32)
                               .reshape(48, 128, 4).transpose(1, 0, 2)
                               .reshape(128, 192))
    CBH = np.ascontiguousarray(np.asarray(conv_b, np.float32)
                               .reshape(48, 128).T)
    ZBH = np.ascontiguousarray(np.asarray(z_bias, np.float32)
                               .reshape(16, 128).T)
    RWH = np.ascontiguousarray(np.asarray(rms_w, np.float32)
                               .reshape(16, 128).T)
    DREP = np.ascontiguousarray(np.tile(
        np.repeat(np.asarray(Dv, np.float32), 64)[None, :], (128, 1)))
    return WA, WB, WQT, CWH, CBH, ZBH, RWH, DREP


def kernel(L0_attn_u, L0_attn_W_in, L0_attn_z_bias, L0_attn_conv_w,
           L0_attn_conv_b, L0_attn_D, L0_attn_rms_w, L0_attn_W_out,
           L0_ffn_weight):
    u = np.ascontiguousarray(np.asarray(L0_attn_u, np.float32))
    WA, WB, WQT, CWH, CBH, ZBH, RWH, DD = _host_prep(
        L0_attn_W_in, L0_attn_W_out, L0_ffn_weight, L0_attn_conv_w,
        L0_attn_conv_b, L0_attn_z_bias, L0_attn_rms_w, L0_attn_D)
    ushards = [np.ascontiguousarray(u[i].reshape(NCHUNK, 128, D))
               for i in range(NCORES)]
    cores = list(range(NCORES))

    nc0 = build_L0()
    r0 = run_bass_kernel_spmd(
        nc0, [dict(u=ushards[i]) for i in range(NCORES)],
        core_ids=cores)
    amax1v = np.ascontiguousarray(
        np.stack([np.asarray(r0.results[i]["amax1"]).reshape(())
                  for i in range(NCORES)]).reshape(1, 8).astype(np.float32))

    nc1 = build_L1()
    in1 = [dict(u=ushards[i], WA=WA, WB=WB, WQT=WQT,
                m1=np.asarray(r0.results[i]["m1"]),
                r1=np.asarray(r0.results[i]["r1"]),
                amax1v=amax1v, cwh=CWH, cbh=CBH, zbh=ZBH,
                rwh=RWH, drep=DD) for i in range(NCORES)]
    r1 = run_bass_kernel_spmd(nc1, in1, core_ids=cores)
    amax3v = np.ascontiguousarray(
        np.stack([np.asarray(r1.results[i]["amax3"]).reshape(())
                  for i in range(NCORES)]).reshape(1, 8).astype(np.float32))

    nc2 = build_L2()
    in2 = [dict(x3=np.asarray(r1.results[i]["x3"]), WQT=WQT,
                m3=np.asarray(r1.results[i]["m3"]),
                r3=np.asarray(r1.results[i]["r3"]),
                amax3v=amax3v) for i in range(NCORES)]
    r2 = run_bass_kernel_spmd(nc2, in2, core_ids=cores)
    out = np.stack([np.asarray(r2.results[i]["out"]).reshape(T, D)
                    for i in range(NCORES)])
    return np.ascontiguousarray(out.astype(np.float32))


# version 12
# speedup vs baseline: 1.2606x; 1.2606x over previous
import contextlib

import numpy as np
import ml_dtypes

import concourse.bass as bass
import concourse.mybir as mybir
from concourse.bass_utils import run_bass_kernel_spmd

F32 = mybir.dt.float32
BF16 = mybir.dt.bfloat16
FP8 = mybir.dt.float8e4
AF = mybir.ActivationFunctionType
ALU = mybir.AluOpType
AX = mybir.AxisListType

NCORES = 8
DEBUG_L1 = False
FLAGS = {"conv_dve": True, "tail_psum": True, "sbf_act": True,
         "tbank": True, "lnexp": True}
T = 2048
D = 2048
NCHUNK = 16
H = 32
PROJ = 8224
NEB = 65
SWEEP = 4
CPS = 4



class _Op:
    __slots__ = ("eng", "emit", "reads", "writes", "dma", "slot", "idx",
                 "deps", "need_inc", "rank", "dbg")

    def __init__(self, eng, emit, reads, writes, dma, slot, idx, dbg):
        self.eng = eng
        self.emit = emit
        self.reads = reads
        self.writes = writes
        self.dma = dma
        self.slot = slot
        self.idx = idx
        self.deps = []
        self.need_inc = False
        self.rank = None
        self.dbg = dbg


QUEUE_ENG = {"sp": "sync", "act": "scalar", "pool": "gpsimd"}
ENGS = ("tensor", "vector", "scalar", "gpsimd", "sync")


class Prog:

    def __init__(self):
        self.ops = []
        self.last_write = {}
        self.readers = {}
        self.epoch_idx = None

    def op(self, eng, emit, reads=(), writes=(), dma=None, slot=None, dbg=""):
        if dma is not None:
            assert slot is not None, dbg
            qt = "sw" if dma == "pool" else "hw"
            if slot.startswith("batch:"):
                slot = f"batch:{qt}:{slot[6:]}"
            else:
                slot = f"{qt}:{slot}"
        o = _Op(eng, emit, tuple(reads), tuple(writes), dma, slot,
                len(self.ops), dbg)
        deps = set()
        if self.epoch_idx is not None and self.epoch_idx != len(self.ops):
            deps.add(self.epoch_idx)
        for k in o.reads:
            w = self.last_write.get(k)
            if w is not None:
                deps.add(w)
        for k in o.writes:
            w = self.last_write.get(k)
            if w is not None:
                deps.add(w)
            for r in self.readers.get(k, ()):
                deps.add(r)
        o.deps = sorted(deps)
        for k in o.reads:
            self.readers.setdefault(k, []).append(o.idx)
        for k in o.writes:
            self.last_write[k] = o.idx
            self.readers[k] = []
        self.ops.append(o)
        return o

    def finalize(self, nc, es):
        ops = self.ops
        for o in ops:
            for d in o.deps:
                dep = ops[d]
                if (dep.dma is None and o.dma is None
                        and dep.eng == "tensor" and o.eng == "tensor"):
                    continue
                dep.need_inc = True
        eng_count = {e: 0 for e in ENGS}
        slot_count = {}
        slot_total = {}
        for o in ops:
            if o.dma is not None:
                slot_count[o.slot] = slot_count.get(o.slot, 0) + 16
                o.rank = slot_count[o.slot]
                o.need_inc = True
            elif o.need_inc:
                eng_count[o.eng] += 1
                o.rank = eng_count[o.eng]
        slot_total = dict(slot_count)
        sems = {e: es.enter_context(nc.semaphore(f"sem_{e}"))
                for e in ("tensor", "vector", "scalar", "gpsimd")}
        dsems = {s: es.enter_context(nc.semaphore(f"ds_{si}"))
                 for si, s in enumerate(slot_total)}
        streams = {e: [] for e in ENGS}
        watermark = {e: {} for e in ENGS}
        for o in ops:
            my_eng = QUEUE_ENG[o.dma] if o.dma is not None else o.eng
            waits = {}
            for d in o.deps:
                dep = ops[d]
                if dep.dma is not None:
                    key = ("d", dep.slot)
                    if dep.slot.startswith("batch:"):
                        val = slot_total[dep.slot]
                    else:
                        val = dep.rank
                else:
                    if (dep.eng == "tensor" and o.dma is None
                            and o.eng == "tensor"):
                        continue
                    key, val = ("e", dep.eng), dep.rank
                    assert val is not None, (o.dbg, dep.dbg)
                if waits.get(key, 0) < val:
                    waits[key] = val
            wm = watermark[my_eng]
            items = []
            for key, val in sorted(waits.items()):
                if wm.get(key, 0) >= val:
                    continue
                wm[key] = val
                items.append((key, val))
            streams[my_eng].append((o, items))
        return streams, sems, dsems


def run_streams(nc, es, prog):
    streams, sems, dsems = prog.finalize(nc, es)

    def emit(engobj, name):
        for o, waits in streams[name]:
            for (kind, qn), val in waits:
                engobj.wait_ge(dsems[qn] if kind == "d" else sems[qn], val)
            inst = o.emit(engobj)
            if o.need_inc:
                if o.dma is not None:
                    inst.then_inc(dsems[o.slot], 16)
                else:
                    inst.then_inc(sems[o.eng], 1)

    block = es.enter_context(nc.Block())

    @block.tensor
    def _(e):
        emit(e, "tensor")

    @block.vector
    def _(e):
        emit(e, "vector")

    @block.scalar
    def _(e):
        emit(e, "scalar")

    @block.gpsimd
    def _(e):
        emit(e, "gpsimd")

    @block.sync
    def _(e):
        emit(e, "sync")



def _consts_np():
    k = np.arange(128)
    cum = (k[:, None] <= k[None, :]).astype(np.float32)
    gt = (k[:, None] > k[None, :]).astype(np.float32)
    neglt = np.where(k[:, None] < k[None, :], np.float32(-1e30), np.float32(0))
    return {
        "I_BF": np.eye(128, dtype=np.float32).astype(ml_dtypes.bfloat16),
        "I_F8": np.eye(128, dtype=np.float32).astype(ml_dtypes.float8_e4m3fn),
        "I_F32": np.eye(128, dtype=np.float32),
        "CUM_BF": cum.astype(ml_dtypes.bfloat16),
        "CUM_F32": cum,
        "GT_BF": gt.astype(ml_dtypes.bfloat16),
        "GT_F32": gt,
        "NEGLT_BF": neglt.astype(ml_dtypes.bfloat16),
        "I4_BF": np.tile(np.eye(128, dtype=np.float32), (1, 4)).astype(
            ml_dtypes.bfloat16),
        "ONES_F32": np.ones((128, 128), np.float32),
    }


_NPDT = {"I_BF": BF16, "I_F8": FP8, "I_F32": F32, "CUM_BF": BF16,
         "CUM_F32": F32, "GT_BF": BF16, "GT_F32": F32, "NEGLT_BF": BF16,
         "I4_BF": BF16, "ONES_F32": F32}


class B:
    def __init__(self):
        self.nc = bass.Bass("TRN2", target_bir_lowering=False)
        self.es = contextlib.ExitStack()
        self.prog = Prog()
        self.t = {}

    def dram_in(self, name, shape, dtype):
        ap = self.nc.dram_tensor(name, list(shape), dtype,
                                 kind="ExternalInput").ap()
        self.t[name] = ap
        return ap

    def dram_out(self, name, shape, dtype):
        ap = self.nc.dram_tensor(name, list(shape), dtype,
                                 kind="ExternalOutput").ap()
        self.t[name] = ap
        return ap

    def dram_tmp(self, name, shape, dtype):
        ap = self.nc.dram_tensor(name, list(shape), dtype, kind="Internal").ap()
        self.t[name] = ap
        return ap

    def sb(self, name, shape, dtype, es=None):
        h = (es or self.es).enter_context(
            self.nc.sbuf_tensor(name, list(shape), dtype))
        self.t[name] = h
        return h

    def ps(self, name, shape):
        h = self.es.enter_context(self.nc.psum_tensor(name, list(shape), F32))
        self.t[name] = h
        return h

    def load_consts(self, names):
        cn = _consts_np()
        out = {}
        for name in names:
            arr = np.ascontiguousarray(cn[name])
            if _NPDT[name] == FP8:
                dr = self.nc.inline_tensor(
                    arr.view(np.uint8), name + "_d").ap().bitcast(FP8)
            else:
                dr = self.nc.inline_tensor(arr, name + "_d").ap()
            t = self.sb(name, list(arr.shape), _NPDT[name])
            self.dma("sp", t[:, :], dr, [], [name], slot="batch:consts",
                     dbg=f"cst_{name}")
            out[name] = t
        return out

    def dma(self, q, out, in_, reads, writes, slot=None, slow=False, dbg=""):
        def em(e, out=out, in_=in_, slow=slow):
            return e.dma_start(out=out, in_=in_,
                               allow_slow_non_contiguous=slow)
        self.prog.op(QUEUE_ENG[q], em, reads, writes, dma=q,
                     slot=slot or "batch:misc", dbg=dbg)

    def mm(self, out, lhsT, rhs, start, stop, reads, writes, dbg="",
           perf_mode=None):
        def em(e, out=out, lhsT=lhsT, rhs=rhs, start=start, stop=stop,
               perf_mode=perf_mode):
            return e.matmul(out, lhsT, rhs, start=start, stop=stop,
                            perf_mode=perf_mode)
        self.prog.op("tensor", em, reads, writes, dbg=dbg)

    def act(self, out, in_, func, reads, writes, bias=0.0, scale=1.0,
            accum=None, dbg=""):
        def em(e, out=out, in_=in_, func=func, bias=bias, scale=scale,
               accum=accum):
            return e.activation(out, in_, func, bias=bias, scale=scale,
                                accum_out=accum)
        self.prog.op("scalar", em, reads, writes, dbg=dbg)

    def vop(self, eng, fn, reads, writes, dbg=""):
        self.prog.op(eng, fn, reads, writes, dbg=dbg)

    def fence(self, keys):
        self.prog.op("gpsimd", lambda e: e.nop(), keys, [], dbg="fence")

    def epoch(self):
        keys = sorted(set(self.prog.last_write) | set(self.prog.readers))
        o = self.prog.op("gpsimd", lambda e: e.nop(), keys, keys, dbg="epoch")
        self.prog.epoch_idx = o.idx



def col_max_to_scalar(b, col_ap, col_key, out_ap, out_key, tag, row_sb,
                      row_key):
    bounce = b.dram_tmp("bnc_" + tag, [128], F32)
    b.dma("pool", bounce, col_ap, [col_key], ["bnc" + tag], slot="bn" + tag,
          slow=True, dbg="bw" + tag)
    b.dma("pool", row_sb, bounce.rearrange("(a k) -> a k", a=1),
          ["bnc" + tag], [row_key], slot="bn" + tag, dbg="br" + tag)

    def em(e):
        return e.reduce_max(out_ap, row_sb, axis=AX.X)
    b.vop("vector", em, [row_key], [out_key], dbg="bm" + tag)


def bcast_row(b, row_ap, row_keys, n, out_ap, out_key, tag):
    bounce = b.dram_tmp("bcr_" + tag, [1, n], F32)
    b.dma("pool", bounce, row_ap, row_keys, ["bcr" + tag], slot="bc" + tag,
          dbg="cw" + tag)
    b.dma("pool", out_ap, bounce.broadcast_to([128, n]), ["bcr" + tag],
          [out_key], slot="bc" + tag, slow=True, dbg="cr" + tag)


def ffn_scalars(b, amax_row_ap, amax_row_key, wam_ap, wam_key, bc, tag,
                es=None):
    sc = b.sb("sc_" + tag, [1, 4], F32, es=es)
    row = b.sb("scrow_" + tag, [1, 4], F32, es=es)

    def e0(e):
        return e.reduce_max(sc[0:1, 0:1], amax_row_ap, axis=AX.X)
    b.vop("vector", e0, [amax_row_key], [tag + "s0"], dbg=tag + "mx")
    b.vop("vector", lambda e: e.tensor_scalar(
        sc[0:1, 0:1], sc[0:1, 0:1], 1.0 / 448.0, 1e-12, ALU.mult, ALU.max),
        [tag + "s0"], [tag + "s0"], dbg=tag + "s")
    b.vop("vector", lambda e: e.tensor_scalar(
        sc[0:1, 1:2], wam_ap, 1.0 / 448.0, 1e-12, ALU.mult, ALU.max),
        [wam_key], [tag + "s1"], dbg=tag + "sw")
    b.vop("vector", lambda e: e.reciprocal(sc[0:1, 2:3], sc[0:1, 0:1]),
          [tag + "s0"], [tag + "s2"], dbg=tag + "inv")
    b.vop("vector", lambda e: e.tensor_scalar(
        sc[0:1, 2:3], sc[0:1, 2:3], 0.5, None, ALU.mult),
        [tag + "s2"], [tag + "s2"], dbg=tag + "half")
    b.vop("vector", lambda e: e.tensor_tensor(
        sc[0:1, 3:4], sc[0:1, 0:1], sc[0:1, 1:2], ALU.mult),
        [tag + "s0", tag + "s1"], [tag + "s3"], dbg=tag + "ssw")
    b.vop("vector", lambda e: e.tensor_scalar(
        sc[0:1, 3:4], sc[0:1, 3:4], 2.0, None, ALU.mult),
        [tag + "s3"], [tag + "s3"], dbg=tag + "g2")
    b.vop("vector", lambda e: e.reciprocal(row[0:1, 2:3], sc[0:1, 1:2]),
          [tag + "s1"], [tag + "r2"], dbg=tag + "isw")
    b.vop("vector", lambda e: e.tensor_scalar(
        row[0:1, 2:3], row[0:1, 2:3], 0.5, None, ALU.mult),
        [tag + "r2"], [tag + "r2"], dbg=tag + "hsw")
    b.vop("vector", lambda e: e.tensor_copy(row[0:1, 0:1], sc[0:1, 2:3]),
          [tag + "s2"], [tag + "r0"], dbg=tag + "c0")
    b.vop("vector", lambda e: e.tensor_copy(row[0:1, 1:2], sc[0:1, 3:4]),
          [tag + "s3"], [tag + "r1"], dbg=tag + "c1")
    bcast_row(b, row[0:1, 0:3], [tag + "r0", tag + "r1", tag + "r2"], 3,
              bc[:, 0:3], "bc", tag)


def qscale_cols(b, qsc, qbs, m_sb, m_key, r_sb, r_key, bc):
    b.vop("vector", lambda e: e.tensor_single_scalar(
        qsc[:, :], r_sb[:, :], bc[:, 0:1], ALU.mult),
        [r_key, "bc"], ["qsc"], dbg="qsc")
    b.vop("vector", lambda e: e.tensor_tensor(
        qbs[:, :], m_sb[:, :], qsc[:, :], ALU.mult),
        [m_key, "qsc"], ["qbs"], dbg="qbs0")
    b.vop("vector", lambda e: e.tensor_scalar(
        qbs[:, :], qbs[:, :], -1.0, None, ALU.mult),
        ["qbs"], ["qbs"], dbg="qbs")


def load_wq(b, es, bc, wq_sb):
    WQT = b.t["WQT"]
    tiles = []
    for dc in range(16):
        t = b.sb(f"wqt_{dc}", [128, 2048], BF16, es=es)
        tiles.append(t)
        b.dma(("sp", "act")[dc % 2], t[:, :], WQT[dc, :, :], [], [f"wqt{dc}"],
              slot=f"batch:wqld{dc % 2}", dbg=f"wql{dc}")
    wmx = b.sb("wmx", [128, 2], F32, es=es)
    for dc in range(16):
        def em(e, dc=dc):
            return e.reduce_max(wmx[:, 1:2], tiles[dc][:, :], axis=AX.X,
                                apply_absolute_value=True)
        b.vop("vector", em, [f"wqt{dc}"], ["wmx1"], dbg=f"wam{dc}")

        def em2(e, dc=dc):
            if dc == 0:
                return e.tensor_copy(wmx[:, 0:1], wmx[:, 1:2])
            return e.tensor_tensor(wmx[:, 0:1], wmx[:, 0:1], wmx[:, 1:2],
                                   ALU.max)
        b.vop("vector", em2, ["wmx1", "wmx0"], ["wmx0"], dbg=f"wamx{dc}")
    return tiles, wmx


def quant_wq(b, tiles, wq_sb):
    for dc in range(16):
        b.act(wq_sb[:, dc * 2048:(dc + 1) * 2048], tiles[dc][:, :],
              AF.Identity, reads=[f"wqt{dc}", "bc"], writes=[f"wq{dc}"],
              scale=b.t["bcb"][:, 2:3], dbg=f"wqq{dc}")


def ffn_chunk(b, c, src_ap, src_key, qsc, qbs, wq_sb, banks, tbanks, q1, q1T,
              kq, kqT, f1, f1key, IF8, bc):
    b.act(q1[:, :], src_ap, AF.Identity, reads=[src_key, "qsc", "qbs"],
          writes=[kq], scale=qsc[:, c:c + 1], bias=qbs[:, c:c + 1],
          dbg=f"qz{c}")
    for dc in range(16):
        tb, tk = tbanks[dc % 2]
        b.mm(tb[:, 0:128], q1[:, dc * 128:(dc + 1) * 128], IF8[:, :],
             True, True, [kq, "I_F8"], [tk], dbg=f"qT{c}_{dc}")
        if dc % 2 == 0:
            def ev(e, dc=dc, tb=tb):
                return e.tensor_copy(q1T[:, dc * 128:(dc + 1) * 128],
                                     tb[:, 0:128])
            b.vop("vector", ev, [tk], [f"{kqT}{dc}"], dbg=f"qTe{c}_{dc}")
        else:
            b.act(q1T[:, dc * 128:(dc + 1) * 128], tb[:, 0:128],
                  AF.Identity, reads=[tk], writes=[f"{kqT}{dc}"],
                  dbg=f"qTe{c}_{dc}")
    wqr = wq_sb.rearrange("p (d x) -> p d x", x=2048)
    for osl in range(4):
        pso, pk = banks[osl]
        for j in range(8):
            b.mm(pso[:, :],
                 q1T[:, j * 256:(j + 1) * 256].rearrange(
                     "p (o t) -> p o t", o=2),
                 wqr[:, 2 * j:2 * j + 2, osl * 512:(osl + 1) * 512],
                 j == 0, j == 7,
                 [f"{kqT}{2 * j}", f"{kqT}{2 * j + 1}",
                  f"wq{2 * j}", f"wq{2 * j + 1}"], [pk],
                 perf_mode=mybir.MatmulPerfMode.DoubleRow,
                 dbg=f"fm{c}_{osl}_{j}")
        if osl % 2 == 0:
            b.act(f1[:, osl * 512:(osl + 1) * 512], pso[:, :], AF.Identity,
                  reads=[pk, "bc"], writes=[f1key], scale=bc[:, 1:2],
                  dbg=f"fe{c}_{osl}")
        else:
            def evf(e, osl=osl, pso=pso):
                return e.tensor_single_scalar(
                    f1[:, osl * 512:(osl + 1) * 512], pso[:, :], bc[:, 1:2],
                    ALU.mult)
            b.vop("vector", evf, [pk, "bc"], [f1key], dbg=f"fe{c}_{osl}")



def build_L0():
    b = B()
    nc = b.nc
    u = b.dram_in("u", [NCHUNK, 128, D], F32)
    M1 = b.dram_out("m1", [128, NCHUNK], F32)
    R1 = b.dram_out("r1", [128, NCHUNK], F32)
    AMAX1 = b.dram_out("amax1", [1, 1], F32)
    with b.es:
        ub = [b.sb(f"ub{i}", [128, D], F32) for i in range(3)]
        scrap = b.sb("scrap", [128, D], F32)
        sumc = b.sb("sumc", [128, NCHUNK], F32)
        sqc = b.sb("sqc", [128, NCHUNK], F32)
        mxc = b.sb("mxc", [128, NCHUNK], F32)
        mnc = b.sb("mnc", [128, NCHUNK], F32)
        m1 = b.sb("m1s", [128, NCHUNK], F32)
        r1 = b.sb("r1s", [128, NCHUNK], F32)
        g = b.sb("g", [128, NCHUNK], F32)
        tmp = b.sb("tmp", [128, NCHUNK], F32)
        acol = b.sb("acol", [128, 1], F32)
        epsc = b.sb("epsc", [128, 1], F32)
        one = b.sb("one", [1, 2], F32)
        row = b.sb("row", [1, 128], F32)
        b.vop("vector", lambda e: e.memset(epsc[:, :], 1e-5), [], ["epsc"])

        for c in range(NCHUNK):
            buf, bk = ub[c % 3], f"ub{c % 3}"
            b.dma(("sp", "pool", "act")[c % 3], buf[:, :], u[c, :, :], [],
                  [bk], slot=bk, dbg=f"u{c}")

            b.act(scrap[:, :], buf[:, :], AF.Identity, reads=[bk],
                  writes=["scrap", f"sum{c}"], accum=sumc[:, c:c + 1],
                  dbg=f"sm{c}")
            b.act(scrap[:, :], buf[:, :], AF.Square, reads=[bk],
                  writes=["scrap", f"sq{c}"], accum=sqc[:, c:c + 1],
                  dbg=f"sq{c}")

            def emx(e, c=c, buf=buf):
                return e.reduce_max(mxc[:, c:c + 1], buf[:, :], axis=AX.X)
            b.vop("vector", emx, [bk], [f"mx{c}"], dbg=f"mx{c}")

            def emn(e, c=c, buf=buf):
                return e.tensor_reduce(mnc[:, c:c + 1], buf[:, :], axis=AX.X,
                                       op=ALU.min)
            b.vop("vector", emn, [bk], [f"mn{c}"], dbg=f"mn{c}")

        sall = [f"sum{c}" for c in range(NCHUNK)]
        qall = [f"sq{c}" for c in range(NCHUNK)]
        mxall = [f"mx{c}" for c in range(NCHUNK)]
        mnall = [f"mn{c}" for c in range(NCHUNK)]
        b.vop("vector", lambda e: e.tensor_scalar(
            m1[:, :], sumc[:, :], 1.0 / D, None, ALU.mult), sall, ["m1"],
            dbg="m1")
        b.vop("vector", lambda e: e.tensor_scalar(
            tmp[:, :], sqc[:, :], 1.0 / D, None, ALU.mult), qall, ["tmp"],
            dbg="ex2")
        b.vop("vector", lambda e: e.tensor_tensor(
            r1[:, :], m1[:, :], m1[:, :], ALU.mult), ["m1"], ["r1"], dbg="m2")
        b.vop("vector", lambda e: e.tensor_tensor(
            r1[:, :], tmp[:, :], r1[:, :], ALU.subtract), ["tmp", "r1"],
            ["r1"], dbg="var")
        b.act(r1[:, :], r1[:, :], AF.Sqrt, reads=["r1", "epsc"], writes=["r1"],
              bias=epsc[:, 0:1], dbg="std")
        b.vop("vector", lambda e: e.reciprocal(r1[:, :], r1[:, :]), ["r1"],
              ["r1"], dbg="rstd")
        b.vop("vector", lambda e: e.tensor_tensor(
            g[:, :], mxc[:, :], m1[:, :], ALU.subtract), mxall + ["m1"], ["g"],
            dbg="g1")
        b.vop("vector", lambda e: e.tensor_tensor(
            tmp[:, :], m1[:, :], mnc[:, :], ALU.subtract), mnall + ["m1"],
            ["tmp"], dbg="g2")
        b.vop("vector", lambda e: e.tensor_tensor(
            g[:, :], g[:, :], tmp[:, :], ALU.max), ["g", "tmp"], ["g"],
            dbg="g3")
        b.vop("vector", lambda e: e.tensor_tensor(
            g[:, :], g[:, :], r1[:, :], ALU.mult), ["g", "r1"], ["g"],
            dbg="g4")

        def ea(e):
            return e.reduce_max(acol[:, :], g[:, :], axis=AX.X)
        b.vop("vector", ea, ["g"], ["acol"], dbg="acol")
        col_max_to_scalar(b, acol[:, 0:1], "acol", one[0:1, 0:1], "a1", "a",
                          row[0:1, :], "rowbuf")

        b.dma("sp", M1, m1[:, :], ["m1"], ["o1"], slot="batch:outs",
              dbg="st_m1")
        b.dma("sp", R1, r1[:, :], ["r1"], ["o2"], slot="batch:outs",
              dbg="st_r1")
        b.dma("sp", AMAX1, one[0:1, 0:1], ["a1"], ["o3"], slot="batch:outs",
              dbg="st_a1")
        b.fence(["o1", "o2", "o3"])
        run_streams(nc, b.es, b.prog)
    return nc



def build_L1():
    b = B()
    nc = b.nc
    u = b.dram_in("u", [NCHUNK, 128, D], F32)
    WA = b.dram_in("WA", [NEB, 128, 16, 128], F32)
    WB = b.dram_in("WB", [4, 16, 128, 512], F32)
    b.dram_in("WQT", [16, 128, 2048], BF16)
    M1 = b.dram_in("m1", [128, NCHUNK], F32)
    R1 = b.dram_in("r1", [128, NCHUNK], F32)
    A1V = b.dram_in("amax1v", [1, 8], F32)
    CWH = b.dram_in("cwh", [128, 192], F32)
    CBH = b.dram_in("cbh", [128, 48], F32)
    ZBH = b.dram_in("zbh", [128, 16], F32)
    RWH = b.dram_in("rwh", [128, 16], F32)
    DREP = b.dram_in("drep", [128, H * 64], F32)
    X3 = b.dram_out("x3", [NCHUNK, 128, D], F32)
    M3 = b.dram_out("m3", [128, NCHUNK], F32)
    R3 = b.dram_out("r3", [128, NCHUNK], F32)
    AMAX3 = b.dram_out("amax3", [1, 1], F32)
    if DEBUG_L1:
        x2d = b.dram_out("x2d", [NCHUNK, 128, D], F32)
        XBCD = b.dram_out("xbcd", [128, 48 * 515], BF16)
        ZSTD = b.dram_out("zstd", [128, CPS * D], BF16)
        ATSD = b.dram_out("atsd", [128, 32], F32)
        EXPAD = b.dram_out("expad", [128, 32], F32)
        YBD = b.dram_out("ybd", [128, D], BF16)
        H2TD = b.dram_out("h2td", [128, 16 * 512], BF16)
        BCD = b.dram_out("bcd", [128, 4], F32)
        QSCD = b.dram_out("qscd", [128, NCHUNK], F32)
        QBSD = b.dram_out("qbsd", [128, NCHUNK], F32)
        Q1D = b.dram_out("q1d", [128, D], mybir.dt.uint8)
        F1D = b.dram_out("f1d", [128, D], BF16)
    else:
        x2d = b.dram_tmp("x2d", [NCHUNK, 128, D], F32)
    WINBF = b.dram_tmp("winbf", [NEB, 128, 2048], BF16)
    WOUTBF = b.dram_tmp("woutbf", [4, 16, 128, 512], BF16)

    with b.es:
        cI = b.load_consts(["I_BF", "I_F8", "I_F32", "CUM_BF", "CUM_F32",
                            "GT_BF", "GT_F32", "NEGLT_BF", "ONES_F32"])
        psA = b.ps("psA", [128, 512])
        psB = b.ps("psB", [128, 512])
        psT = b.ps("psT", [128, 512])
        psE = b.ps("psE", [128, 512])
        psS = b.ps("psS", [128, 512])
        psYa = b.ps("psYa", [128, 512])
        psYb = b.ps("psYb", [128, 512])
        psSt = b.ps("psSt", [128, 512])

        st = {}
        for nm in ("m1s", "r1s", "m2s", "r2s", "m3s", "r3s", "qsc", "qbs",
                   "h2sc", "h2bs", "mx3s", "mn3s"):
            st[nm] = b.sb(nm, [128, NCHUNK], F32)
        epsc = b.sb("epsc", [128, 1], F32)
        b.vop("vector", lambda e: e.memset(epsc[:, :], 1e-5), [], ["epsc"])
        rmsw = b.sb("rmsw", [128, 16], F32)
        zbias = b.sb("zbias", [128, 16], F32)
        cw = b.sb("cw", [128, 192], F32)
        cb = b.sb("cb", [128, 48], F32)
        DREPb = b.sb("DREPb", [128, H * 64], BF16)
        scrap = b.sb("scrap", [128, D], BF16)
        bc = b.sb("bcb", [128, 4], F32)
        a1row = b.sb("a1row", [1, 8], F32)
        warow = b.sb("warow", [1, 1], F32)
        acol3 = b.sb("acol3", [128, 1], F32)
        one3 = b.sb("one3", [1, 2], F32)
        rowb = b.sb("rowb", [1, 128], F32)
        stc = b.sb("stc", [128, 4], F32)

        b.dma("sp", st["m1s"][:, :], M1, [], ["m1"], slot="batch:init", dbg="ld_m1")
        b.dma("sp", st["r1s"][:, :], R1, [], ["r1"], slot="batch:init", dbg="ld_r1")
        b.dma("sp", rmsw[:, :], RWH, [], ["rmsw"], slot="batch:init", dbg="ld_rw")
        b.dma("sp", zbias[:, :], ZBH, [], ["zbias"], slot="batch:init", dbg="ld_zb")
        b.dma("sp", cw[:, :], CWH, [], ["cw"], slot="batch:init", dbg="ld_cw")
        b.dma("sp", cb[:, :], CBH, [], ["cb"], slot="batch:init", dbg="ld_cb")
        b.dma("sp", a1row[0:1, :], A1V, [], ["a1row"], slot="batch:init", dbg="ld_a1")

        with contextlib.ExitStack() as fes:
            wq = b.sb("wq", [128, 16 * 2048], FP8, es=fes)
            drepf = b.sb("drepf", [128, H * 64], F32, es=fes)
            b.dma("pool", drepf[:, :], DREP, [], ["drepr"], slot="drep",
                  dbg="ld_drepr")
            b.vop("vector", lambda e: e.tensor_copy(DREPb[:, :],
                                                    drepf[:, :]),
                  ["drepr"], ["DREPb"], dbg="drepc")
            tiles, wmx = load_wq(b, fes, bc, wq)
            col_max_to_scalar(b, wmx[:, 0:1], "wmx0", warow[0:1, 0:1], "wam",
                              "w1", rowb[0:1, :], "rowbufW")
            ffn_scalars(b, a1row[0:1, :], "a1row", warow[0:1, 0:1], "wam",
                        bc, "f1", es=fes)
            qscale_cols(b, st["qsc"], st["qbs"], st["m1s"], "m1", st["r1s"],
                        "r1", bc)
            quant_wq(b, tiles, wq)
            ub = [b.sb(f"ub{i}", [128, D], F32, es=fes) for i in range(2)]
            q1 = [b.sb(f"q1_{i}", [128, D], FP8, es=fes) for i in range(2)]
            q1T = [b.sb(f"q1T_{i}", [128, D], FP8, es=fes) for i in range(2)]
            f1 = [b.sb(f"f1_{i}", [128, D], BF16, es=fes) for i in range(2)]
            x2b = [b.sb(f"x2b{i}", [128, D], F32, es=fes) for i in range(2)]
            for c in range(NCHUNK):
                i = c % 2
                b.dma("pool", ub[i][:, :], u[c, :, :], [], [f"ub{i}"],
                      slot=f"ub{i}", dbg=f"u{c}")
                ffn_chunk(b, c, ub[i][:, :], f"ub{i}", st["qsc"], st["qbs"],
                          wq,
                          [(psA, "psA"), (psB, "psB"), (psE, "psE"),
                           (psS, "psS")],
                          [(psT, "psT"), (psSt, "psSt")],
                          q1[i], q1T[i], f"q1{i}", f"q1T{i}", f1[i],
                          f"f1{i}", cI["I_F8"], bc)
                if DEBUG_L1 and c == 0:
                    b.dma("sp", BCD, bc[:, :], ["bc"], ["dg7"], dbg="dBC", slot="batch:dbg")
                    b.dma("sp", QSCD, st["qsc"][:, :], ["qsc"], ["dg8"],
                          dbg="dQS", slot="batch:dbg")
                    b.dma("sp", QBSD, st["qbs"][:, :], ["qbs"], ["dg9"],
                          dbg="dQB", slot="batch:dbg")
                    b.dma("sp", Q1D, q1[0][:, :].bitcast(mybir.dt.uint8),
                          ["q10"], ["dgA"], slot="batch:dbg", dbg="dQ1")
                    b.dma("sp", F1D, f1[0][:, :], ["f10"], ["dgB"], dbg="dF1", slot="batch:dbg")

                def eadd(e, i=i):
                    return e.tensor_tensor(x2b[i][:, :], ub[i][:, :],
                                           f1[i][:, :], ALU.add)
                b.vop("vector", eadd, [f"ub{i}", f"f1{i}"], [f"x2b{i}"],
                      dbg=f"x2a{c}")
                b.act(scrap[:, :], x2b[i][:, :], AF.Identity,
                      reads=[f"x2b{i}"], writes=["scrap", "h2", f"s2_{c}"],
                      accum=st["m2s"][:, c:c + 1], dbg=f"s2{c}")
                b.act(scrap[:, :], x2b[i][:, :], AF.Square,
                      reads=[f"x2b{i}"], writes=["scrap", "h2", f"v2_{c}"],
                      accum=st["r2s"][:, c:c + 1], dbg=f"v2{c}")
                b.dma("pool", x2d[c, :, :], x2b[i][:, :], [f"x2b{i}"],
                      [f"x2d{c}"], slot=f"x2b{i}", dbg=f"x2s{c}")
        b.epoch()
        s2a = [f"s2_{c}" for c in range(NCHUNK)]
        v2a = [f"v2_{c}" for c in range(NCHUNK)]
        tmps = b.sb("tmps", [128, NCHUNK], F32)
        b.vop("vector", lambda e: e.tensor_scalar(
            st["m2s"][:, :], st["m2s"][:, :], 1.0 / D, None, ALU.mult),
            s2a, ["m2"], dbg="m2")
        b.vop("vector", lambda e: e.tensor_scalar(
            st["r2s"][:, :], st["r2s"][:, :], 1.0 / D, None, ALU.mult),
            v2a, ["r2x"], dbg="r2x")
        b.vop("vector", lambda e: e.tensor_tensor(
            tmps[:, :], st["m2s"][:, :], st["m2s"][:, :], ALU.mult),
            ["m2"], ["tmps"], dbg="m2q")
        b.vop("vector", lambda e: e.tensor_tensor(
            st["r2s"][:, :], st["r2s"][:, :], tmps[:, :], ALU.subtract),
            ["r2x", "tmps"], ["r2y"], dbg="v2f")
        b.act(st["r2s"][:, :], st["r2s"][:, :], AF.Sqrt,
              reads=["r2y", "epsc"], writes=["r2z"], bias=epsc[:, 0:1],
              dbg="sd2")
        b.vop("vector", lambda e: e.reciprocal(st["r2s"][:, :],
                                               st["r2s"][:, :]),
              ["r2z"], ["r2"], dbg="rs2")
        b.vop("vector", lambda e: e.tensor_tensor(
            st["h2bs"][:, :], st["m2s"][:, :], st["r2s"][:, :], ALU.mult),
            ["m2", "r2"], ["h2b0"], dbg="h2b0")
        b.vop("vector", lambda e: e.tensor_scalar(
            st["h2bs"][:, :], st["h2bs"][:, :], -1.0, None, ALU.mult),
            ["h2b0"], ["h2bs"], dbg="h2b")

        x2s = b.sb("x2s", [128, CPS * D], F32)
        h2 = scrap
        h2T = b.sb("h2T", [128, 16 * 512], BF16)
        wtmp = [b.sb(f"wtmp{i}", [128, 2048], F32) for i in range(2)]
        winb = [b.sb(f"winb{i}", [128, 2048], BF16) for i in range(2)]
        xbc = b.sb("xbc", [128, 48 * 515], BF16)
        tailn = b.sb("tailn", [128, 48 * 4], BF16)
        zfm = [b.sb("zfm0", [128, 512], BF16)] * 2
        zst = b.sb("zst", [128, CPS * D], BF16)
        afm = b.sb("afm", [32, 512], F32)
        asp = b.sb("asp", [32, 512], F32)
        ats = b.sb("ats", [128, 32], F32)
        cssb = b.sb("cssb", [128, 32], F32)
        expA = b.sb("expA", [128, 32], F32)
        decay = b.sb("decay", [128, 32], F32)
        expTb = b.sb("expTb", [128, 32], F32)
        Xt = b.sb("Xt", [128, 2048], BF16)
        Bt = b.sb("Bt", [128, 2048], BF16)
        amb8 = b.sb("amb8", [128, 1024], BF16)
        lmb2 = b.sb("lmb2", [128, 256], BF16)
        mpb2 = b.sb("mpb2", [128, 256], BF16)
        xdb8 = b.sb("xdb8", [128, 512], BF16)
        XDt = b.sb("XDt", [128, D], BF16)
        yob = [b.sb(f"yob{i}", [128, 512], BF16) for i in range(2)]
        Sf = b.sb("Sf", [128, 16 * 64], F32)
        Sb_ = b.sb("Sb", [128, 16 * 64], BF16)
        Ybuf = b.sb("Ybuf", [128, D], BF16)
        ynT = b.sb("ynT", [128, 16 * 512], BF16)
        wob = [wtmp[0][:, 0:512], wtmp[1][:, 0:512]]
        wobb = [b.sb(f"wobb{i}", [128, 512], BF16) for i in range(2)]
        cdg4 = b.sb("cdg4", [128, 512], BF16)

        skeys = [f"Sf{h}" for h in range(H)]
        sbkeys = [f"Sb{h}" for h in range(H)]
        b.vop("vector", lambda e: e.memset(Sf[:, :], 0.0), [], skeys, dbg="S0")
        b.vop("gpsimd", lambda e: e.memset(Sb_[:, :], 0.0), [], sbkeys,
              dbg="Sb0")

        def eguard(e):
            return e.memset(
                xbc.rearrange("p (t c) -> p t c", c=515)[:, :, 0:3], 0.0)
        b.vop("gpsimd", eguard, [], ["xbcg"], dbg="guard")

        for s in range(SWEEP):
            sw = f"s{s}"
            h2Tkeys = [f"h2T{dc}" for dc in range(16)]
            for k in range(CPS):
                c = s * CPS + k
                b.dma("pool", x2s[:, k * D:(k + 1) * D], x2d[c, :, :],
                      [f"x2d{c}"], [f"x2s{k}"], slot=f"x2sl{k}",
                      dbg=f"{sw}x2l{k}")
                b.act(h2[:, :], x2s[:, k * D:(k + 1) * D], AF.Identity,
                      reads=[f"x2s{k}", "r2", "h2bs"], writes=["h2", "scrap"],
                      scale=st["r2s"][:, c:c + 1], bias=st["h2bs"][:, c:c + 1],
                      dbg=f"{sw}h2{k}")
                for dc in range(16):
                    tb, tk = (psT, "psT") if dc % 2 == 0 else (psSt, "psSt")
                    b.mm(tb[:, 0:128], h2[:, dc * 128:(dc + 1) * 128],
                         cI["I_BF"][:, :], True, True, ["h2", "I_BF"], [tk],
                         dbg=f"{sw}hT{k}_{dc}")
                    dst = h2T[:, dc * 512 + k * 128: dc * 512 + k * 128 + 128]
                    if dc % 2 == 0:
                        def ev(e, dst=dst, tb=tb):
                            return e.tensor_copy(dst, tb[:, 0:128])
                        b.vop("vector", ev, [tk], [f"h2T{dc}"],
                              dbg=f"{sw}hTe{k}_{dc}")
                    else:
                        b.act(dst, tb[:, 0:128], AF.Identity, reads=[tk],
                              writes=[f"h2T{dc}"], dbg=f"{sw}hTe{k}_{dc}")
            if s > 0:
                def etail(e):
                    return e.tensor_copy(
                        xbc.rearrange("p (t c) -> p t c", c=515)[:, :, 0:3],
                        tailn.rearrange("p (t c) -> p t c", c=4)[:, :, 1:4])
                b.vop("gpsimd", etail, ["tailn"], ["xbcg"], dbg=f"{sw}tlr")

            for eb in range(NEB):
                i = eb % 2
                wkey = f"win{i}"
                if s == 0:
                    b.dma("sp", wtmp[i][:, :],
                          WA[eb].rearrange("p a j -> p (a j)"), [],
                          [f"wt{i}"], slot=f"wt{i}", dbg=f"{sw}wl{eb}")

                    def ecast(e, i=i):
                        return e.tensor_copy(winb[i][:, :], wtmp[i][:, :])
                    b.vop(("vector", "gpsimd")[i], ecast, [f"wt{i}"], [wkey],
                          dbg=f"{sw}wc{eb}")
                    b.dma("act", WINBF[eb, :, :], winb[i][:, :], [wkey],
                          [f"wbf{eb}"], slot=f"wb{i}", dbg=f"{sw}ws{eb}")
                else:
                    b.dma("sp", winb[i][:, :], WINBF[eb, :, :], [f"wbf{eb}"],
                          [wkey], slot=f"wb{i}", dbg=f"{sw}wl{eb}")
                pk = ("psA", "psB")[i]
                pso = (psA, psB)[i]
                for dc in range(16):
                    b.mm(pso[:, :], winb[i][:, dc * 128:(dc + 1) * 128],
                         h2T[:, dc * 512:(dc + 1) * 512], dc == 0, dc == 15,
                         [wkey] + h2Tkeys, [pk], dbg=f"{sw}pj{eb}_{dc}")
                if eb < 48:
                    dst = xbc[:, eb * 515 + 3: eb * 515 + 515]

                    def evx(e, dst=dst, pso=pso):
                        return e.tensor_copy(dst, pso[:, :])
                    b.vop("vector", evx, [pk], [f"xbcp{eb}"],
                          dbg=f"{sw}xe{eb}")
                    if s < SWEEP - 1 and FLAGS["tail_psum"]:
                        def etl(e, eb=eb, pso=pso):
                            return e.tensor_copy(
                                tailn[:, eb * 4:(eb + 1) * 4],
                                pso[:, 508:512])
                        b.vop("vector", etl, [pk], ["tailn"],
                              dbg=f"{sw}tlp{eb}")
                elif eb < 64:
                    zi = eb - 48
                    zb = zfm[0]
                    b.act(zb[:, :], pso[:, :], AF.Silu, reads=[pk, "zbias"],
                          writes=["zfm"], bias=zbias[:, zi:zi + 1],
                          dbg=f"{sw}zs{zi}")
                    for k in range(CPS):
                        tb, tk = (psT, "psT") if k % 2 == 0 else (psSt, "psSt")
                        b.mm(tb[:, 0:128], zb[:, k * 128:(k + 1) * 128],
                             cI["I_BF"][:, :], True, True,
                             ["zfm", "I_BF"], [tk],
                             dbg=f"{sw}zT{zi}_{k}")
                        dst = zst[:, k * D + zi * 128: k * D + zi * 128 + 128]
                        if k % 2 == 0:
                            def evz(e, dst=dst, tb=tb):
                                return e.tensor_copy(dst, tb[:, 0:128])
                            b.vop("vector", evz, [tk], [f"zst{k}"],
                                  dbg=f"{sw}zTe{zi}_{k}")
                        else:
                            b.act(dst, tb[:, 0:128], AF.Identity, reads=[tk],
                                  writes=[f"zst{k}"], dbg=f"{sw}zTe{zi}_{k}")
                else:
                    def eva(e, pso=pso):
                        return e.tensor_copy(asp[:, :], pso[0:32, :])
                    b.vop("vector", eva, [pk], ["asp"], dbg=f"{sw}ae")
                    b.act(afm[:, :], asp[:, :], AF.Abs, reads=["asp"],
                          writes=["afm0"], dbg=f"{sw}ab")
                    b.act(afm[:, :], afm[:, :], AF.Exp, reads=["afm0"],
                          writes=["afm0"], scale=-1.0, dbg=f"{sw}ax")
                    b.vop("vector", lambda e: e.tensor_scalar(
                        afm[:, :], afm[:, :], 1.0, None, ALU.add),
                        ["afm0"], ["afm0"], dbg=f"{sw}a1")
                    b.act(afm[:, :], afm[:, :], AF.Ln, reads=["afm0"],
                          writes=["afm0"], dbg=f"{sw}al")
                    b.act(asp[:, :], asp[:, :], AF.Relu, reads=["asp"],
                          writes=["asp"], dbg=f"{sw}ar")
                    b.vop("vector", lambda e: e.tensor_tensor(
                        afm[:, :], afm[:, :], asp[:, :], ALU.add),
                        ["afm0", "asp"], ["afm0"], dbg=f"{sw}aa")
                    b.vop("vector", lambda e: e.tensor_scalar(
                        afm[:, :], afm[:, :], -1.0, None, ALU.mult),
                        ["afm0"], ["afm"], dbg=f"{sw}an")

            for t in range(48):
                base = t * 515
                pc, pck = ((psS, "psS"), (psB, "psB"))[t % 2]
                for w in range(4):
                    def edg(e, t=t, w=w):
                        return e.tensor_scalar(
                            cdg4[:, w * 128:(w + 1) * 128], cI["I_BF"][:, :],
                            cw[:, t * 4 + w: t * 4 + w + 1], None, ALU.mult)
                    b.vop("vector", edg, ["cw", "I_BF"], [f"cd{w}"],
                          dbg=f"{sw}cd{t}_{w}")
                for w in range(4):
                    b.mm(pc[:, :], cdg4[:, w * 128:(w + 1) * 128],
                         xbc[:, base + w: base + w + 512], w == 0, w == 3,
                         [f"cd{w}", f"xbcp{t}", "xbcg"], [pck],
                         dbg=f"{sw}cv{t}_{w}")
                b.act(xbc[:, base + 3: base + 515], pc[:, :], AF.Silu,
                      reads=[pck, "cb"], writes=[f"xbcp{t}"],
                      bias=cb[:, t:t + 1], dbg=f"{sw}cs{t}")

            if DEBUG_L1 and s == 0:
                b.dma("sp", XBCD, xbc[:, :],
                      [f"xbcp{t}" for t in range(48, slot="batch:dbg")], ["dbg1"], dbg="dXBC")
                b.dma("sp", ZSTD, zst[:, :],
                      [f"zst{k}" for k in range(CPS, slot="batch:dbg")], ["dbg2"], dbg="dZST")
                b.dma("sp", H2TD, h2T[:, :], h2Tkeys, ["dbg3"], dbg="dH2T", slot="batch:dbg")
            for k in range(CPS):
                c = s * CPS + k
                ck = k * 128
                for j in range(16):
                    b.mm(psT[:, 0:128],
                         xbc[:, j * 515 + 3 + ck: j * 515 + 3 + ck + 128],
                         cI["I_BF"][:, :], True, True, [f"xbcp{j}", "I_BF"],
                         ["psT"], dbg=f"{sw}Xt{k}_{j}")

                    def evxt(e, j=j):
                        return e.tensor_copy(Xt[:, j * 128:(j + 1) * 128],
                                             psT[:, 0:128])
                    b.vop("vector", evxt, ["psT"], ["Xt"],
                          dbg=f"{sw}Xte{k}_{j}")
                    b.mm(psSt[:, 0:128],
                         xbc[:, (16 + j) * 515 + 3 + ck:
                             (16 + j) * 515 + 3 + ck + 128],
                         cI["I_BF"][:, :], True, True,
                         [f"xbcp{16 + j}", "I_BF"], ["psSt"],
                         dbg=f"{sw}Bt{k}_{j}")

                    def evbt(e, j=j):
                        return e.activation(Bt[:, j * 128:(j + 1) * 128],
                                            psSt[:, 0:128], AF.Identity)
                    b.prog.op("scalar", evbt, ["psSt"], ["Bt"],
                              dbg=f"{sw}Bte{k}_{j}")
                b.mm(psE[:, 0:32], afm[0:32, ck:ck + 128],
                     cI["I_F32"][0:32, 0:32], True, True, ["afm", "I_F32"],
                     ["psE"], dbg=f"{sw}at{k}")

                def evat(e):
                    return e.tensor_copy(ats[:, :], psE[:, 0:32])
                b.vop("vector", evat, ["psE"], ["ats"], dbg=f"{sw}ate{k}")
                b.mm(psE[:, 0:32], cI["CUM_F32"][:, :], ats[:, :], True, True,
                     ["ats", "CUM_F32"], ["psE"], dbg=f"{sw}cs{k}")

                def evcs(e):
                    return e.tensor_copy(cssb[:, :], psE[:, 0:32])
                b.vop("vector", evcs, ["psE"], ["cssb"], dbg=f"{sw}cse{k}")
                b.act(expA[:, :], cssb[:, :], AF.Exp, reads=["cssb"],
                      writes=["expA"], dbg=f"{sw}eA{k}")
                b.mm(psE[:, 0:32], cI["GT_F32"][:, :], ats[:, :], True, True,
                     ["ats", "GT_F32"], ["psE"], dbg=f"{sw}dc{k}")
                b.act(decay[:, :], psE[:, 0:32], AF.Exp, reads=["psE"],
                      writes=["decay"], dbg=f"{sw}dcE{k}")
                def etb(e):
                    return e.tensor_tensor(expTb[:, :], expA[:, :],
                                           decay[:, :], ALU.mult)
                b.vop("vector", etb, ["expA", "decay"], ["expTb"],
                      dbg=f"{sw}tc{k}")

                def exdall(e):
                    return e.tensor_tensor(XDt[:, :], Xt[:, :], DREPb[:, :],
                                           ALU.mult)
                b.vop("vector", exdall, ["Xt", "DREPb"], ["XDt"],
                      dbg=f"{sw}xda{k}")

                for half in range(4):
                    hkey = f"yo{half % 2}"
                    for hh in range(8):
                        h = half * 8 + hh

                        def eam(e, hh=hh, h=h):
                            return e.tensor_scalar(
                                amb8[:, hh * 128:(hh + 1) * 128],
                                cI["CUM_BF"][:, :],
                                ats[:, h:h + 1], None, ALU.mult)
                        b.vop("vector", eam, ["ats", "CUM_BF"], [f"am{hh}"],
                              dbg=f"{sw}am{k}_{h}")

                        def exd(e, hh=hh, h=h):
                            return e.tensor_scalar(
                                xdb8[:, hh * 64:(hh + 1) * 64],
                                Xt[:, h * 64:(h + 1) * 64],
                                decay[:, h:h + 1], None, ALU.mult)
                        b.vop("vector", exd, ["Xt", "decay"], [f"xd{hh}"],
                              dbg=f"{sw}xd{k}_{h}")
                    for hh in range(9):
                        if hh < 8:
                            h = half * 8 + hh
                            r = h % 2
                            tile = h // 2
                            prow = (h % 2) * 64
                            bE, bEk = ((psE, "psE"), (psA, "psA"))[r]
                            bS, bSk = ((psS, "psS"), (psB, "psB"))[r]
                            fmBh = xbc[prow:prow + 64,
                                       (16 + tile) * 515 + 3 + ck:
                                       (16 + tile) * 515 + 3 + ck + 128]
                            fmCh = xbc[prow:prow + 64,
                                       (32 + tile) * 515 + 3 + ck:
                                       (32 + tile) * 515 + 3 + ck + 128]
                            b.mm(bE[:, 0:128], cI["GT_BF"][:, :],
                                 amb8[:, hh * 128:(hh + 1) * 128],
                                 True, False, [f"am{hh}", "GT_BF"], [bEk],
                                 dbg=f"{sw}E1{k}_{h}")
                            b.mm(bE[:, 0:128], cI["NEGLT_BF"][:, :],
                                 cI["I_BF"][:, :], False, True,
                                 ["NEGLT_BF", "I_BF"], [bEk],
                                 dbg=f"{sw}E2{k}_{h}")
                            b.act(lmb2[:, r * 128:(r + 1) * 128],
                                  bE[:, 0:128], AF.Exp, reads=[bEk],
                                  writes=[f"lm{r}"], dbg=f"{sw}L{k}_{h}")
                            b.mm(bS[:, 0:128], fmBh, fmCh, True, True,
                                 [f"xbcp{16 + tile}", f"xbcp{32 + tile}"],
                                 [bSk], dbg=f"{sw}sc{k}_{h}")

                            def emp(e, r=r, bS=bS):
                                return e.tensor_tensor(
                                    mpb2[:, r * 128:(r + 1) * 128],
                                    lmb2[:, r * 128:(r + 1) * 128],
                                    bS[:, 0:128], ALU.mult)
                            b.vop("vector", emp, [f"lm{r}", bSk], [f"mp{r}"],
                                  dbg=f"{sw}M{k}_{h}")
                        if hh >= 1:
                            h = half * 8 + hh - 1
                            hp = hh - 1
                            r = h % 2
                            tile = h // 2
                            prow = (h % 2) * 64
                            fmCh = xbc[prow:prow + 64,
                                       (32 + tile) * 515 + 3 + ck:
                                       (32 + tile) * 515 + 3 + ck + 128]
                            bT, bTk = ((psSt, "psSt"), (psT, "psT"))[r]
                            b.mm(psYa[:, hp * 64:(hp + 1) * 64],
                                 mpb2[:, r * 128:(r + 1) * 128],
                                 Xt[:, h * 64:(h + 1) * 64], True, True,
                                 [f"mp{r}", "Xt"], ["psYa"],
                                 dbg=f"{sw}Yd{k}_{h}")
                            if prow == 0:
                                stl = Bt[:, h * 64:(h + 1) * 64]
                                sto = bT[0:64, 0:64]
                            else:
                                stl = Bt[:, (h - 1) * 64:(h + 1) * 64]
                                sto = bT[:, 0:64]
                            b.mm(sto, stl, xdb8[:, hp * 64:(hp + 1) * 64],
                                 True, True, ["Bt", f"xd{hp}"], [bTk],
                                 dbg=f"{sw}st{k}_{h}")
                            b.mm(psYb[:, hp * 64:(hp + 1) * 64], fmCh,
                                 Sb_[prow:prow + 64,
                                     tile * 64:(tile + 1) * 64],
                                 True, True,
                                 [f"xbcp{32 + tile}", f"Sb{h}"], ["psYb"],
                                 dbg=f"{sw}Yo{k}_{h}")

                            def est(e, h=h, tile=tile, prow=prow, bT=bT):
                                return e.scalar_tensor_tensor(
                                    Sf[prow:prow + 64,
                                       tile * 64:(tile + 1) * 64],
                                    Sf[prow:prow + 64,
                                       tile * 64:(tile + 1) * 64],
                                    expTb[prow:prow + 64, h:h + 1],
                                    bT[prow:prow + 64, 0:64],
                                    ALU.mult, ALU.add)
                            b.vop("vector", est,
                                  [bTk, "expTb", f"Sf{h}"], [f"Sf{h}"],
                                  dbg=f"{sw}Su{k}_{h}")
                    for hh in range(8):
                        h = half * 8 + hh
                        b.act(yob[half % 2][:, hh * 64:(hh + 1) * 64],
                              psYb[:, hh * 64:(hh + 1) * 64], AF.Identity,
                              reads=["psYb", "expA"], writes=[hkey],
                              scale=expA[:, h:h + 1], dbg=f"{sw}yoe{k}_{h}")

                    def eass(e, half=half):
                        return e.tensor_tensor(
                            Ybuf[:, half * 512:(half + 1) * 512], psYa[:, :],
                            yob[half % 2][:, :], ALU.add)
                    b.vop("vector", eass, ["psYa", hkey], ["Ybuf"],
                          dbg=f"{sw}Ya{k}_{half}")

                    def eassB(e, half=half):
                        return e.tensor_tensor(
                            Ybuf[:, half * 512:(half + 1) * 512],
                            Ybuf[:, half * 512:(half + 1) * 512],
                            XDt[:, half * 512:(half + 1) * 512], ALU.add)
                    b.vop("vector", eassB, ["Ybuf", "XDt"], ["Ybuf"],
                          dbg=f"{sw}Yb{k}_{half}")
                skl = [f"Sf{h}" for h in range(H)]
                sbl = [f"Sb{h}" for h in range(H)]

                def esb(e):
                    return e.copy(Sb_[:, :], Sf[:, :])
                b.vop("scalar", esb, skl, sbl, dbg=f"{sw}Sc{k}")

                if DEBUG_L1 and s == 0 and k == 0:
                    b.dma("sp", ATSD, ats[:, :], ["ats"], ["dbg4"], dbg="dATS", slot="batch:dbg")
                    b.dma("sp", EXPAD, expA[:, :], ["expA"], ["dbg5"],
                          dbg="dEXP", slot="batch:dbg")
                    b.dma("sp", YBD, Ybuf[:, :], ["Ybuf"], ["dbg6"], dbg="dYB", slot="batch:dbg")

                def egate(e, k=k):
                    return e.tensor_tensor(Ybuf[:, :], Ybuf[:, :],
                                           zst[:, k * D:(k + 1) * D],
                                           ALU.mult)
                b.vop("vector", egate, ["Ybuf", f"zst{k}"], ["Ybuf"],
                      dbg=f"{sw}gt{k}")
                b.act(scrap[:, :], Ybuf[:, :], AF.Square, reads=["Ybuf"],
                      writes=["scrap", "h2", "ms0"], accum=stc[:, 0:1],
                      dbg=f"{sw}ms{k}")
                b.vop("vector", lambda e: e.tensor_scalar(
                    stc[:, 1:2], stc[:, 0:1], 1.0 / D, epsc[:, 0:1],
                    ALU.mult, ALU.add),
                    ["ms0", "epsc"], ["ms1"], dbg=f"{sw}msd{k}")
                if FLAGS["lnexp"]:
                    b.act(stc[:, 1:2], stc[:, 1:2], AF.Ln,
                          reads=["ms1"], writes=["ms2"], dbg=f"{sw}msq{k}")
                    b.act(stc[:, 2:3], stc[:, 1:2], AF.Exp, scale=-0.5,
                          reads=["ms2"], writes=["ms3"], dbg=f"{sw}msr{k}")
                else:
                    b.act(stc[:, 1:2], stc[:, 1:2], AF.Sqrt,
                          reads=["ms1"], writes=["ms2"], dbg=f"{sw}msq{k}")
                    b.vop("vector", lambda e: e.reciprocal(stc[:, 2:3],
                                                           stc[:, 1:2]),
                          ["ms2"], ["ms3"], dbg=f"{sw}msr{k}")

                def eyn(e):
                    return e.tensor_scalar(Ybuf[:, :], Ybuf[:, :], stc[:, 2:3],
                                           None, ALU.mult)
                b.vop("vector", eyn, ["Ybuf", "ms3"], ["Ybuf"],
                      dbg=f"{sw}yn{k}")
                for ec in range(16):
                    tb, tk = (psT, "psT") if ec % 2 == 0 else (psSt, "psSt")
                    b.mm(tb[:, 0:128], Ybuf[:, ec * 128:(ec + 1) * 128],
                         cI["I_BF"][:, :], True, True, ["Ybuf", "I_BF"],
                         [tk], dbg=f"{sw}yT{k}_{ec}")
                    dst = ynT[:, ec * 512 + k * 128: ec * 512 + k * 128 + 128]
                    if ec % 2 == 0:
                        def evy(e, dst=dst, tb=tb):
                            return e.tensor_copy(dst, tb[:, 0:128])
                        b.vop("vector", evy, [tk], [f"ynT{ec}"],
                              dbg=f"{sw}yTe{k}_{ec}")
                    else:
                        b.act(dst, tb[:, 0:128], AF.Identity, reads=[tk],
                              writes=[f"ynT{ec}"], dbg=f"{sw}yTe{k}_{ec}")

            ynTall = [f"ynT{ec}" for ec in range(16)]
            for sd in range(4):
                for ec in range(16):
                    i = (sd * 16 + ec) % 2
                    wkey = f"wo{i}"
                    if s == 0:
                        b.dma("sp", wob[i], WB[sd, ec, :, :], [],
                              [f"wt{i}"], slot=f"wt{i}", dbg=f"{sw}ol{sd}_{ec}")
                        b.act(wobb[i][:, :], wob[i], AF.Identity,
                              reads=[f"wt{i}", "rmsw"], writes=[wkey],
                              scale=rmsw[:, ec:ec + 1], dbg=f"{sw}oc{sd}_{ec}")
                        b.dma("act", WOUTBF[sd, ec, :, :], wobb[i][:, :],
                              [wkey], [f"obf{sd}_{ec}"], slot=f"wob{i}",
                              dbg=f"{sw}os{sd}_{ec}")
                    else:
                        b.dma("sp", wobb[i][:, :], WOUTBF[sd, ec, :, :],
                              [f"obf{sd}_{ec}"], [wkey], slot=f"wob{i}",
                              dbg=f"{sw}ol{sd}_{ec}")
                    for k in range(CPS):
                        pk = ("psA", "psB", "psE", "psS")[k]
                        pso = (psA, psB, psE, psS)[k]
                        b.mm(pso[:, :],
                             ynT[:, ec * 512 + k * 128: ec * 512 + k * 128 + 128],
                             wobb[i][:, :], ec == 0, ec == 15,
                             [wkey] + ynTall, [pk], dbg=f"{sw}om{sd}_{ec}_{k}")
                        if ec == 15:
                            def eres(e, k=k, sd=sd, pso=pso):
                                return e.tensor_tensor(
                                    x2s[:, k * D + sd * 512:
                                        k * D + sd * 512 + 512],
                                    x2s[:, k * D + sd * 512:
                                        k * D + sd * 512 + 512],
                                    pso[:, :], ALU.add)
                            b.vop("vector", eres, [pk, f"x2s{k}"],
                                  [f"x2s{k}"], dbg=f"{sw}rs{sd}_{k}")
            for k in range(CPS):
                c = s * CPS + k
                xk = f"x2s{k}"
                b.act(scrap[:, :], x2s[:, k * D:(k + 1) * D], AF.Identity,
                      reads=[xk], writes=["scrap", "h2", f"s3_{c}"],
                      accum=st["m3s"][:, c:c + 1], dbg=f"{sw}s3{k}")
                b.act(scrap[:, :], x2s[:, k * D:(k + 1) * D], AF.Square,
                      reads=[xk], writes=["scrap", "h2", f"v3_{c}"],
                      accum=st["r3s"][:, c:c + 1], dbg=f"{sw}v3{k}")

                def emx3(e, c=c, k=k):
                    return e.reduce_max(st["mx3s"][:, c:c + 1],
                                        x2s[:, k * D:(k + 1) * D], axis=AX.X)
                b.vop("vector", emx3, [xk], [f"mh{c}"], dbg=f"{sw}mh{k}")

                def emn3(e, c=c, k=k):
                    return e.tensor_reduce(st["mn3s"][:, c:c + 1],
                                           x2s[:, k * D:(k + 1) * D],
                                           axis=AX.X, op=ALU.min)
                b.vop("vector", emn3, [xk], [f"ml{c}"], dbg=f"{sw}ml{k}")
                b.dma("sp", X3[c, :, :], x2s[:, k * D:(k + 1) * D], [xk],
                      [f"x3o{c}"], slot=f"x2sl{k}", dbg=f"{sw}x3s{k}")

        s3a = [f"s3_{c}" for c in range(NCHUNK)]
        v3a = [f"v3_{c}" for c in range(NCHUNK)]
        mha = [f"mh{c}" for c in range(NCHUNK)]
        mla = [f"ml{c}" for c in range(NCHUNK)]
        tmp3 = b.sb("tmp3", [128, NCHUNK], F32)
        b.vop("vector", lambda e: e.tensor_scalar(
            st["m3s"][:, :], st["m3s"][:, :], 1.0 / D, None, ALU.mult),
            s3a, ["m3"], dbg="m3")
        b.vop("vector", lambda e: e.tensor_scalar(
            st["r3s"][:, :], st["r3s"][:, :], 1.0 / D, None, ALU.mult),
            v3a, ["r3x"], dbg="r3x")
        b.vop("vector", lambda e: e.tensor_tensor(
            tmp3[:, :], st["m3s"][:, :], st["m3s"][:, :], ALU.mult),
            ["m3"], ["tmp3"], dbg="m3q")
        b.vop("vector", lambda e: e.tensor_tensor(
            st["r3s"][:, :], st["r3s"][:, :], tmp3[:, :], ALU.subtract),
            ["r3x", "tmp3"], ["r3y"], dbg="v3f")
        b.act(st["r3s"][:, :], st["r3s"][:, :], AF.Sqrt,
              reads=["r3y", "epsc"], writes=["r3z"], bias=epsc[:, 0:1],
              dbg="sd3")
        b.vop("vector", lambda e: e.reciprocal(st["r3s"][:, :],
                                               st["r3s"][:, :]),
              ["r3z"], ["r3"], dbg="rs3")
        b.vop("vector", lambda e: e.tensor_tensor(
            st["mx3s"][:, :], st["mx3s"][:, :], st["m3s"][:, :], ALU.subtract),
            mha + ["m3"], ["g3a"], dbg="g3a")
        b.vop("vector", lambda e: e.tensor_tensor(
            st["mn3s"][:, :], st["m3s"][:, :], st["mn3s"][:, :], ALU.subtract),
            mla + ["m3"], ["g3b"], dbg="g3b")
        b.vop("vector", lambda e: e.tensor_tensor(
            st["mx3s"][:, :], st["mx3s"][:, :], st["mn3s"][:, :], ALU.max),
            ["g3a", "g3b"], ["g3c"], dbg="g3c")
        b.vop("vector", lambda e: e.tensor_tensor(
            st["mx3s"][:, :], st["mx3s"][:, :], st["r3s"][:, :], ALU.mult),
            ["g3c", "r3"], ["g3"], dbg="g3")

        def ea3(e):
            return e.reduce_max(acol3[:, 0:1], st["mx3s"][:, :], axis=AX.X)
        b.vop("vector", ea3, ["g3"], ["ac3"], dbg="ac3")
        col_max_to_scalar(b, acol3[:, 0:1], "ac3", one3[0:1, 0:1], "a3", "a3",
                          rowb[0:1, :], "rw3")
        b.dma("sp", M3, st["m3s"][:, :], ["m3"], ["om3"], slot="batch:outs",
              dbg="st_m3")
        b.dma("sp", R3, st["r3s"][:, :], ["r3"], ["or3"], slot="batch:outs",
              dbg="st_r3")
        b.dma("sp", AMAX3, one3[0:1, 0:1], ["a3"], ["oa3"],
              slot="batch:outs", dbg="st_a3")
        b.fence(["om3", "or3", "oa3"] + [f"x3o{c}" for c in range(NCHUNK)])
        run_streams(nc, b.es, b.prog)
    return nc



def build_L2():
    b = B()
    nc = b.nc
    X3 = b.dram_in("x3", [NCHUNK, 128, D], F32)
    b.dram_in("WQT", [16, 128, 2048], BF16)
    M3 = b.dram_in("m3", [128, NCHUNK], F32)
    R3 = b.dram_in("r3", [128, NCHUNK], F32)
    A3V = b.dram_in("amax3v", [1, 8], F32)
    OUT = b.dram_out("out", [NCHUNK, 128, D], F32)
    with b.es:
        cI = b.load_consts(["I_F8"])
        psA = b.ps("psA", [128, 512])
        psB = b.ps("psB", [128, 512])
        psE = b.ps("psE", [128, 512])
        psS = b.ps("psS", [128, 512])
        psT = b.ps("psT", [128, 512])
        psSt = b.ps("psSt", [128, 512])
        m3 = b.sb("m3s", [128, NCHUNK], F32)
        r3 = b.sb("r3s", [128, NCHUNK], F32)
        qsc = b.sb("qsc", [128, NCHUNK], F32)
        qbs = b.sb("qbs", [128, NCHUNK], F32)
        bc = b.sb("bcb", [128, 4], F32)
        a3row = b.sb("a3row", [1, 8], F32)
        warow = b.sb("warow", [1, 1], F32)
        rowb = b.sb("rowb", [1, 128], F32)
        b.dma("sp", m3[:, :], M3, [], ["m3"], dbg="ld_m3")
        b.dma("sp", r3[:, :], R3, [], ["r3"], dbg="ld_r3")
        b.dma("sp", a3row[0:1, :], A3V, [], ["a3row"], dbg="ld_a3")
        wq = b.sb("wq", [128, 16 * 2048], FP8)
        tiles, wmx = load_wq(b, b.es, bc, wq)
        col_max_to_scalar(b, wmx[:, 0:1], "wmx0", warow[0:1, 0:1], "wam",
                          "w2", rowb[0:1, :], "rowbufW")
        ffn_scalars(b, a3row[0:1, :], "a3row", warow[0:1, 0:1], "wam", bc,
                    "f2")
        qscale_cols(b, qsc, qbs, m3, "m3", r3, "r3", bc)
        quant_wq(b, tiles, wq)
        xb = [b.sb(f"xb{i}", [128, D], F32) for i in range(2)]
        q1 = [b.sb(f"q1_{i}", [128, D], FP8) for i in range(2)]
        q1T = [b.sb(f"q1T_{i}", [128, D], FP8) for i in range(2)]
        f1 = [b.sb(f"f1_{i}", [128, D], BF16) for i in range(2)]
        ob = [b.sb(f"ob{i}", [128, D], F32) for i in range(2)]
        outk = []
        for c in range(NCHUNK):
            i = c % 2
            b.dma("pool", xb[i][:, :], X3[c, :, :], [], [f"xb{i}"],
                  slot=f"xb{i}", dbg=f"x3l{c}")
            ffn_chunk(b, c, xb[i][:, :], f"xb{i}", qsc, qbs, wq,
                      [(psA, "psA"), (psB, "psB"), (psE, "psE"),
                       (psS, "psS")],
                      [(psT, "psT"), (psSt, "psSt")],
                      q1[i], q1T[i], f"q1{i}", f"q1T{i}", f1[i], f"f1{i}",
                      cI["I_F8"], bc)

            def eadd(e, i=i):
                return e.tensor_tensor(ob[i][:, :], xb[i][:, :], f1[i][:, :],
                                       ALU.add)
            b.vop("vector", eadd, [f"xb{i}", f"f1{i}"], [f"ob{i}"],
                  dbg=f"oa{c}")
            b.dma("sp", OUT[c, :, :], ob[i][:, :], [f"ob{i}"], [f"oo{c}"],
                  slot=f"ob{i}", dbg=f"os{c}")
            outk.append(f"oo{c}")
        b.fence(outk)
        run_streams(nc, b.es, b.prog)
    return nc



def _host_prep(W_in, W_out, ffn_w, conv_w, conv_b, z_bias, rms_w, Dv):
    Wp = np.zeros((NEB * 128, D), np.float32)
    Wp[:PROJ] = np.asarray(W_in, np.float32)
    WA = np.ascontiguousarray(Wp.reshape(NEB, 128, 16, 128)
                              .transpose(0, 3, 2, 1))
    WB = np.ascontiguousarray(np.asarray(W_out, np.float32)
                              .reshape(4, 512, 16, 128).transpose(0, 2, 3, 1))
    WQT = np.ascontiguousarray(np.asarray(ffn_w).T.reshape(16, 128, 2048))
    CWH = np.ascontiguousarray(np.asarray(conv_w, np.float32)
                               .reshape(48, 128, 4).transpose(1, 0, 2)
                               .reshape(128, 192))
    CBH = np.ascontiguousarray(np.asarray(conv_b, np.float32)
                               .reshape(48, 128).T)
    ZBH = np.ascontiguousarray(np.asarray(z_bias, np.float32)
                               .reshape(16, 128).T)
    RWH = np.ascontiguousarray(np.asarray(rms_w, np.float32)
                               .reshape(16, 128).T)
    DREP = np.ascontiguousarray(np.tile(
        np.repeat(np.asarray(Dv, np.float32), 64)[None, :], (128, 1)))
    return WA, WB, WQT, CWH, CBH, ZBH, RWH, DREP


def kernel(L0_attn_u, L0_attn_W_in, L0_attn_z_bias, L0_attn_conv_w,
           L0_attn_conv_b, L0_attn_D, L0_attn_rms_w, L0_attn_W_out,
           L0_ffn_weight):
    u = np.ascontiguousarray(np.asarray(L0_attn_u, np.float32))
    WA, WB, WQT, CWH, CBH, ZBH, RWH, DD = _host_prep(
        L0_attn_W_in, L0_attn_W_out, L0_ffn_weight, L0_attn_conv_w,
        L0_attn_conv_b, L0_attn_z_bias, L0_attn_rms_w, L0_attn_D)
    ushards = [np.ascontiguousarray(u[i].reshape(NCHUNK, 128, D))
               for i in range(NCORES)]
    cores = list(range(NCORES))

    nc0 = build_L0()
    r0 = run_bass_kernel_spmd(
        nc0, [dict(u=ushards[i]) for i in range(NCORES)],
        core_ids=cores)
    amax1v = np.ascontiguousarray(
        np.stack([np.asarray(r0.results[i]["amax1"]).reshape(())
                  for i in range(NCORES)]).reshape(1, 8).astype(np.float32))

    nc1 = build_L1()
    in1 = [dict(u=ushards[i], WA=WA, WB=WB, WQT=WQT,
                m1=np.asarray(r0.results[i]["m1"]),
                r1=np.asarray(r0.results[i]["r1"]),
                amax1v=amax1v, cwh=CWH, cbh=CBH, zbh=ZBH,
                rwh=RWH, drep=DD) for i in range(NCORES)]
    r1 = run_bass_kernel_spmd(nc1, in1, core_ids=cores)
    amax3v = np.ascontiguousarray(
        np.stack([np.asarray(r1.results[i]["amax3"]).reshape(())
                  for i in range(NCORES)]).reshape(1, 8).astype(np.float32))

    nc2 = build_L2()
    in2 = [dict(x3=np.asarray(r1.results[i]["x3"]), WQT=WQT,
                m3=np.asarray(r1.results[i]["m3"]),
                r3=np.asarray(r1.results[i]["r3"]),
                amax3v=amax3v) for i in range(NCORES)]
    r2 = run_bass_kernel_spmd(nc2, in2, core_ids=cores)
    out = np.stack([np.asarray(r2.results[i]["out"]).reshape(T, D)
                    for i in range(NCORES)])
    return np.ascontiguousarray(out.astype(np.float32))


# version 13 (final)
# speedup vs baseline: 1.2791x; 1.0147x over previous
import contextlib

import numpy as np
import ml_dtypes

import concourse.bass as bass
import concourse.mybir as mybir
from concourse.bass_utils import run_bass_kernel_spmd

F32 = mybir.dt.float32
BF16 = mybir.dt.bfloat16
FP8 = mybir.dt.float8e4
AF = mybir.ActivationFunctionType
ALU = mybir.AluOpType
AX = mybir.AxisListType

NCORES = 8
DEBUG_L1 = False
FLAGS = {"conv_dve": True, "tail_psum": True, "sbf_act": True,
         "tbank": True, "lnexp": True}
T = 2048
D = 2048
NCHUNK = 16
H = 32
PROJ = 8224
NEB = 65
SWEEP = 4
CPS = 4



class _Op:
    __slots__ = ("eng", "emit", "reads", "writes", "dma", "slot", "idx",
                 "deps", "need_inc", "rank", "dbg")

    def __init__(self, eng, emit, reads, writes, dma, slot, idx, dbg):
        self.eng = eng
        self.emit = emit
        self.reads = reads
        self.writes = writes
        self.dma = dma
        self.slot = slot
        self.idx = idx
        self.deps = []
        self.need_inc = False
        self.rank = None
        self.dbg = dbg


QUEUE_ENG = {"sp": "sync", "act": "scalar", "pool": "gpsimd"}
ENGS = ("tensor", "vector", "scalar", "gpsimd", "sync")


class Prog:

    def __init__(self):
        self.ops = []
        self.last_write = {}
        self.readers = {}
        self.epoch_idx = None

    def op(self, eng, emit, reads=(), writes=(), dma=None, slot=None, dbg=""):
        if dma is not None:
            assert slot is not None, dbg
            qt = "sw" if dma == "pool" else "hw"
            if slot.startswith("batch:"):
                slot = f"batch:{qt}:{slot[6:]}"
            else:
                slot = f"{qt}:{slot}"
        o = _Op(eng, emit, tuple(reads), tuple(writes), dma, slot,
                len(self.ops), dbg)
        deps = set()
        if self.epoch_idx is not None and self.epoch_idx != len(self.ops):
            deps.add(self.epoch_idx)
        for k in o.reads:
            w = self.last_write.get(k)
            if w is not None:
                deps.add(w)
        for k in o.writes:
            w = self.last_write.get(k)
            if w is not None:
                deps.add(w)
            for r in self.readers.get(k, ()):
                deps.add(r)
        o.deps = sorted(deps)
        for k in o.reads:
            self.readers.setdefault(k, []).append(o.idx)
        for k in o.writes:
            self.last_write[k] = o.idx
            self.readers[k] = []
        self.ops.append(o)
        return o

    def finalize(self, nc, es):
        ops = self.ops
        for o in ops:
            for d in o.deps:
                dep = ops[d]
                if (dep.dma is None and o.dma is None
                        and dep.eng == "tensor" and o.eng == "tensor"):
                    continue
                dep.need_inc = True
        eng_count = {e: 0 for e in ENGS}
        slot_count = {}
        slot_total = {}
        for o in ops:
            if o.dma is not None:
                slot_count[o.slot] = slot_count.get(o.slot, 0) + 16
                o.rank = slot_count[o.slot]
                o.need_inc = True
            elif o.need_inc:
                eng_count[o.eng] += 1
                o.rank = eng_count[o.eng]
        slot_total = dict(slot_count)
        sems = {e: es.enter_context(nc.semaphore(f"sem_{e}"))
                for e in ("tensor", "vector", "scalar", "gpsimd")}
        dsems = {s: es.enter_context(nc.semaphore(f"ds_{si}"))
                 for si, s in enumerate(slot_total)}
        streams = {e: [] for e in ENGS}
        watermark = {e: {} for e in ENGS}
        for o in ops:
            my_eng = QUEUE_ENG[o.dma] if o.dma is not None else o.eng
            waits = {}
            for d in o.deps:
                dep = ops[d]
                if dep.dma is not None:
                    key = ("d", dep.slot)
                    if dep.slot.startswith("batch:"):
                        val = slot_total[dep.slot]
                    else:
                        val = dep.rank
                else:
                    if (dep.eng == "tensor" and o.dma is None
                            and o.eng == "tensor"):
                        continue
                    key, val = ("e", dep.eng), dep.rank
                    assert val is not None, (o.dbg, dep.dbg)
                if waits.get(key, 0) < val:
                    waits[key] = val
            wm = watermark[my_eng]
            items = []
            for key, val in sorted(waits.items()):
                if wm.get(key, 0) >= val:
                    continue
                wm[key] = val
                items.append((key, val))
            streams[my_eng].append((o, items))
        return streams, sems, dsems


def run_streams(nc, es, prog):
    streams, sems, dsems = prog.finalize(nc, es)

    def emit(engobj, name):
        for o, waits in streams[name]:
            for (kind, qn), val in waits:
                engobj.wait_ge(dsems[qn] if kind == "d" else sems[qn], val)
            inst = o.emit(engobj)
            if o.need_inc:
                if o.dma is not None:
                    inst.then_inc(dsems[o.slot], 16)
                else:
                    inst.then_inc(sems[o.eng], 1)

    block = es.enter_context(nc.Block())

    @block.tensor
    def _(e):
        emit(e, "tensor")

    @block.vector
    def _(e):
        emit(e, "vector")

    @block.scalar
    def _(e):
        emit(e, "scalar")

    @block.gpsimd
    def _(e):
        emit(e, "gpsimd")

    @block.sync
    def _(e):
        emit(e, "sync")



def _consts_np():
    k = np.arange(128)
    cum = (k[:, None] <= k[None, :]).astype(np.float32)
    gt = (k[:, None] > k[None, :]).astype(np.float32)
    neglt = np.where(k[:, None] < k[None, :], np.float32(-1e30), np.float32(0))
    return {
        "I_BF": np.eye(128, dtype=np.float32).astype(ml_dtypes.bfloat16),
        "I_F8": np.eye(128, dtype=np.float32).astype(ml_dtypes.float8_e4m3fn),
        "I_F32": np.eye(128, dtype=np.float32),
        "CUM_BF": cum.astype(ml_dtypes.bfloat16),
        "CUM_F32": cum,
        "GT_BF": gt.astype(ml_dtypes.bfloat16),
        "GT_F32": gt,
        "NEGLT_BF": neglt.astype(ml_dtypes.bfloat16),
        "I4_BF": np.tile(np.eye(128, dtype=np.float32), (1, 4)).astype(
            ml_dtypes.bfloat16),
        "ONES_F32": np.ones((128, 128), np.float32),
    }


_NPDT = {"I_BF": BF16, "I_F8": FP8, "I_F32": F32, "CUM_BF": BF16,
         "CUM_F32": F32, "GT_BF": BF16, "GT_F32": F32, "NEGLT_BF": BF16,
         "I4_BF": BF16, "ONES_F32": F32}


class B:
    def __init__(self):
        self.nc = bass.Bass("TRN2", target_bir_lowering=False)
        self.es = contextlib.ExitStack()
        self.prog = Prog()
        self.t = {}

    def dram_in(self, name, shape, dtype):
        ap = self.nc.dram_tensor(name, list(shape), dtype,
                                 kind="ExternalInput").ap()
        self.t[name] = ap
        return ap

    def dram_out(self, name, shape, dtype):
        ap = self.nc.dram_tensor(name, list(shape), dtype,
                                 kind="ExternalOutput").ap()
        self.t[name] = ap
        return ap

    def dram_tmp(self, name, shape, dtype):
        ap = self.nc.dram_tensor(name, list(shape), dtype, kind="Internal").ap()
        self.t[name] = ap
        return ap

    def sb(self, name, shape, dtype, es=None):
        h = (es or self.es).enter_context(
            self.nc.sbuf_tensor(name, list(shape), dtype))
        self.t[name] = h
        return h

    def ps(self, name, shape):
        h = self.es.enter_context(self.nc.psum_tensor(name, list(shape), F32))
        self.t[name] = h
        return h

    def load_consts(self, names):
        cn = _consts_np()
        out = {}
        for name in names:
            arr = np.ascontiguousarray(cn[name])
            if _NPDT[name] == FP8:
                dr = self.nc.inline_tensor(
                    arr.view(np.uint8), name + "_d").ap().bitcast(FP8)
            else:
                dr = self.nc.inline_tensor(arr, name + "_d").ap()
            t = self.sb(name, list(arr.shape), _NPDT[name])
            self.dma("sp", t[:, :], dr, [], [name], slot="batch:consts",
                     dbg=f"cst_{name}")
            out[name] = t
        return out

    def dma(self, q, out, in_, reads, writes, slot=None, slow=False, dbg=""):
        def em(e, out=out, in_=in_, slow=slow):
            return e.dma_start(out=out, in_=in_,
                               allow_slow_non_contiguous=slow)
        self.prog.op(QUEUE_ENG[q], em, reads, writes, dma=q,
                     slot=slot or "batch:misc", dbg=dbg)

    def mm(self, out, lhsT, rhs, start, stop, reads, writes, dbg="",
           perf_mode=None):
        def em(e, out=out, lhsT=lhsT, rhs=rhs, start=start, stop=stop,
               perf_mode=perf_mode):
            return e.matmul(out, lhsT, rhs, start=start, stop=stop,
                            perf_mode=perf_mode)
        self.prog.op("tensor", em, reads, writes, dbg=dbg)

    def act(self, out, in_, func, reads, writes, bias=0.0, scale=1.0,
            accum=None, dbg=""):
        def em(e, out=out, in_=in_, func=func, bias=bias, scale=scale,
               accum=accum):
            return e.activation(out, in_, func, bias=bias, scale=scale,
                                accum_out=accum)
        self.prog.op("scalar", em, reads, writes, dbg=dbg)

    def vop(self, eng, fn, reads, writes, dbg=""):
        self.prog.op(eng, fn, reads, writes, dbg=dbg)

    def fence(self, keys):
        self.prog.op("gpsimd", lambda e: e.nop(), keys, [], dbg="fence")

    def epoch(self):
        keys = sorted(set(self.prog.last_write) | set(self.prog.readers))
        o = self.prog.op("gpsimd", lambda e: e.nop(), keys, keys, dbg="epoch")
        self.prog.epoch_idx = o.idx



def col_max_to_scalar(b, col_ap, col_key, out_ap, out_key, tag, row_sb,
                      row_key):
    bounce = b.dram_tmp("bnc_" + tag, [128], F32)
    b.dma("pool", bounce, col_ap, [col_key], ["bnc" + tag], slot="bn" + tag,
          slow=True, dbg="bw" + tag)
    b.dma("pool", row_sb, bounce.rearrange("(a k) -> a k", a=1),
          ["bnc" + tag], [row_key], slot="bn" + tag, dbg="br" + tag)

    def em(e):
        return e.reduce_max(out_ap, row_sb, axis=AX.X)
    b.vop("vector", em, [row_key], [out_key], dbg="bm" + tag)


def bcast_row(b, row_ap, row_keys, n, out_ap, out_key, tag):
    bounce = b.dram_tmp("bcr_" + tag, [1, n], F32)
    b.dma("pool", bounce, row_ap, row_keys, ["bcr" + tag], slot="bc" + tag,
          dbg="cw" + tag)
    b.dma("pool", out_ap, bounce.broadcast_to([128, n]), ["bcr" + tag],
          [out_key], slot="bc" + tag, slow=True, dbg="cr" + tag)


def ffn_scalars(b, amax_row_ap, amax_row_key, wam_ap, wam_key, bc, tag,
                es=None):
    sc = b.sb("sc_" + tag, [1, 4], F32, es=es)
    row = b.sb("scrow_" + tag, [1, 4], F32, es=es)

    def e0(e):
        return e.reduce_max(sc[0:1, 0:1], amax_row_ap, axis=AX.X)
    b.vop("vector", e0, [amax_row_key], [tag + "s0"], dbg=tag + "mx")
    b.vop("vector", lambda e: e.tensor_scalar(
        sc[0:1, 0:1], sc[0:1, 0:1], 1.0 / 448.0, 1e-12, ALU.mult, ALU.max),
        [tag + "s0"], [tag + "s0"], dbg=tag + "s")
    b.vop("vector", lambda e: e.tensor_scalar(
        sc[0:1, 1:2], wam_ap, 1.0 / 448.0, 1e-12, ALU.mult, ALU.max),
        [wam_key], [tag + "s1"], dbg=tag + "sw")
    b.vop("vector", lambda e: e.reciprocal(sc[0:1, 2:3], sc[0:1, 0:1]),
          [tag + "s0"], [tag + "s2"], dbg=tag + "inv")
    b.vop("vector", lambda e: e.tensor_scalar(
        sc[0:1, 2:3], sc[0:1, 2:3], 0.5, None, ALU.mult),
        [tag + "s2"], [tag + "s2"], dbg=tag + "half")
    b.vop("vector", lambda e: e.tensor_tensor(
        sc[0:1, 3:4], sc[0:1, 0:1], sc[0:1, 1:2], ALU.mult),
        [tag + "s0", tag + "s1"], [tag + "s3"], dbg=tag + "ssw")
    b.vop("vector", lambda e: e.tensor_scalar(
        sc[0:1, 3:4], sc[0:1, 3:4], 2.0, None, ALU.mult),
        [tag + "s3"], [tag + "s3"], dbg=tag + "g2")
    b.vop("vector", lambda e: e.reciprocal(row[0:1, 2:3], sc[0:1, 1:2]),
          [tag + "s1"], [tag + "r2"], dbg=tag + "isw")
    b.vop("vector", lambda e: e.tensor_scalar(
        row[0:1, 2:3], row[0:1, 2:3], 0.5, None, ALU.mult),
        [tag + "r2"], [tag + "r2"], dbg=tag + "hsw")
    b.vop("vector", lambda e: e.tensor_copy(row[0:1, 0:1], sc[0:1, 2:3]),
          [tag + "s2"], [tag + "r0"], dbg=tag + "c0")
    b.vop("vector", lambda e: e.tensor_copy(row[0:1, 1:2], sc[0:1, 3:4]),
          [tag + "s3"], [tag + "r1"], dbg=tag + "c1")
    bcast_row(b, row[0:1, 0:3], [tag + "r0", tag + "r1", tag + "r2"], 3,
              bc[:, 0:3], "bc", tag)


def qscale_cols(b, qsc, qbs, m_sb, m_key, r_sb, r_key, bc):
    b.vop("vector", lambda e: e.tensor_single_scalar(
        qsc[:, :], r_sb[:, :], bc[:, 0:1], ALU.mult),
        [r_key, "bc"], ["qsc"], dbg="qsc")
    b.vop("vector", lambda e: e.tensor_tensor(
        qbs[:, :], m_sb[:, :], qsc[:, :], ALU.mult),
        [m_key, "qsc"], ["qbs"], dbg="qbs0")
    b.vop("vector", lambda e: e.tensor_scalar(
        qbs[:, :], qbs[:, :], -1.0, None, ALU.mult),
        ["qbs"], ["qbs"], dbg="qbs")


def load_wq(b, es, bc, wq_sb):
    WQT = b.t["WQT"]
    tiles = []
    for dc in range(16):
        t = b.sb(f"wqt_{dc}", [128, 2048], BF16, es=es)
        tiles.append(t)
        b.dma(("sp", "act")[dc % 2], t[:, :], WQT[dc, :, :], [], [f"wqt{dc}"],
              slot=f"batch:wqld{dc % 2}", dbg=f"wql{dc}")
    wmx = b.sb("wmx", [128, 2], F32, es=es)
    for dc in range(16):
        def em(e, dc=dc):
            return e.reduce_max(wmx[:, 1:2], tiles[dc][:, :], axis=AX.X,
                                apply_absolute_value=True)
        b.vop("vector", em, [f"wqt{dc}"], ["wmx1"], dbg=f"wam{dc}")

        def em2(e, dc=dc):
            if dc == 0:
                return e.tensor_copy(wmx[:, 0:1], wmx[:, 1:2])
            return e.tensor_tensor(wmx[:, 0:1], wmx[:, 0:1], wmx[:, 1:2],
                                   ALU.max)
        b.vop("vector", em2, ["wmx1", "wmx0"], ["wmx0"], dbg=f"wamx{dc}")
    return tiles, wmx


def quant_wq(b, tiles, wq_sb):
    for dc in range(16):
        b.act(wq_sb[:, dc * 2048:(dc + 1) * 2048], tiles[dc][:, :],
              AF.Identity, reads=[f"wqt{dc}", "bc"], writes=[f"wq{dc}"],
              scale=b.t["bcb"][:, 2:3], dbg=f"wqq{dc}")


def ffn_chunk(b, c, src_ap, src_key, qsc, qbs, wq_sb, banks, tbanks, q1, q1T,
              kq, kqT, f1, f1key, IF8, bc):
    b.act(q1[:, :], src_ap, AF.Identity, reads=[src_key, "qsc", "qbs"],
          writes=[kq], scale=qsc[:, c:c + 1], bias=qbs[:, c:c + 1],
          dbg=f"qz{c}")
    for dc in range(16):
        tb, tk = tbanks[dc % 2]
        b.mm(tb[:, 0:128], q1[:, dc * 128:(dc + 1) * 128], IF8[:, :],
             True, True, [kq, "I_F8"], [tk], dbg=f"qT{c}_{dc}")
        if dc % 2 == 0:
            def ev(e, dc=dc, tb=tb):
                return e.tensor_copy(q1T[:, dc * 128:(dc + 1) * 128],
                                     tb[:, 0:128])
            b.vop("vector", ev, [tk], [f"{kqT}{dc}"], dbg=f"qTe{c}_{dc}")
        else:
            b.act(q1T[:, dc * 128:(dc + 1) * 128], tb[:, 0:128],
                  AF.Identity, reads=[tk], writes=[f"{kqT}{dc}"],
                  dbg=f"qTe{c}_{dc}")
    wqr = wq_sb.rearrange("p (d x) -> p d x", x=2048)
    for osl in range(4):
        pso, pk = banks[osl]
        for j in range(8):
            b.mm(pso[:, :],
                 q1T[:, j * 256:(j + 1) * 256].rearrange(
                     "p (o t) -> p o t", o=2),
                 wqr[:, 2 * j:2 * j + 2, osl * 512:(osl + 1) * 512],
                 j == 0, j == 7,
                 [f"{kqT}{2 * j}", f"{kqT}{2 * j + 1}",
                  f"wq{2 * j}", f"wq{2 * j + 1}"], [pk],
                 perf_mode=mybir.MatmulPerfMode.DoubleRow,
                 dbg=f"fm{c}_{osl}_{j}")
        if osl % 2 == 0:
            b.act(f1[:, osl * 512:(osl + 1) * 512], pso[:, :], AF.Identity,
                  reads=[pk, "bc"], writes=[f1key], scale=bc[:, 1:2],
                  dbg=f"fe{c}_{osl}")
        else:
            def evf(e, osl=osl, pso=pso):
                return e.tensor_single_scalar(
                    f1[:, osl * 512:(osl + 1) * 512], pso[:, :], bc[:, 1:2],
                    ALU.mult)
            b.vop("vector", evf, [pk, "bc"], [f1key], dbg=f"fe{c}_{osl}")



def build_L0():
    b = B()
    nc = b.nc
    u = b.dram_in("u", [NCHUNK, 128, D], F32)
    M1 = b.dram_out("m1", [128, NCHUNK], F32)
    R1 = b.dram_out("r1", [128, NCHUNK], F32)
    AMAX1 = b.dram_out("amax1", [1, 1], F32)
    with b.es:
        ub = [b.sb(f"ub{i}", [128, D], F32) for i in range(3)]
        scrap = b.sb("scrap", [128, D], F32)
        sumc = b.sb("sumc", [128, NCHUNK], F32)
        sqc = b.sb("sqc", [128, NCHUNK], F32)
        mxc = b.sb("mxc", [128, NCHUNK], F32)
        mnc = b.sb("mnc", [128, NCHUNK], F32)
        m1 = b.sb("m1s", [128, NCHUNK], F32)
        r1 = b.sb("r1s", [128, NCHUNK], F32)
        g = b.sb("g", [128, NCHUNK], F32)
        tmp = b.sb("tmp", [128, NCHUNK], F32)
        acol = b.sb("acol", [128, 1], F32)
        epsc = b.sb("epsc", [128, 1], F32)
        one = b.sb("one", [1, 2], F32)
        row = b.sb("row", [1, 128], F32)
        b.vop("vector", lambda e: e.memset(epsc[:, :], 1e-5), [], ["epsc"])

        for c in range(NCHUNK):
            buf, bk = ub[c % 3], f"ub{c % 3}"
            b.dma(("sp", "pool", "act")[c % 3], buf[:, :], u[c, :, :], [],
                  [bk], slot=bk, dbg=f"u{c}")

            b.act(scrap[:, :], buf[:, :], AF.Identity, reads=[bk],
                  writes=["scrap", f"sum{c}"], accum=sumc[:, c:c + 1],
                  dbg=f"sm{c}")
            b.act(scrap[:, :], buf[:, :], AF.Square, reads=[bk],
                  writes=["scrap", f"sq{c}"], accum=sqc[:, c:c + 1],
                  dbg=f"sq{c}")

            def emx(e, c=c, buf=buf):
                return e.reduce_max(mxc[:, c:c + 1], buf[:, :], axis=AX.X)
            b.vop("vector", emx, [bk], [f"mx{c}"], dbg=f"mx{c}")

            def emn(e, c=c, buf=buf):
                return e.tensor_reduce(mnc[:, c:c + 1], buf[:, :], axis=AX.X,
                                       op=ALU.min)
            b.vop("vector", emn, [bk], [f"mn{c}"], dbg=f"mn{c}")

        sall = [f"sum{c}" for c in range(NCHUNK)]
        qall = [f"sq{c}" for c in range(NCHUNK)]
        mxall = [f"mx{c}" for c in range(NCHUNK)]
        mnall = [f"mn{c}" for c in range(NCHUNK)]
        b.vop("vector", lambda e: e.tensor_scalar(
            m1[:, :], sumc[:, :], 1.0 / D, None, ALU.mult), sall, ["m1"],
            dbg="m1")
        b.vop("vector", lambda e: e.tensor_scalar(
            tmp[:, :], sqc[:, :], 1.0 / D, None, ALU.mult), qall, ["tmp"],
            dbg="ex2")
        b.vop("vector", lambda e: e.tensor_tensor(
            r1[:, :], m1[:, :], m1[:, :], ALU.mult), ["m1"], ["r1"], dbg="m2")
        b.vop("vector", lambda e: e.tensor_tensor(
            r1[:, :], tmp[:, :], r1[:, :], ALU.subtract), ["tmp", "r1"],
            ["r1"], dbg="var")
        b.act(r1[:, :], r1[:, :], AF.Sqrt, reads=["r1", "epsc"], writes=["r1"],
              bias=epsc[:, 0:1], dbg="std")
        b.vop("vector", lambda e: e.reciprocal(r1[:, :], r1[:, :]), ["r1"],
              ["r1"], dbg="rstd")
        b.vop("vector", lambda e: e.tensor_tensor(
            g[:, :], mxc[:, :], m1[:, :], ALU.subtract), mxall + ["m1"], ["g"],
            dbg="g1")
        b.vop("vector", lambda e: e.tensor_tensor(
            tmp[:, :], m1[:, :], mnc[:, :], ALU.subtract), mnall + ["m1"],
            ["tmp"], dbg="g2")
        b.vop("vector", lambda e: e.tensor_tensor(
            g[:, :], g[:, :], tmp[:, :], ALU.max), ["g", "tmp"], ["g"],
            dbg="g3")
        b.vop("vector", lambda e: e.tensor_tensor(
            g[:, :], g[:, :], r1[:, :], ALU.mult), ["g", "r1"], ["g"],
            dbg="g4")

        def ea(e):
            return e.reduce_max(acol[:, :], g[:, :], axis=AX.X)
        b.vop("vector", ea, ["g"], ["acol"], dbg="acol")
        col_max_to_scalar(b, acol[:, 0:1], "acol", one[0:1, 0:1], "a1", "a",
                          row[0:1, :], "rowbuf")

        b.dma("sp", M1, m1[:, :], ["m1"], ["o1"], slot="batch:outs",
              dbg="st_m1")
        b.dma("sp", R1, r1[:, :], ["r1"], ["o2"], slot="batch:outs",
              dbg="st_r1")
        b.dma("sp", AMAX1, one[0:1, 0:1], ["a1"], ["o3"], slot="batch:outs",
              dbg="st_a1")
        b.fence(["o1", "o2", "o3"])
        run_streams(nc, b.es, b.prog)
    return nc



def build_L1():
    b = B()
    nc = b.nc
    u = b.dram_in("u", [NCHUNK, 128, D], F32)
    WA = b.dram_in("WA", [NEB, 128, 16, 128], F32)
    WB = b.dram_in("WB", [4, 16, 128, 512], F32)
    b.dram_in("WQT", [16, 128, 2048], BF16)
    M1 = b.dram_in("m1", [128, NCHUNK], F32)
    R1 = b.dram_in("r1", [128, NCHUNK], F32)
    A1V = b.dram_in("amax1v", [1, 8], F32)
    CWH = b.dram_in("cwh", [128, 192], F32)
    CBH = b.dram_in("cbh", [128, 48], F32)
    ZBH = b.dram_in("zbh", [128, 16], F32)
    RWH = b.dram_in("rwh", [128, 16], F32)
    DREP = b.dram_in("drep", [128, H * 64], F32)
    X3 = b.dram_out("x3", [NCHUNK, 128, D], F32)
    M3 = b.dram_out("m3", [128, NCHUNK], F32)
    R3 = b.dram_out("r3", [128, NCHUNK], F32)
    AMAX3 = b.dram_out("amax3", [1, 1], F32)
    if DEBUG_L1:
        x2d = b.dram_out("x2d", [NCHUNK, 128, D], F32)
        XBCD = b.dram_out("xbcd", [128, 48 * 515], BF16)
        ZSTD = b.dram_out("zstd", [128, CPS * D], BF16)
        ATSD = b.dram_out("atsd", [128, 32], F32)
        EXPAD = b.dram_out("expad", [128, 32], F32)
        YBD = b.dram_out("ybd", [128, D], BF16)
        H2TD = b.dram_out("h2td", [128, 16 * 512], BF16)
        BCD = b.dram_out("bcd", [128, 4], F32)
        QSCD = b.dram_out("qscd", [128, NCHUNK], F32)
        QBSD = b.dram_out("qbsd", [128, NCHUNK], F32)
        Q1D = b.dram_out("q1d", [128, D], mybir.dt.uint8)
        F1D = b.dram_out("f1d", [128, D], BF16)
    else:
        x2d = b.dram_tmp("x2d", [NCHUNK, 128, D], F32)
    WINBF = b.dram_tmp("winbf", [NEB, 128, 2048], BF16)
    WOUTBF = b.dram_tmp("woutbf", [4, 16, 128, 512], BF16)

    with b.es:
        cI = b.load_consts(["I_BF", "I_F8", "I_F32", "CUM_BF", "CUM_F32",
                            "GT_BF", "GT_F32", "NEGLT_BF", "ONES_F32"])
        psA = b.ps("psA", [128, 512])
        psB = b.ps("psB", [128, 512])
        psT = b.ps("psT", [128, 512])
        psE = b.ps("psE", [128, 512])
        psS = b.ps("psS", [128, 512])
        psYa = b.ps("psYa", [128, 512])
        psYb = b.ps("psYb", [128, 512])
        psSt = b.ps("psSt", [128, 512])

        st = {}
        for nm in ("m1s", "r1s", "m2s", "r2s", "m3s", "r3s", "qsc", "qbs",
                   "h2sc", "h2bs", "mx3s", "mn3s"):
            st[nm] = b.sb(nm, [128, NCHUNK], F32)
        epsc = b.sb("epsc", [128, 1], F32)
        b.vop("vector", lambda e: e.memset(epsc[:, :], 1e-5), [], ["epsc"])
        rmsw = b.sb("rmsw", [128, 16], F32)
        zbias = b.sb("zbias", [128, 16], F32)
        cw = b.sb("cw", [128, 192], F32)
        cb = b.sb("cb", [128, 48], F32)
        DREPb = b.sb("DREPb", [128, H * 64], BF16)
        scrap = b.sb("scrap", [128, D], BF16)
        bc = b.sb("bcb", [128, 4], F32)
        a1row = b.sb("a1row", [1, 8], F32)
        warow = b.sb("warow", [1, 1], F32)
        acol3 = b.sb("acol3", [128, 1], F32)
        one3 = b.sb("one3", [1, 2], F32)
        rowb = b.sb("rowb", [1, 128], F32)
        stc = b.sb("stc", [128, 4], F32)

        b.dma("sp", st["m1s"][:, :], M1, [], ["m1"], slot="batch:init", dbg="ld_m1")
        b.dma("sp", st["r1s"][:, :], R1, [], ["r1"], slot="batch:init", dbg="ld_r1")
        b.dma("sp", rmsw[:, :], RWH, [], ["rmsw"], slot="batch:init", dbg="ld_rw")
        b.dma("sp", zbias[:, :], ZBH, [], ["zbias"], slot="batch:init", dbg="ld_zb")
        b.dma("sp", cw[:, :], CWH, [], ["cw"], slot="batch:init", dbg="ld_cw")
        b.dma("sp", cb[:, :], CBH, [], ["cb"], slot="batch:init", dbg="ld_cb")
        b.dma("sp", a1row[0:1, :], A1V, [], ["a1row"], slot="batch:init", dbg="ld_a1")

        with contextlib.ExitStack() as fes:
            wq = b.sb("wq", [128, 16 * 2048], FP8, es=fes)
            drepf = b.sb("drepf", [128, H * 64], F32, es=fes)
            b.dma("pool", drepf[:, :], DREP, [], ["drepr"], slot="drep",
                  dbg="ld_drepr")
            b.vop("vector", lambda e: e.tensor_copy(DREPb[:, :],
                                                    drepf[:, :]),
                  ["drepr"], ["DREPb"], dbg="drepc")
            tiles, wmx = load_wq(b, fes, bc, wq)
            col_max_to_scalar(b, wmx[:, 0:1], "wmx0", warow[0:1, 0:1], "wam",
                              "w1", rowb[0:1, :], "rowbufW")
            ffn_scalars(b, a1row[0:1, :], "a1row", warow[0:1, 0:1], "wam",
                        bc, "f1", es=fes)
            qscale_cols(b, st["qsc"], st["qbs"], st["m1s"], "m1", st["r1s"],
                        "r1", bc)
            quant_wq(b, tiles, wq)
            ub = [b.sb(f"ub{i}", [128, D], F32, es=fes) for i in range(2)]
            q1 = [b.sb(f"q1_{i}", [128, D], FP8, es=fes) for i in range(2)]
            q1T = [b.sb(f"q1T_{i}", [128, D], FP8, es=fes) for i in range(2)]
            f1 = [b.sb(f"f1_{i}", [128, D], BF16, es=fes) for i in range(2)]
            x2b = [b.sb(f"x2b{i}", [128, D], F32, es=fes) for i in range(2)]
            for c in range(NCHUNK):
                i = c % 2
                b.dma("pool", ub[i][:, :], u[c, :, :], [], [f"ub{i}"],
                      slot=f"ub{i}", dbg=f"u{c}")
                ffn_chunk(b, c, ub[i][:, :], f"ub{i}", st["qsc"], st["qbs"],
                          wq,
                          [(psA, "psA"), (psB, "psB"), (psE, "psE"),
                           (psS, "psS")],
                          [(psT, "psT"), (psSt, "psSt")],
                          q1[i], q1T[i], f"q1{i}", f"q1T{i}", f1[i],
                          f"f1{i}", cI["I_F8"], bc)
                if DEBUG_L1 and c == 0:
                    b.dma("sp", BCD, bc[:, :], ["bc"], ["dg7"], dbg="dBC", slot="batch:dbg")
                    b.dma("sp", QSCD, st["qsc"][:, :], ["qsc"], ["dg8"],
                          dbg="dQS", slot="batch:dbg")
                    b.dma("sp", QBSD, st["qbs"][:, :], ["qbs"], ["dg9"],
                          dbg="dQB", slot="batch:dbg")
                    b.dma("sp", Q1D, q1[0][:, :].bitcast(mybir.dt.uint8),
                          ["q10"], ["dgA"], slot="batch:dbg", dbg="dQ1")
                    b.dma("sp", F1D, f1[0][:, :], ["f10"], ["dgB"], dbg="dF1", slot="batch:dbg")

                def eadd(e, i=i):
                    return e.tensor_tensor(x2b[i][:, :], ub[i][:, :],
                                           f1[i][:, :], ALU.add)
                b.vop("vector", eadd, [f"ub{i}", f"f1{i}"], [f"x2b{i}"],
                      dbg=f"x2a{c}")
                b.act(scrap[:, :], x2b[i][:, :], AF.Identity,
                      reads=[f"x2b{i}"], writes=["scrap", "h2", f"s2_{c}"],
                      accum=st["m2s"][:, c:c + 1], dbg=f"s2{c}")
                b.act(scrap[:, :], x2b[i][:, :], AF.Square,
                      reads=[f"x2b{i}"], writes=["scrap", "h2", f"v2_{c}"],
                      accum=st["r2s"][:, c:c + 1], dbg=f"v2{c}")
                b.dma("pool", x2d[c, :, :], x2b[i][:, :], [f"x2b{i}"],
                      [f"x2d{c}"], slot=f"x2b{i}", dbg=f"x2s{c}")
        b.epoch()
        s2a = [f"s2_{c}" for c in range(NCHUNK)]
        v2a = [f"v2_{c}" for c in range(NCHUNK)]
        tmps = b.sb("tmps", [128, NCHUNK], F32)
        b.vop("vector", lambda e: e.tensor_scalar(
            st["m2s"][:, :], st["m2s"][:, :], 1.0 / D, None, ALU.mult),
            s2a, ["m2"], dbg="m2")
        b.vop("vector", lambda e: e.tensor_scalar(
            st["r2s"][:, :], st["r2s"][:, :], 1.0 / D, None, ALU.mult),
            v2a, ["r2x"], dbg="r2x")
        b.vop("vector", lambda e: e.tensor_tensor(
            tmps[:, :], st["m2s"][:, :], st["m2s"][:, :], ALU.mult),
            ["m2"], ["tmps"], dbg="m2q")
        b.vop("vector", lambda e: e.tensor_tensor(
            st["r2s"][:, :], st["r2s"][:, :], tmps[:, :], ALU.subtract),
            ["r2x", "tmps"], ["r2y"], dbg="v2f")
        b.act(st["r2s"][:, :], st["r2s"][:, :], AF.Sqrt,
              reads=["r2y", "epsc"], writes=["r2z"], bias=epsc[:, 0:1],
              dbg="sd2")
        b.vop("vector", lambda e: e.reciprocal(st["r2s"][:, :],
                                               st["r2s"][:, :]),
              ["r2z"], ["r2"], dbg="rs2")
        b.vop("vector", lambda e: e.tensor_tensor(
            st["h2bs"][:, :], st["m2s"][:, :], st["r2s"][:, :], ALU.mult),
            ["m2", "r2"], ["h2b0"], dbg="h2b0")
        b.vop("vector", lambda e: e.tensor_scalar(
            st["h2bs"][:, :], st["h2bs"][:, :], -1.0, None, ALU.mult),
            ["h2b0"], ["h2bs"], dbg="h2b")

        x2s = b.sb("x2s", [128, CPS * D], F32)
        h2 = scrap
        h2T = b.sb("h2T", [128, 16 * 512], BF16)
        wtmp = [b.sb(f"wtmp{i}", [128, 2048], F32) for i in range(2)]
        winb = [b.sb(f"winb{i}", [128, 2048], BF16) for i in range(2)]
        xbc = b.sb("xbc", [128, 48 * 515], BF16)
        tailn = b.sb("tailn", [128, 48 * 4], BF16)
        zfm = [b.sb("zfm0", [128, 512], BF16)] * 2
        zst = b.sb("zst", [128, CPS * D], BF16)
        afm = b.sb("afm", [32, 512], F32)
        asp = b.sb("asp", [32, 512], F32)
        ats = b.sb("ats", [128, 32], F32)
        cssb = b.sb("cssb", [128, 32], F32)
        expA = b.sb("expA", [128, 32], F32)
        decay = b.sb("decay", [128, 32], F32)
        expTb = b.sb("expTb", [128, 32], F32)
        Xt = b.sb("Xt", [128, 2048], BF16)
        Bt = b.sb("Bt", [128, 2048], BF16)
        amb8 = b.sb("amb8", [128, 1024], BF16)
        lmb2 = b.sb("lmb2", [128, 256], BF16)
        mpb2 = b.sb("mpb2", [128, 256], BF16)
        xdb8 = b.sb("xdb8", [128, 512], BF16)
        XDt = b.sb("XDt", [128, D], BF16)
        yob = [b.sb(f"yob{i}", [128, 512], BF16) for i in range(2)]
        Sf = b.sb("Sf", [128, 16 * 64], F32)
        Sb_ = b.sb("Sb", [128, 16 * 64], BF16)
        Ybuf = b.sb("Ybuf", [128, D], BF16)
        ynT = b.sb("ynT", [128, 16 * 512], BF16)
        wob = [wtmp[0][:, 0:512], wtmp[1][:, 0:512]]
        wobb = [b.sb(f"wobb{i}", [128, 512], BF16) for i in range(2)]
        cdg4 = b.sb("cdg4", [128, 512], BF16)

        skeys = [f"Sf{h}" for h in range(H)]
        sbkeys = [f"Sb{h}" for h in range(H)]
        b.vop("vector", lambda e: e.memset(Sf[:, :], 0.0), [], skeys, dbg="S0")
        b.vop("gpsimd", lambda e: e.memset(Sb_[:, :], 0.0), [], sbkeys,
              dbg="Sb0")

        def eguard(e):
            return e.memset(
                xbc.rearrange("p (t c) -> p t c", c=515)[:, :, 0:3], 0.0)
        b.vop("gpsimd", eguard, [], ["xbcg"], dbg="guard")

        for s in range(SWEEP):
            sw = f"s{s}"
            h2Tkeys = [f"h2T{dc}" for dc in range(16)]
            for k in range(CPS):
                c = s * CPS + k
                b.dma("pool", x2s[:, k * D:(k + 1) * D], x2d[c, :, :],
                      [f"x2d{c}"], [f"x2s{k}"], slot=f"x2sl{k}",
                      dbg=f"{sw}x2l{k}")
                b.act(h2[:, :], x2s[:, k * D:(k + 1) * D], AF.Identity,
                      reads=[f"x2s{k}", "r2", "h2bs"], writes=["h2", "scrap"],
                      scale=st["r2s"][:, c:c + 1], bias=st["h2bs"][:, c:c + 1],
                      dbg=f"{sw}h2{k}")
                for dc in range(16):
                    tb, tk = (psT, "psT") if dc % 2 == 0 else (psSt, "psSt")
                    b.mm(tb[:, 0:128], h2[:, dc * 128:(dc + 1) * 128],
                         cI["I_BF"][:, :], True, True, ["h2", "I_BF"], [tk],
                         dbg=f"{sw}hT{k}_{dc}")
                    dst = h2T[:, dc * 512 + k * 128: dc * 512 + k * 128 + 128]
                    if dc % 2 == 0:
                        def ev(e, dst=dst, tb=tb):
                            return e.tensor_copy(dst, tb[:, 0:128])
                        b.vop("vector", ev, [tk], [f"h2T{dc}"],
                              dbg=f"{sw}hTe{k}_{dc}")
                    else:
                        b.act(dst, tb[:, 0:128], AF.Identity, reads=[tk],
                              writes=[f"h2T{dc}"], dbg=f"{sw}hTe{k}_{dc}")
            if s > 0:
                def etail(e):
                    return e.tensor_copy(
                        xbc.rearrange("p (t c) -> p t c", c=515)[:, :, 0:3],
                        tailn.rearrange("p (t c) -> p t c", c=4)[:, :, 1:4])
                b.vop("gpsimd", etail, ["tailn"], ["xbcg"], dbg=f"{sw}tlr")

            for eb in range(NEB):
                i = eb % 2
                wkey = f"win{i}"
                if s == 0:
                    b.dma("sp", wtmp[i][:, :],
                          WA[eb].rearrange("p a j -> p (a j)"), [],
                          [f"wt{i}"], slot=f"wt{i}", dbg=f"{sw}wl{eb}")

                    def ecast(e, i=i):
                        return e.tensor_copy(winb[i][:, :], wtmp[i][:, :])
                    b.vop(("vector", "gpsimd")[i], ecast, [f"wt{i}"], [wkey],
                          dbg=f"{sw}wc{eb}")
                    b.dma("act", WINBF[eb, :, :], winb[i][:, :], [wkey],
                          [f"wbf{eb}"], slot=f"wb{i}", dbg=f"{sw}ws{eb}")
                else:
                    b.dma("sp", winb[i][:, :], WINBF[eb, :, :], [f"wbf{eb}"],
                          [wkey], slot=f"wb{i}", dbg=f"{sw}wl{eb}")
                pk = ("psA", "psB")[i]
                pso = (psA, psB)[i]
                for dc in range(16):
                    b.mm(pso[:, :], winb[i][:, dc * 128:(dc + 1) * 128],
                         h2T[:, dc * 512:(dc + 1) * 512], dc == 0, dc == 15,
                         [wkey] + h2Tkeys, [pk], dbg=f"{sw}pj{eb}_{dc}")
                if eb < 48:
                    dst = xbc[:, eb * 515 + 3: eb * 515 + 515]

                    def evx(e, dst=dst, pso=pso):
                        return e.tensor_copy(dst, pso[:, :])
                    b.vop("vector", evx, [pk], [f"xbcp{eb}"],
                          dbg=f"{sw}xe{eb}")
                    if s < SWEEP - 1 and FLAGS["tail_psum"]:
                        def etl(e, eb=eb, pso=pso):
                            return e.tensor_copy(
                                tailn[:, eb * 4:(eb + 1) * 4],
                                pso[:, 508:512])
                        b.vop("vector", etl, [pk], ["tailn"],
                              dbg=f"{sw}tlp{eb}")
                elif eb < 64:
                    zi = eb - 48
                    zb = zfm[0]
                    b.act(zb[:, :], pso[:, :], AF.Silu, reads=[pk, "zbias"],
                          writes=["zfm"], bias=zbias[:, zi:zi + 1],
                          dbg=f"{sw}zs{zi}")
                    for k in range(CPS):
                        tb, tk = (psT, "psT") if k % 2 == 0 else (psSt, "psSt")
                        b.mm(tb[:, 0:128], zb[:, k * 128:(k + 1) * 128],
                             cI["I_BF"][:, :], True, True,
                             ["zfm", "I_BF"], [tk],
                             dbg=f"{sw}zT{zi}_{k}")
                        dst = zst[:, k * D + zi * 128: k * D + zi * 128 + 128]
                        if k % 2 == 0:
                            def evz(e, dst=dst, tb=tb):
                                return e.tensor_copy(dst, tb[:, 0:128])
                            b.vop("vector", evz, [tk], [f"zst{k}"],
                                  dbg=f"{sw}zTe{zi}_{k}")
                        else:
                            b.act(dst, tb[:, 0:128], AF.Identity, reads=[tk],
                                  writes=[f"zst{k}"], dbg=f"{sw}zTe{zi}_{k}")
                else:
                    def eva(e, pso=pso):
                        return e.tensor_copy(asp[:, :], pso[0:32, :])
                    b.vop("vector", eva, [pk], ["asp"], dbg=f"{sw}ae")
                    b.act(afm[:, :], asp[:, :], AF.Abs, reads=["asp"],
                          writes=["afm0"], dbg=f"{sw}ab")
                    b.act(afm[:, :], afm[:, :], AF.Exp, reads=["afm0"],
                          writes=["afm0"], scale=-1.0, dbg=f"{sw}ax")
                    b.vop("vector", lambda e: e.tensor_scalar(
                        afm[:, :], afm[:, :], 1.0, None, ALU.add),
                        ["afm0"], ["afm0"], dbg=f"{sw}a1")
                    b.act(afm[:, :], afm[:, :], AF.Ln, reads=["afm0"],
                          writes=["afm0"], dbg=f"{sw}al")
                    b.act(asp[:, :], asp[:, :], AF.Relu, reads=["asp"],
                          writes=["asp"], dbg=f"{sw}ar")
                    b.vop("vector", lambda e: e.tensor_tensor(
                        afm[:, :], afm[:, :], asp[:, :], ALU.add),
                        ["afm0", "asp"], ["afm0"], dbg=f"{sw}aa")
                    b.vop("vector", lambda e: e.tensor_scalar(
                        afm[:, :], afm[:, :], -1.0, None, ALU.mult),
                        ["afm0"], ["afm"], dbg=f"{sw}an")

            for t in range(48):
                base = t * 515
                pc, pck = ((psS, "psS"), (psB, "psB"))[t % 2]
                for w in range(4):
                    def edg(e, t=t, w=w):
                        return e.tensor_scalar(
                            cdg4[:, w * 128:(w + 1) * 128], cI["I_BF"][:, :],
                            cw[:, t * 4 + w: t * 4 + w + 1], None, ALU.mult)
                    b.vop("vector", edg, ["cw", "I_BF"], [f"cd{w}"],
                          dbg=f"{sw}cd{t}_{w}")
                for w in range(4):
                    b.mm(pc[:, :], cdg4[:, w * 128:(w + 1) * 128],
                         xbc[:, base + w: base + w + 512], w == 0, w == 3,
                         [f"cd{w}", f"xbcp{t}", "xbcg"], [pck],
                         dbg=f"{sw}cv{t}_{w}")
                b.act(xbc[:, base + 3: base + 515], pc[:, :], AF.Silu,
                      reads=[pck, "cb"], writes=[f"xbcp{t}"],
                      bias=cb[:, t:t + 1], dbg=f"{sw}cs{t}")

            if DEBUG_L1 and s == 0:
                b.dma("sp", XBCD, xbc[:, :],
                      [f"xbcp{t}" for t in range(48, slot="batch:dbg")], ["dbg1"], dbg="dXBC")
                b.dma("sp", ZSTD, zst[:, :],
                      [f"zst{k}" for k in range(CPS, slot="batch:dbg")], ["dbg2"], dbg="dZST")
                b.dma("sp", H2TD, h2T[:, :], h2Tkeys, ["dbg3"], dbg="dH2T", slot="batch:dbg")
            for k in range(CPS):
                c = s * CPS + k
                ck = k * 128
                for j in range(16):
                    px, pxk = ((psT, "psT"), (psE, "psE"))[j % 2]
                    pb, pbk = ((psSt, "psSt"), (psA, "psA"))[j % 2]
                    b.mm(px[:, 0:128],
                         xbc[:, j * 515 + 3 + ck: j * 515 + 3 + ck + 128],
                         cI["I_BF"][:, :], True, True, [f"xbcp{j}", "I_BF"],
                         [pxk], dbg=f"{sw}Xt{k}_{j}")

                    def evxt(e, j=j, px=px):
                        return e.tensor_copy(Xt[:, j * 128:(j + 1) * 128],
                                             px[:, 0:128])
                    b.vop("vector", evxt, [pxk], ["Xt"],
                          dbg=f"{sw}Xte{k}_{j}")
                    b.mm(pb[:, 0:128],
                         xbc[:, (16 + j) * 515 + 3 + ck:
                             (16 + j) * 515 + 3 + ck + 128],
                         cI["I_BF"][:, :], True, True,
                         [f"xbcp{16 + j}", "I_BF"], [pbk],
                         dbg=f"{sw}Bt{k}_{j}")

                    def evbt(e, j=j, pb=pb):
                        return e.activation(Bt[:, j * 128:(j + 1) * 128],
                                            pb[:, 0:128], AF.Identity)
                    b.prog.op("scalar", evbt, [pbk], ["Bt"],
                              dbg=f"{sw}Bte{k}_{j}")
                b.mm(psE[:, 0:32], afm[0:32, ck:ck + 128],
                     cI["I_F32"][0:32, 0:32], True, True, ["afm", "I_F32"],
                     ["psE"], dbg=f"{sw}at{k}")

                def evat(e):
                    return e.tensor_copy(ats[:, :], psE[:, 0:32])
                b.vop("vector", evat, ["psE"], ["ats"], dbg=f"{sw}ate{k}")
                b.mm(psE[:, 0:32], cI["CUM_F32"][:, :], ats[:, :], True, True,
                     ["ats", "CUM_F32"], ["psE"], dbg=f"{sw}cs{k}")

                def evcs(e):
                    return e.tensor_copy(cssb[:, :], psE[:, 0:32])
                b.vop("vector", evcs, ["psE"], ["cssb"], dbg=f"{sw}cse{k}")
                b.act(expA[:, :], cssb[:, :], AF.Exp, reads=["cssb"],
                      writes=["expA"], dbg=f"{sw}eA{k}")
                b.mm(psE[:, 0:32], cI["GT_F32"][:, :], ats[:, :], True, True,
                     ["ats", "GT_F32"], ["psE"], dbg=f"{sw}dc{k}")
                b.act(decay[:, :], psE[:, 0:32], AF.Exp, reads=["psE"],
                      writes=["decay"], dbg=f"{sw}dcE{k}")
                def etb(e):
                    return e.tensor_tensor(expTb[:, :], expA[:, :],
                                           decay[:, :], ALU.mult)
                b.vop("vector", etb, ["expA", "decay"], ["expTb"],
                      dbg=f"{sw}tc{k}")

                def exdall(e):
                    return e.tensor_tensor(XDt[:, :], Xt[:, :], DREPb[:, :],
                                           ALU.mult)
                b.vop("vector", exdall, ["Xt", "DREPb"], ["XDt"],
                      dbg=f"{sw}xda{k}")

                for half in range(4):
                    hkey = f"yo{half % 2}"
                    for hh in range(8):
                        h = half * 8 + hh

                        def eam(e, hh=hh, h=h):
                            return e.tensor_scalar(
                                amb8[:, hh * 128:(hh + 1) * 128],
                                cI["CUM_BF"][:, :],
                                ats[:, h:h + 1], None, ALU.mult)
                        b.vop("vector", eam, ["ats", "CUM_BF"], [f"am{hh}"],
                              dbg=f"{sw}am{k}_{h}")

                        def exd(e, hh=hh, h=h):
                            return e.tensor_scalar(
                                xdb8[:, hh * 64:(hh + 1) * 64],
                                Xt[:, h * 64:(h + 1) * 64],
                                decay[:, h:h + 1], None, ALU.mult)
                        b.vop("vector", exd, ["Xt", "decay"], [f"xd{hh}"],
                              dbg=f"{sw}xd{k}_{h}")
                    for hh in range(9):
                        if hh < 8:
                            h = half * 8 + hh
                            r = h % 2
                            tile = h // 2
                            prow = (h % 2) * 64
                            bE, bEk = ((psE, "psE"), (psA, "psA"))[r]
                            bS, bSk = ((psS, "psS"), (psB, "psB"))[r]
                            fmBh = xbc[prow:prow + 64,
                                       (16 + tile) * 515 + 3 + ck:
                                       (16 + tile) * 515 + 3 + ck + 128]
                            fmCh = xbc[prow:prow + 64,
                                       (32 + tile) * 515 + 3 + ck:
                                       (32 + tile) * 515 + 3 + ck + 128]
                            b.mm(bE[:, 0:128], cI["GT_BF"][:, :],
                                 amb8[:, hh * 128:(hh + 1) * 128],
                                 True, False, [f"am{hh}", "GT_BF"], [bEk],
                                 dbg=f"{sw}E1{k}_{h}")
                            b.mm(bE[:, 0:128], cI["NEGLT_BF"][:, :],
                                 cI["I_BF"][:, :], False, True,
                                 ["NEGLT_BF", "I_BF"], [bEk],
                                 dbg=f"{sw}E2{k}_{h}")
                            b.act(lmb2[:, r * 128:(r + 1) * 128],
                                  bE[:, 0:128], AF.Exp, reads=[bEk],
                                  writes=[f"lm{r}"], dbg=f"{sw}L{k}_{h}")
                            b.mm(bS[:, 0:128], fmBh, fmCh, True, True,
                                 [f"xbcp{16 + tile}", f"xbcp{32 + tile}"],
                                 [bSk], dbg=f"{sw}sc{k}_{h}")

                            def emp(e, r=r, bS=bS):
                                return e.tensor_tensor(
                                    mpb2[:, r * 128:(r + 1) * 128],
                                    lmb2[:, r * 128:(r + 1) * 128],
                                    bS[:, 0:128], ALU.mult)
                            b.vop("vector", emp, [f"lm{r}", bSk], [f"mp{r}"],
                                  dbg=f"{sw}M{k}_{h}")
                        if hh >= 1:
                            h = half * 8 + hh - 1
                            hp = hh - 1
                            r = h % 2
                            tile = h // 2
                            prow = (h % 2) * 64
                            fmCh = xbc[prow:prow + 64,
                                       (32 + tile) * 515 + 3 + ck:
                                       (32 + tile) * 515 + 3 + ck + 128]
                            bT, bTk = ((psSt, "psSt"), (psT, "psT"))[r]
                            b.mm(psYa[:, hp * 64:(hp + 1) * 64],
                                 mpb2[:, r * 128:(r + 1) * 128],
                                 Xt[:, h * 64:(h + 1) * 64], True, True,
                                 [f"mp{r}", "Xt"], ["psYa"],
                                 dbg=f"{sw}Yd{k}_{h}")
                            if prow == 0:
                                stl = Bt[:, h * 64:(h + 1) * 64]
                                sto = bT[0:64, 0:64]
                            else:
                                stl = Bt[:, (h - 1) * 64:(h + 1) * 64]
                                sto = bT[:, 0:64]
                            b.mm(sto, stl, xdb8[:, hp * 64:(hp + 1) * 64],
                                 True, True, ["Bt", f"xd{hp}"], [bTk],
                                 dbg=f"{sw}st{k}_{h}")
                            b.mm(psYb[:, hp * 64:(hp + 1) * 64], fmCh,
                                 Sb_[prow:prow + 64,
                                     tile * 64:(tile + 1) * 64],
                                 True, True,
                                 [f"xbcp{32 + tile}", f"Sb{h}"], ["psYb"],
                                 dbg=f"{sw}Yo{k}_{h}")

                            def est(e, h=h, tile=tile, prow=prow, bT=bT):
                                return e.scalar_tensor_tensor(
                                    Sf[prow:prow + 64,
                                       tile * 64:(tile + 1) * 64],
                                    Sf[prow:prow + 64,
                                       tile * 64:(tile + 1) * 64],
                                    expTb[prow:prow + 64, h:h + 1],
                                    bT[prow:prow + 64, 0:64],
                                    ALU.mult, ALU.add)
                            b.vop("vector", est,
                                  [bTk, "expTb", f"Sf{h}"], [f"Sf{h}"],
                                  dbg=f"{sw}Su{k}_{h}")
                    for hh in range(8):
                        h = half * 8 + hh
                        b.act(yob[half % 2][:, hh * 64:(hh + 1) * 64],
                              psYb[:, hh * 64:(hh + 1) * 64], AF.Identity,
                              reads=["psYb", "expA"], writes=[hkey],
                              scale=expA[:, h:h + 1], dbg=f"{sw}yoe{k}_{h}")

                    def eass(e, half=half):
                        return e.tensor_tensor(
                            Ybuf[:, half * 512:(half + 1) * 512], psYa[:, :],
                            yob[half % 2][:, :], ALU.add)
                    b.vop("vector", eass, ["psYa", hkey], ["Ybuf"],
                          dbg=f"{sw}Ya{k}_{half}")

                    def eassB(e, half=half):
                        return e.tensor_tensor(
                            Ybuf[:, half * 512:(half + 1) * 512],
                            Ybuf[:, half * 512:(half + 1) * 512],
                            XDt[:, half * 512:(half + 1) * 512], ALU.add)
                    b.vop("vector", eassB, ["Ybuf", "XDt"], ["Ybuf"],
                          dbg=f"{sw}Yb{k}_{half}")
                skl = [f"Sf{h}" for h in range(H)]
                sbl = [f"Sb{h}" for h in range(H)]

                def esb(e):
                    return e.copy(Sb_[:, :], Sf[:, :])
                b.vop("scalar", esb, skl, sbl, dbg=f"{sw}Sc{k}")

                if DEBUG_L1 and s == 0 and k == 0:
                    b.dma("sp", ATSD, ats[:, :], ["ats"], ["dbg4"], dbg="dATS", slot="batch:dbg")
                    b.dma("sp", EXPAD, expA[:, :], ["expA"], ["dbg5"],
                          dbg="dEXP", slot="batch:dbg")
                    b.dma("sp", YBD, Ybuf[:, :], ["Ybuf"], ["dbg6"], dbg="dYB", slot="batch:dbg")

                def egate(e, k=k):
                    return e.tensor_tensor(Ybuf[:, :], Ybuf[:, :],
                                           zst[:, k * D:(k + 1) * D],
                                           ALU.mult)
                b.vop("vector", egate, ["Ybuf", f"zst{k}"], ["Ybuf"],
                      dbg=f"{sw}gt{k}")
                b.act(scrap[:, :], Ybuf[:, :], AF.Square, reads=["Ybuf"],
                      writes=["scrap", "h2", "ms0"], accum=stc[:, 0:1],
                      dbg=f"{sw}ms{k}")
                b.vop("vector", lambda e: e.tensor_scalar(
                    stc[:, 1:2], stc[:, 0:1], 1.0 / D, epsc[:, 0:1],
                    ALU.mult, ALU.add),
                    ["ms0", "epsc"], ["ms1"], dbg=f"{sw}msd{k}")
                if FLAGS["lnexp"]:
                    b.act(stc[:, 1:2], stc[:, 1:2], AF.Ln,
                          reads=["ms1"], writes=["ms2"], dbg=f"{sw}msq{k}")
                    b.act(stc[:, 2:3], stc[:, 1:2], AF.Exp, scale=-0.5,
                          reads=["ms2"], writes=["ms3"], dbg=f"{sw}msr{k}")
                else:
                    b.act(stc[:, 1:2], stc[:, 1:2], AF.Sqrt,
                          reads=["ms1"], writes=["ms2"], dbg=f"{sw}msq{k}")
                    b.vop("vector", lambda e: e.reciprocal(stc[:, 2:3],
                                                           stc[:, 1:2]),
                          ["ms2"], ["ms3"], dbg=f"{sw}msr{k}")

                def eyn(e):
                    return e.tensor_scalar(Ybuf[:, :], Ybuf[:, :], stc[:, 2:3],
                                           None, ALU.mult)
                b.vop("vector", eyn, ["Ybuf", "ms3"], ["Ybuf"],
                      dbg=f"{sw}yn{k}")
                for ec in range(16):
                    tb, tk = ((psT, "psT"), (psSt, "psSt"), (psE, "psE"),
                              (psA, "psA"))[ec % 4]
                    b.mm(tb[:, 0:128], Ybuf[:, ec * 128:(ec + 1) * 128],
                         cI["I_BF"][:, :], True, True, ["Ybuf", "I_BF"],
                         [tk], dbg=f"{sw}yT{k}_{ec}")
                    dst = ynT[:, ec * 512 + k * 128: ec * 512 + k * 128 + 128]
                    if ec % 2 == 0:
                        def evy(e, dst=dst, tb=tb):
                            return e.tensor_copy(dst, tb[:, 0:128])
                        b.vop("vector", evy, [tk], [f"ynT{ec}"],
                              dbg=f"{sw}yTe{k}_{ec}")
                    else:
                        b.act(dst, tb[:, 0:128], AF.Identity, reads=[tk],
                              writes=[f"ynT{ec}"], dbg=f"{sw}yTe{k}_{ec}")

            ynTall = [f"ynT{ec}" for ec in range(16)]
            for sd in range(4):
                for ec in range(16):
                    i = (sd * 16 + ec) % 2
                    wkey = f"wo{i}"
                    if s == 0:
                        b.dma("sp", wob[i], WB[sd, ec, :, :], [],
                              [f"wt{i}"], slot=f"wt{i}", dbg=f"{sw}ol{sd}_{ec}")
                        b.act(wobb[i][:, :], wob[i], AF.Identity,
                              reads=[f"wt{i}", "rmsw"], writes=[wkey],
                              scale=rmsw[:, ec:ec + 1], dbg=f"{sw}oc{sd}_{ec}")
                        b.dma("act", WOUTBF[sd, ec, :, :], wobb[i][:, :],
                              [wkey], [f"obf{sd}_{ec}"], slot=f"wob{i}",
                              dbg=f"{sw}os{sd}_{ec}")
                    else:
                        b.dma("sp", wobb[i][:, :], WOUTBF[sd, ec, :, :],
                              [f"obf{sd}_{ec}"], [wkey], slot=f"wob{i}",
                              dbg=f"{sw}ol{sd}_{ec}")
                    for k in range(CPS):
                        pk = ("psA", "psB", "psE", "psS")[k]
                        pso = (psA, psB, psE, psS)[k]
                        b.mm(pso[:, :],
                             ynT[:, ec * 512 + k * 128: ec * 512 + k * 128 + 128],
                             wobb[i][:, :], ec == 0, ec == 15,
                             [wkey] + ynTall, [pk], dbg=f"{sw}om{sd}_{ec}_{k}")
                        if ec == 15:
                            def eres(e, k=k, sd=sd, pso=pso):
                                return e.tensor_tensor(
                                    x2s[:, k * D + sd * 512:
                                        k * D + sd * 512 + 512],
                                    x2s[:, k * D + sd * 512:
                                        k * D + sd * 512 + 512],
                                    pso[:, :], ALU.add)
                            b.vop("vector", eres, [pk, f"x2s{k}"],
                                  [f"x2s{k}"], dbg=f"{sw}rs{sd}_{k}")
            for k in range(CPS):
                c = s * CPS + k
                xk = f"x2s{k}"
                b.act(scrap[:, :], x2s[:, k * D:(k + 1) * D], AF.Identity,
                      reads=[xk], writes=["scrap", "h2", f"s3_{c}"],
                      accum=st["m3s"][:, c:c + 1], dbg=f"{sw}s3{k}")
                b.act(scrap[:, :], x2s[:, k * D:(k + 1) * D], AF.Square,
                      reads=[xk], writes=["scrap", "h2", f"v3_{c}"],
                      accum=st["r3s"][:, c:c + 1], dbg=f"{sw}v3{k}")

                def emx3(e, c=c, k=k):
                    return e.reduce_max(st["mx3s"][:, c:c + 1],
                                        x2s[:, k * D:(k + 1) * D], axis=AX.X)
                b.vop("vector", emx3, [xk], [f"mh{c}"], dbg=f"{sw}mh{k}")

                def emn3(e, c=c, k=k):
                    return e.tensor_reduce(st["mn3s"][:, c:c + 1],
                                           x2s[:, k * D:(k + 1) * D],
                                           axis=AX.X, op=ALU.min)
                b.vop("vector", emn3, [xk], [f"ml{c}"], dbg=f"{sw}ml{k}")
                b.dma("sp", X3[c, :, :], x2s[:, k * D:(k + 1) * D], [xk],
                      [f"x3o{c}"], slot=f"x2sl{k}", dbg=f"{sw}x3s{k}")

        s3a = [f"s3_{c}" for c in range(NCHUNK)]
        v3a = [f"v3_{c}" for c in range(NCHUNK)]
        mha = [f"mh{c}" for c in range(NCHUNK)]
        mla = [f"ml{c}" for c in range(NCHUNK)]
        tmp3 = b.sb("tmp3", [128, NCHUNK], F32)
        b.vop("vector", lambda e: e.tensor_scalar(
            st["m3s"][:, :], st["m3s"][:, :], 1.0 / D, None, ALU.mult),
            s3a, ["m3"], dbg="m3")
        b.vop("vector", lambda e: e.tensor_scalar(
            st["r3s"][:, :], st["r3s"][:, :], 1.0 / D, None, ALU.mult),
            v3a, ["r3x"], dbg="r3x")
        b.vop("vector", lambda e: e.tensor_tensor(
            tmp3[:, :], st["m3s"][:, :], st["m3s"][:, :], ALU.mult),
            ["m3"], ["tmp3"], dbg="m3q")
        b.vop("vector", lambda e: e.tensor_tensor(
            st["r3s"][:, :], st["r3s"][:, :], tmp3[:, :], ALU.subtract),
            ["r3x", "tmp3"], ["r3y"], dbg="v3f")
        b.act(st["r3s"][:, :], st["r3s"][:, :], AF.Sqrt,
              reads=["r3y", "epsc"], writes=["r3z"], bias=epsc[:, 0:1],
              dbg="sd3")
        b.vop("vector", lambda e: e.reciprocal(st["r3s"][:, :],
                                               st["r3s"][:, :]),
              ["r3z"], ["r3"], dbg="rs3")
        b.vop("vector", lambda e: e.tensor_tensor(
            st["mx3s"][:, :], st["mx3s"][:, :], st["m3s"][:, :], ALU.subtract),
            mha + ["m3"], ["g3a"], dbg="g3a")
        b.vop("vector", lambda e: e.tensor_tensor(
            st["mn3s"][:, :], st["m3s"][:, :], st["mn3s"][:, :], ALU.subtract),
            mla + ["m3"], ["g3b"], dbg="g3b")
        b.vop("vector", lambda e: e.tensor_tensor(
            st["mx3s"][:, :], st["mx3s"][:, :], st["mn3s"][:, :], ALU.max),
            ["g3a", "g3b"], ["g3c"], dbg="g3c")
        b.vop("vector", lambda e: e.tensor_tensor(
            st["mx3s"][:, :], st["mx3s"][:, :], st["r3s"][:, :], ALU.mult),
            ["g3c", "r3"], ["g3"], dbg="g3")

        def ea3(e):
            return e.reduce_max(acol3[:, 0:1], st["mx3s"][:, :], axis=AX.X)
        b.vop("vector", ea3, ["g3"], ["ac3"], dbg="ac3")
        col_max_to_scalar(b, acol3[:, 0:1], "ac3", one3[0:1, 0:1], "a3", "a3",
                          rowb[0:1, :], "rw3")
        b.dma("sp", M3, st["m3s"][:, :], ["m3"], ["om3"], slot="batch:outs",
              dbg="st_m3")
        b.dma("sp", R3, st["r3s"][:, :], ["r3"], ["or3"], slot="batch:outs",
              dbg="st_r3")
        b.dma("sp", AMAX3, one3[0:1, 0:1], ["a3"], ["oa3"],
              slot="batch:outs", dbg="st_a3")
        b.fence(["om3", "or3", "oa3"] + [f"x3o{c}" for c in range(NCHUNK)])
        run_streams(nc, b.es, b.prog)
    return nc



def build_L2():
    b = B()
    nc = b.nc
    X3 = b.dram_in("x3", [NCHUNK, 128, D], F32)
    b.dram_in("WQT", [16, 128, 2048], BF16)
    M3 = b.dram_in("m3", [128, NCHUNK], F32)
    R3 = b.dram_in("r3", [128, NCHUNK], F32)
    A3V = b.dram_in("amax3v", [1, 8], F32)
    OUT = b.dram_out("out", [NCHUNK, 128, D], F32)
    with b.es:
        cI = b.load_consts(["I_F8"])
        psA = b.ps("psA", [128, 512])
        psB = b.ps("psB", [128, 512])
        psE = b.ps("psE", [128, 512])
        psS = b.ps("psS", [128, 512])
        psT = b.ps("psT", [128, 512])
        psSt = b.ps("psSt", [128, 512])
        m3 = b.sb("m3s", [128, NCHUNK], F32)
        r3 = b.sb("r3s", [128, NCHUNK], F32)
        qsc = b.sb("qsc", [128, NCHUNK], F32)
        qbs = b.sb("qbs", [128, NCHUNK], F32)
        bc = b.sb("bcb", [128, 4], F32)
        a3row = b.sb("a3row", [1, 8], F32)
        warow = b.sb("warow", [1, 1], F32)
        rowb = b.sb("rowb", [1, 128], F32)
        b.dma("sp", m3[:, :], M3, [], ["m3"], dbg="ld_m3")
        b.dma("sp", r3[:, :], R3, [], ["r3"], dbg="ld_r3")
        b.dma("sp", a3row[0:1, :], A3V, [], ["a3row"], dbg="ld_a3")
        wq = b.sb("wq", [128, 16 * 2048], FP8)
        tiles, wmx = load_wq(b, b.es, bc, wq)
        col_max_to_scalar(b, wmx[:, 0:1], "wmx0", warow[0:1, 0:1], "wam",
                          "w2", rowb[0:1, :], "rowbufW")
        ffn_scalars(b, a3row[0:1, :], "a3row", warow[0:1, 0:1], "wam", bc,
                    "f2")
        qscale_cols(b, qsc, qbs, m3, "m3", r3, "r3", bc)
        quant_wq(b, tiles, wq)
        xb = [b.sb(f"xb{i}", [128, D], F32) for i in range(2)]
        q1 = [b.sb(f"q1_{i}", [128, D], FP8) for i in range(2)]
        q1T = [b.sb(f"q1T_{i}", [128, D], FP8) for i in range(2)]
        f1 = [b.sb(f"f1_{i}", [128, D], BF16) for i in range(2)]
        ob = [b.sb(f"ob{i}", [128, D], F32) for i in range(2)]
        outk = []
        for c in range(NCHUNK):
            i = c % 2
            b.dma("pool", xb[i][:, :], X3[c, :, :], [], [f"xb{i}"],
                  slot=f"xb{i}", dbg=f"x3l{c}")
            ffn_chunk(b, c, xb[i][:, :], f"xb{i}", qsc, qbs, wq,
                      [(psA, "psA"), (psB, "psB"), (psE, "psE"),
                       (psS, "psS")],
                      [(psT, "psT"), (psSt, "psSt")],
                      q1[i], q1T[i], f"q1{i}", f"q1T{i}", f1[i], f"f1{i}",
                      cI["I_F8"], bc)

            def eadd(e, i=i):
                return e.tensor_tensor(ob[i][:, :], xb[i][:, :], f1[i][:, :],
                                       ALU.add)
            b.vop("vector", eadd, [f"xb{i}", f"f1{i}"], [f"ob{i}"],
                  dbg=f"oa{c}")
            b.dma("sp", OUT[c, :, :], ob[i][:, :], [f"ob{i}"], [f"oo{c}"],
                  slot=f"ob{i}", dbg=f"os{c}")
            outk.append(f"oo{c}")
        b.fence(outk)
        run_streams(nc, b.es, b.prog)
    return nc



def _host_prep(W_in, W_out, ffn_w, conv_w, conv_b, z_bias, rms_w, Dv):
    Wp = np.zeros((NEB * 128, D), np.float32)
    Wp[:PROJ] = np.asarray(W_in, np.float32)
    WA = np.ascontiguousarray(Wp.reshape(NEB, 128, 16, 128)
                              .transpose(0, 3, 2, 1))
    WB = np.ascontiguousarray(np.asarray(W_out, np.float32)
                              .reshape(4, 512, 16, 128).transpose(0, 2, 3, 1))
    WQT = np.ascontiguousarray(np.asarray(ffn_w).T.reshape(16, 128, 2048))
    CWH = np.ascontiguousarray(np.asarray(conv_w, np.float32)
                               .reshape(48, 128, 4).transpose(1, 0, 2)
                               .reshape(128, 192))
    CBH = np.ascontiguousarray(np.asarray(conv_b, np.float32)
                               .reshape(48, 128).T)
    ZBH = np.ascontiguousarray(np.asarray(z_bias, np.float32)
                               .reshape(16, 128).T)
    RWH = np.ascontiguousarray(np.asarray(rms_w, np.float32)
                               .reshape(16, 128).T)
    DREP = np.ascontiguousarray(np.tile(
        np.repeat(np.asarray(Dv, np.float32), 64)[None, :], (128, 1)))
    return WA, WB, WQT, CWH, CBH, ZBH, RWH, DREP


def kernel(L0_attn_u, L0_attn_W_in, L0_attn_z_bias, L0_attn_conv_w,
           L0_attn_conv_b, L0_attn_D, L0_attn_rms_w, L0_attn_W_out,
           L0_ffn_weight):
    u = np.ascontiguousarray(np.asarray(L0_attn_u, np.float32))
    WA, WB, WQT, CWH, CBH, ZBH, RWH, DD = _host_prep(
        L0_attn_W_in, L0_attn_W_out, L0_ffn_weight, L0_attn_conv_w,
        L0_attn_conv_b, L0_attn_z_bias, L0_attn_rms_w, L0_attn_D)
    ushards = [np.ascontiguousarray(u[i].reshape(NCHUNK, 128, D))
               for i in range(NCORES)]
    cores = list(range(NCORES))

    nc0 = build_L0()
    r0 = run_bass_kernel_spmd(
        nc0, [dict(u=ushards[i]) for i in range(NCORES)],
        core_ids=cores)
    amax1v = np.ascontiguousarray(
        np.stack([np.asarray(r0.results[i]["amax1"]).reshape(())
                  for i in range(NCORES)]).reshape(1, 8).astype(np.float32))

    nc1 = build_L1()
    in1 = [dict(u=ushards[i], WA=WA, WB=WB, WQT=WQT,
                m1=np.asarray(r0.results[i]["m1"]),
                r1=np.asarray(r0.results[i]["r1"]),
                amax1v=amax1v, cwh=CWH, cbh=CBH, zbh=ZBH,
                rwh=RWH, drep=DD) for i in range(NCORES)]
    r1 = run_bass_kernel_spmd(nc1, in1, core_ids=cores)
    amax3v = np.ascontiguousarray(
        np.stack([np.asarray(r1.results[i]["amax3"]).reshape(())
                  for i in range(NCORES)]).reshape(1, 8).astype(np.float32))

    nc2 = build_L2()
    in2 = [dict(x3=np.asarray(r1.results[i]["x3"]), WQT=WQT,
                m3=np.asarray(r1.results[i]["m3"]),
                r3=np.asarray(r1.results[i]["r3"]),
                amax3v=amax3v) for i in range(NCORES)]
    r2 = run_bass_kernel_spmd(nc2, in2, core_ids=cores)
    out = np.stack([np.asarray(r2.results[i]["out"]).reshape(T, D)
                    for i in range(NCORES)])
    return np.ascontiguousarray(out.astype(np.float32))
